# version 01
# baseline (speedup 1.0000x reference)
import math
import numpy as np
import ml_dtypes
import concourse.bass as bass
import concourse.mybir as mybir
from concourse.bass_utils import run_bass_kernel_spmd

F32 = mybir.dt.float32
BF16 = mybir.dt.bfloat16
AF = mybir.ActivationFunctionType
ALU = mybir.AluOpType
AX = mybir.AxisListType

T = 4096
D = 1024
NH = 16
DH = 64
NT = T // 128
NEG = -1e30

SBUF_BASE = 16640
SBUF_END = 229248

ENGINES = ("sync", "tensor", "vector", "scalar", "gpsimd")


def host_consts():
    c = {}
    bf = ml_dtypes.bfloat16
    c["ident"] = np.eye(128, dtype=np.float32).astype(bf)

    m = np.zeros((128, 4, 512), dtype=np.float32)
    for v in range(4):
        q = np.arange(128)[:, None]
        k = np.arange(512)[None, :]
        m[:, v, :] = np.where(k <= v * 128 + q, 0.0, NEG)
    c["cmask"] = m.astype(bf)

    dhe = 32
    inv = 1.0 / 10000.0 ** (np.arange(0, dhe, 2, dtype=np.float64) / dhe)
    tpos = np.arange(T, dtype=np.float64)
    fr = tpos[:, None] * inv[None, :]
    emb = np.concatenate([fr, fr], axis=1)
    cosA32 = np.cos(emb).T
    sinA32 = np.sin(emb).T
    c["cosA"] = np.tile(cosA32, (4, 1)).astype(bf)
    c["sinA"] = np.tile(sinA32, (4, 1)).astype(bf)
    R32 = np.zeros((32, 32), dtype=np.float32)
    for i in range(16):
        R32[2 * i, 2 * i + 1] = -1.0
        R32[2 * i + 1, 2 * i] = 1.0
    c["R32T"] = np.tile(R32.T, (4, 1)).astype(bf)
    Rhat = np.zeros((128, 128), dtype=np.float32)
    Rhat[32:64, 32:64] = R32
    Rhat[96:128, 96:128] = R32
    c["RhatAT"] = Rhat.T.copy().astype(bf)

    half = 32
    invf = 1.0 / 10000.0 ** (np.arange(half, dtype=np.float64) / half)
    for name, poss in (("N1", np.arange(1, T + 1, dtype=np.float64)),
                       ("N2", np.arange(T, dtype=np.float64))):
        ang = poss[:, None] * invf[None, :]
        cc = np.concatenate([np.cos(ang), np.cos(ang)], axis=1).T
        ss = np.concatenate([np.sin(ang), np.sin(ang)], axis=1).T
        c["cos" + name] = np.tile(cc, (2, 1)).astype(bf)
        c["sin" + name] = np.tile(ss, (2, 1)).astype(bf)
    R64 = np.zeros((64, 64), dtype=np.float32)
    R64[:32, 32:] = -np.eye(32)
    R64[32:, :32] = np.eye(32)
    R128 = np.zeros((128, 128), dtype=np.float32)
    R128[:64, :64] = R64
    R128[64:, 64:] = R64
    c["R128T"] = R128.T.copy().astype(bf)

    hd = 64
    invb = 1.0 / 10000.0 ** (np.arange(0, hd, 2, dtype=np.float64) / hd)
    Mall = np.zeros((D, D), dtype=np.float64)
    for h in range(NH):
        Mh = np.eye(hd)
        for f in range(8):
            a = h * invb[f]
            cv, sv = np.cos(a), np.sin(a)
            Mh[2 * f, 2 * f] = cv
            Mh[2 * f, 2 * f + 1] = -sv
            Mh[2 * f + 1, 2 * f] = sv
            Mh[2 * f + 1, 2 * f + 1] = cv
        Mall[h * hd:(h + 1) * hd, h * hd:(h + 1) * hd] = Mh
    MT = np.zeros((128, 8, 128), dtype=np.float32)
    for cc in range(8):
        blk = Mall[cc * 128:(cc + 1) * 128, cc * 128:(cc + 1) * 128]
        MT[:, cc, :] = blk.T
    c["MallT"] = MT.astype(bf)
    return c


def reshape_col_chunks(v):
    n = v.shape[0]
    return np.ascontiguousarray(v.reshape(n // 128, 128).T)


class Rec:
    def __init__(self, nc):
        self.nc = nc
        self.ops = []
        self.cnt = {}
        self.sems = {}

    def sem(self, name):
        ctx = self.nc.semaphore(name)
        h = ctx.__enter__()
        self.sems[name] = h
        self.cnt[name] = 0
        self.last_wait = getattr(self, "last_wait", {})
        self.last_wait[name] = 0
        return h

    def val(self, name):
        return self.cnt[name]

    ENG_SEM = {"vector": "ve", "scalar": "se", "gpsimd": "po"}

    def op(self, eng, fn, waits=(), incs=(), selfsync=True):
        waits = list(waits)
        if selfsync and fn is not None and eng in self.ENG_SEM:
            s = self.ENG_SEM[eng]
            if self.cnt[s] > 0:
                waits.append((s, self.cnt[s]))
        for (s, v) in waits:
            assert v <= self.cnt[s], f"wait on future: {s} {v} > {self.cnt[s]}"
            if v > self.last_wait[s]:
                self.last_wait[s] = v
        for (s, k) in incs:
            if k == 16 and self.last_wait[s] > 0:
                have = dict(waits).get(s, -1)
                if have < self.last_wait[s]:
                    waits.append((s, self.last_wait[s]))
        for (s, k) in incs:
            self.cnt[s] += k
        self.ops.append((eng, fn, tuple(waits), tuple(incs)))

    def dma(self, queue, out, in_, sem, waits=()):
        waits = list(waits)
        if sem == "stx" and self.cnt[sem] > 0:
            waits.append((sem, self.cnt[sem]))
        def fn(e, out=out, in_=in_):
            return e.dma_start(out=out, in_=in_)
        self.op(queue, fn, waits, ((sem, 16),), selfsync=False)

    def barrier(self):
        if "bar" not in self.cnt:
            self.sem("bar")
        base = self.cnt["bar"]
        for eng in ENGINES:
            self.op(eng, lambda e: e.nop(), (), (("bar", 1),), selfsync=False)
        tgt = base + 5
        for eng in ENGINES:
            self.op(eng, None, (("bar", tgt),), ())

    def emit(self):
        nc = self.nc
        with nc.Block() as block:
            for eng_name in ENGINES:
                def make(eng_name=eng_name):
                    def f(eng):
                        guaranteed = {}
                        for (en, fn, waits, incs) in self.ops:
                            if en != eng_name:
                                continue
                            for (s, v) in waits:
                                if guaranteed.get(s, -1) >= v:
                                    continue
                                eng.wait_ge(self.sems[s], v)
                                guaranteed[s] = v
                            if fn is None:
                                if incs:
                                    ins = eng.nop()
                                    for (s, k) in incs:
                                        ins.then_inc(self.sems[s], k)
                                continue
                            ins = fn(eng)
                            for (s, k) in incs:
                                ins.then_inc(self.sems[s], k)
                    return f
                getattr(block, eng_name)(make())


class Arena:
    def __init__(self, nc, base=SBUF_BASE, end=SBUF_END):
        self.nc = nc
        self.base = base
        self.end = end
        self.off = base
        self.marks = []
        self.n = 0

    def alloc(self, name, shape, dtype, align=64):
        self.off = (self.off + align - 1) // align * align
        nbytes = int(np.prod(shape[1:])) * mybir.dt.size(dtype)
        assert shape[0] <= 128
        t = self.nc.alloc_sbuf_tensor_at(
            f"{name}_{self.n}", list(shape), dtype, offset=self.off)
        self.n += 1
        self.off += nbytes
        assert self.off <= self.end, f"SBUF overflow at {name}: {self.off}"
        return t.ap()

    def mark(self):
        self.marks.append(self.off)

    def release(self):
        self.off = self.marks.pop()


class Bufs:
    def __init__(self, ar):
        self.cpool = ar.alloc("cpool", [128, 4], F32)
        self.ident = ar.alloc("ident", [128, 128], BF16)
        self.cmask = ar.alloc("cmask", [128, 4, 512], BF16)
        self.trash = ar.alloc("trash", [128, 1024], F32)
        self.mu_raw = ar.alloc("mu_raw", [128, 16], F32)
        self.s2 = ar.alloc("s2", [128, 16], F32)
        self.mu = ar.alloc("mu", [128, 16], F32)
        self.mu2 = ar.alloc("mu2", [128, 16], F32)
        self.var = ar.alloc("var", [128, 16], F32)
        self.sd = ar.alloc("sd", [128, 16], F32)
        self.rstd = ar.alloc("rstd", [128, 16], F32)
        self.nmr = ar.alloc("nmr", [128, 16], F32)
        self.mst = ar.alloc("mst", [128, 8], F32)
        self.sst = ar.alloc("sst", [128, 8], F32)
        self.bst = ar.alloc("bst", [128, 8], F32)
        self.corr = ar.alloc("corr", [128, 8], F32)
        self.negMx = ar.alloc("negMx", [128, 1], F32)
        self.ssum = ar.alloc("ssum", [128, 1], F32)
        self.rs = ar.alloc("rs", [128, 1], F32)
        self.pscale = ar.alloc("pscale", [128, 8], F32)
        self.tmp8 = ar.alloc("tmp8", [128, 8], F32)
        self.ln_eps = 1e-5


ACT_CONSTS = (0.125,)


def emit_load_consts(r, B, cap):
    r.dma("sync", B.ident[:, :], cap["ident"][:, :], "ldx")
    r.dma("sync", B.cmask[:, :, :], cap["cmask"][:, :, :], "ldx")
    for i, v in enumerate(ACT_CONSTS):
        r.op("gpsimd",
             lambda e, i=i, v=v: e.memset(B.cpool[:, i:i + 1], v),
             (), (("po", 1),))
        r.nc.const_aps.register(F32, v, B.cpool[:, i:i + 1])
    r.op("sync", None, (("ldx", r.val("ldx")),), ())
    r.barrier()


def emit_ln(r, B, xtiles, jcols, ln_wait):
    nt = len(xtiles)
    ln_wait = list(ln_wait) + [("po", r.val("po"))]
    for (xt, _), j in zip(xtiles, jcols):
        r.op("vector",
             lambda e, xt=xt, j=j: e.reduce_sum(B.mu_raw[:, j:j + 1], xt, axis=AX.X),
             ln_wait, (("ve", 1),))
        r.op("scalar",
             lambda e, xt=xt, j=j: e.activation(
                 B.trash[:, :], xt, AF.Square, accum_out=B.s2[:, j:j + 1]),
             ln_wait, (("se", 1),))
    ve_after = r.val("ve")
    se_after = r.val("se")
    j0, j1 = jcols[0], jcols[0] + nt
    r.op("gpsimd",
         lambda e: e.tensor_scalar_mul(B.mu[:, j0:j1], B.mu_raw[:, j0:j1], 1.0 / D),
         (("ve", ve_after),), (("po", 1),))
    r.op("gpsimd",
         lambda e: e.tensor_mul(B.mu2[:, j0:j1], B.mu[:, j0:j1], B.mu[:, j0:j1]),
         (), (("po", 1),))
    r.op("vector",
         lambda e: e.scalar_tensor_tensor(
             B.var[:, j0:j1], B.s2[:, j0:j1], 1.0 / D, B.mu2[:, j0:j1],
             ALU.mult, ALU.subtract),
         (("se", se_after), ("po", r.val("po"))), (("ve", 1),))
    r.op("gpsimd",
         lambda e: e.tensor_scalar_add(B.var[:, j0:j1], B.var[:, j0:j1],
                                       B.ln_eps),
         (("ve", r.val("ve")),), (("po", 1),))
    r.op("scalar",
         lambda e: e.activation(B.sd[:, j0:j1], B.var[:, j0:j1], AF.Sqrt),
         (("po", r.val("po")),), (("se", 1),))
    r.op("vector",
         lambda e: e.reciprocal(B.rstd[:, j0:j1], B.sd[:, j0:j1]),
         (("se", r.val("se")),), (("ve", 1),))
    r.op("vector",
         lambda e: e.scalar_tensor_tensor(
             B.nmr[:, j0:j1], B.mu[:, j0:j1], -1.0, B.rstd[:, j0:j1],
             ALU.mult, ALU.mult),
         (), (("ve", 1),))
    stats_wait = (("po", r.val("po")), ("ve", r.val("ve")))
    for (xt, yt), j in zip(xtiles, jcols):
        r.op("scalar",
             lambda e, xt=xt, yt=yt, j=j: e.activation(
                 yt, xt, AF.Identity, scale=B.rstd[:, j:j + 1],
                 bias=B.nmr[:, j:j + 1]),
             stats_wait, (("se", 1),))
    return r.val("se")


def emit_transpose(r, B, psT, src, dst, pidx, src_wait, evac_eng="vector"):
    key = ("psT", pidx)
    prev = r._psT_free.get(key, ())
    r.op("tensor",
         lambda e, src=src, pidx=pidx: e.transpose(psT[:, pidx, 0:128], src, B.ident[:, :]),
         tuple(src_wait) + tuple(prev), (("pe", 1),))
    sem = {"vector": "ve", "scalar": "se"}[evac_eng]
    r.op(evac_eng,
         lambda e, dst=dst, pidx=pidx: e.tensor_copy(dst, psT[:, pidx, 0:128])
         if evac_eng == "vector" else e.activation(dst, psT[:, pidx, 0:128], AF.Copy),
         (("pe", r.val("pe")),), ((sem, 1),))
    r._psT_free[key] = ((sem, r.val(sem)),)


def emit_ffn_l0(r, B, ar, wap, xin, xout, psA, psB, psC, psT):
    TH = 2048
    ar.mark()
    x_half = ar.alloc("ffn_x", [128, 16, 1024], F32)
    yT = ar.alloc("ffn_yT", [128, 8, TH], BF16)
    ystage = ar.alloc("ffn_y", [128, 2, 1024], BF16)
    wg = ar.alloc("ffn_wg", [128, 8, 1024], BF16)
    wu = ar.alloc("ffn_wu", [128, 8, 1024], BF16)
    wd = ar.alloc("ffn_wd", [128, 8, 1024], BF16)
    hbuf = ar.alloc("ffn_h", [128, 8, 512], BF16)
    silu_st = ar.alloc("ffn_silu", [128, 2, 512], BF16)

    Wg = wap["Wg"]
    Wu = wap["Wu"]
    Wd = wap["Wd"]

    for half in range(2):
        t0 = half * TH
        for j in range(16):
            r.dma("sync", x_half[:, j], xin[t0 + j * 128: t0 + (j + 1) * 128, :],
                  "ldx")
        ldx_all = r.val("ldx")
        for j in range(16):
            yt_done = emit_ln(r, B, [(x_half[:, j], ystage[:, j % 2])], [j],
                              [("ldx", ldx_all)] + r._ystage_free.get(j % 2, []))
            for c in range(8):
                emit_transpose(
                    r, B, psT,
                    ystage[:, j % 2, c * 128:(c + 1) * 128],
                    yT[:, c, j * 128:(j + 1) * 128],
                    (j * 8 + c) % 2,
                    [("se", yt_done)],
                    evac_eng="vector" if c % 2 == 0 else "scalar")
            r._ystage_free[j % 2] = [("pe", r.val("pe"))]
        yT_ready = (("ve", r.val("ve")), ("se", r.val("se")))

        for e8 in range(8):
            for g in range(4):
                gc = g * 1024
                wg_wait = r._wgwu_free
                for kc in range(8):
                    r.dma("gpsimd", wg[:, kc], Wg[e8, kc * 128:(kc + 1) * 128,
                                                  gc:gc + 1024], "wdma",
                          waits=wg_wait)
                    r.dma("gpsimd", wu[:, kc], Wu[e8, kc * 128:(kc + 1) * 128,
                                                  gc:gc + 1024], "wdma",
                          waits=wg_wait)
                wd_wait = r._wd_free
                for jj in range(8):
                    r.dma("gpsimd", wd[:, jj],
                          Wd[e8, gc + jj * 128:gc + (jj + 1) * 128, :], "wdma",
                          waits=wd_wait)
                w_ld = r.val("wdma")

                for tb in range(4):
                    for jj in range(8):
                        b = (tb * 8 + jj) % 2
                        g_wait = [("wdma", w_ld)] + r._psA_free.get(b, []) + \
                                 list(yT_ready)
                        for kc in range(8):
                            r.op("tensor",
                                 lambda e, b=b, kc=kc, jj=jj, tb=tb: e.matmul(
                                     psA[:, b], wg[:, kc, jj * 128:(jj + 1) * 128],
                                     yT[:, kc, tb * 512:(tb + 1) * 512],
                                     start=(kc == 0), stop=(kc == 7)),
                                 g_wait if kc == 0 else (),
                                 (("pe", 1),) if kc == 7 else ())
                        u_wait = r._psB_free.get(b, [])
                        for kc in range(8):
                            r.op("tensor",
                                 lambda e, b=b, kc=kc, jj=jj, tb=tb: e.matmul(
                                     psB[:, b], wu[:, kc, jj * 128:(jj + 1) * 128],
                                     yT[:, kc, tb * 512:(tb + 1) * 512],
                                     start=(kc == 0), stop=(kc == 7)),
                                 u_wait if kc == 0 else (),
                                 (("pe", 1),) if kc == 7 else ())
                        pe_now = r.val("pe")
                        r.op("scalar",
                             lambda e, b=b: e.activation(
                                 silu_st[:, b], psA[:, b], AF.Silu),
                             [("pe", pe_now - 1)] + r._silu_free.get(b, []),
                             (("se", 1),))
                        r._psA_free[b] = [("se", r.val("se"))]
                        r.op("vector",
                             lambda e, b=b, jj=jj: e.tensor_mul(
                                 hbuf[:, jj], psB[:, b], silu_st[:, b]),
                             [("pe", pe_now), ("se", r.val("se"))] +
                             r._h_free.get(jj, []),
                             (("ve", 1),))
                        r._psB_free[b] = [("ve", r.val("ve"))]
                        r._silu_free[b] = [("ve", r.val("ve"))]
                    h_done = r.val("ve")
                    for tsub in range(4):
                        tt = tb * 4 + tsub
                        for oh in range(2):
                            b = (tt * 2 + oh) % 2
                            d_wait = [("ve", h_done), ("wdma", w_ld)] + \
                                     r._psC_free.get(b, [])
                            for jj in range(8):
                                r.op("tensor",
                                     lambda e, b=b, jj=jj, tsub=tsub, oh=oh:
                                     e.matmul(
                                         psC[:, b],
                                         hbuf[:, jj, tsub * 128:(tsub + 1) * 128],
                                         wd[:, jj, oh * 512:(oh + 1) * 512],
                                         start=(jj == 0), stop=(jj == 7)),
                                     d_wait if jj == 0 else (),
                                     (("pe", 1),) if jj == 7 else ())
                            r.op("vector",
                                 lambda e, b=b, tt=tt, oh=oh: e.tensor_add(
                                     x_half[:, tt, oh * 512:(oh + 1) * 512],
                                     psC[:, b],
                                     x_half[:, tt, oh * 512:(oh + 1) * 512]),
                                 [("pe", r.val("pe"))], (("ve", 1),))
                            r._psC_free[b] = [("ve", r.val("ve"))]
                    for jj in range(8):
                        r._h_free[jj] = [("pe", r.val("pe"))]
                r._wgwu_free = [("pe", r.val("pe"))]
                r._wd_free = [("pe", r.val("pe"))]

        ve_done = r.val("ve")
        for j in range(16):
            r.dma("sync", xout[t0 + j * 128: t0 + (j + 1) * 128, :], x_half[:, j],
                  "stx", waits=[("ve", ve_done)])
        r.op("sync", None, (("stx", r.val("stx")),), ())
        r.barrier()
    ar.release()


class AttnBufs:
    def __init__(self, r, ar, nc, tag):
        self.kt0 = ar.alloc(tag + "kt0", [128, 4, T], BF16)
        self.v0 = ar.alloc(tag + "v0", [128, NT, 512], BF16)
        self.qt_dram = nc.dram_tensor(tag + "qt", [D, T], BF16, kind="Internal").ap()
        self.kt_dram = nc.dram_tensor(tag + "kt", [512, T], BF16, kind="Internal").ap()
        self.v_dram = nc.dram_tensor(tag + "v", [T, 512], BF16, kind="Internal").ap()


def emit_phaseA(r, B, ar, A, cfg, xin, ps):
    psA, psB, psC, psT = ps
    ar.mark()
    A.xs = ar.alloc("pa_xs", [128, 4, 1024], F32)
    A.ystage = ar.alloc("pa_y", [128, 2, 1024], BF16)
    A.yT = ar.alloc("pa_yT", [128, 8, 512], BF16)
    A.qstage = ar.alloc("pa_qs", [128, 8, 512], BF16)
    A.kstage = ar.alloc("pa_ks", [128, 4, 512], BF16)
    A.vt = ar.alloc("pa_vt", [128, 8, 512], BF16)
    A.vstage = ar.alloc("pa_vs", [128, 4, 128], BF16)
    cfg.alloc_phaseA(r, B, ar, A)

    for ib in range(8):
        for j in range(4):
            r.dma("sync", A.xs[:, j], xin[ib * 512 + j * 128: ib * 512 + (j + 1) * 128, :],
                  "ldx", waits=r._xs_free)
        ld = r.val("ldx")
        for j in range(4):
            yd = emit_ln(r, B, [(A.xs[:, j], A.ystage[:, j % 2])], [j],
                         [("ldx", ld)] + r._ystage_free.get(j % 2, []))
            for c in range(8):
                emit_transpose(r, B, psT, A.ystage[:, j % 2, c * 128:(c + 1) * 128],
                               A.yT[:, c, j * 128:(j + 1) * 128], (j * 8 + c) % 2,
                               [("se", yd)],
                               evac_eng="vector" if c % 2 == 0 else "scalar")
            r._ystage_free[j % 2] = [("pe", r.val("pe"))]
        r._xs_free = [("pe", r.val("pe"))]
        yT_done = (("ve", r.val("ve")), ("se", r.val("se")))

        cfg.qkv(r, B, A, ib, yT_done)

        for rc in range(8):
            r.dma("sync", A.qt_dram[rc * 128:(rc + 1) * 128, ib * 512:(ib + 1) * 512],
                  A.qstage[:, rc], "stx", waits=A.q_done[rc])
        for rc in range(4):
            r.dma("sync", A.kt_dram[rc * 128:(rc + 1) * 128, ib * 512:(ib + 1) * 512],
                  A.kstage[:, rc], "stx", waits=A.k_done[rc + 4])
        for tt in range(4):
            trow = ib * 4 + tt
            for rc in range(8):
                src = A.vt[:, rc, tt * 128:(tt + 1) * 128]
                if rc < 4:
                    dst = A.v0[:, trow, rc * 128:(rc + 1) * 128]
                else:
                    dst = A.vstage[:, rc - 4]
                emit_transpose(r, B, psT, src, dst, (tt * 8 + rc) % 2,
                               A.v_done[rc] + r._vstage_free,
                               evac_eng="vector" if rc % 2 == 0 else "scalar")
            vs_done = [("ve", r.val("ve")), ("se", r.val("se"))]
            r.dma("sync", A.v_dram[trow * 128:(trow + 1) * 128, :],
                  A.vstage[:, 0:4].rearrange("p a b -> p (a b)"), "stx",
                  waits=vs_done)
            r._vstage_free = [("stx", r.val("stx"))]
        r._vt_free = [("pe", r.val("pe"))]
        r._qstage_free = [("stx", r.val("stx"))]
    ar.release()
    r.barrier()


def emit_phaseB(r, B, ar, A, cfg, xin, xout, ps):
    psA, psB, psC, psT = ps
    ar.mark()
    wo = cfg.alloc_phaseB(r, B, ar, A)
    P = ar.alloc("pb_P", [128, T], BF16)
    PT = ar.alloc("pb_PT", [128, NT, 128], BF16)
    Sd = ar.alloc("pb_Sd", [128, 512], F32)
    OT = ar.alloc("pb_OT", [128, 4, 128], BF16)
    qts = ar.alloc("pb_qts", [128, 2, 4, 128], BF16)
    xsr = ar.alloc("pb_xs", [128, 2, 1024], F32)
    xnw = ar.alloc("pb_xn", [128, 2, 1024], F32)

    qt_r = A.qt_dram.rearrange("(n p) t -> p n t", p=128)
    kt_r = A.kt_dram.rearrange("(n p) t -> p n t", p=128)
    v_r = A.v_dram.rearrange("(n p) c -> p n c", p=128)

    for hg in range(2):
        if hg == 1:
            r.barrier()
            r.dma("sync", A.kt0[:, :, :], kt_r[:, :, :], "ldx")
            r.dma("sync", A.v0[:, :, :], v_r[:, :, :], "ldx")
        kv_ld = r.val("ldx")
        xsrc = xin if hg == 0 else xout

        r.op("sync", None, (("ldx", kv_ld),), ())
        r.dma("sync", qts[:, 0], qt_r[:, hg * 4:hg * 4 + 4, 0:128], "ldx",
              waits=r._qts_free.get(0, []))
        r.dma("sync", xsr[:, 0], xsrc[0:128, :], "ldx",
              waits=[("stx", r.val("stx"))] + r._xsr_free.get(0, []))
        pf = {0: r.val("ldx")}

        for i in range(NT):
            qb = i % 2
            if i + 1 < NT:
                r.op("sync", None, (("ldx", pf[i]),), ())
                r.dma("sync", qts[:, (i + 1) % 2],
                      qt_r[:, hg * 4:hg * 4 + 4, (i + 1) * 128:(i + 2) * 128],
                      "ldx", waits=r._qts_free.get((i + 1) % 2, []))
                r.dma("sync", xsr[:, (i + 1) % 2],
                      xsrc[(i + 1) * 128:(i + 2) * 128, :], "ldx",
                      waits=r._xsr_free.get((i + 1) % 2, []))
                pf[i + 1] = r.val("ldx")
            nkc = (i + 1) if cfg.causal else NT
            nc5 = (nkc * 128 + 511) // 512
            for h8 in range(8):
                hc = h8 // 2
                hb = (h8 % 2) * 64
                for c in range(nc5):
                    b = c % 2
                    diag = cfg.causal and (c == nc5 - 1)
                    w = [("ldx", max(pf[i], kv_ld))] + r._psSc_free.get(b, [])
                    r.op("tensor",
                         lambda e, b=b, c=c, hc=hc, hb=hb, qb=qb: e.matmul(
                             psA[:, b],
                             qts[hb:hb + 64, qb, hc, :],
                             A.kt0[hb:hb + 64, hc, c * 512:(c + 1) * 512],
                             start=True, stop=True, tile_position=(hb, 0)),
                         w, (("pe", 1),))
                    pe_s = r.val("pe")
                    if diag:
                        r.op("vector",
                             lambda e, b=b, i=i: e.tensor_add(
                                 Sd[:, :], psA[:, b], B.cmask[:, i % 4]),
                             [("pe", pe_s)] + r._Sd_free, (("ve", 1),))
                        ve_tt = r.val("ve")
                        r._psSc_free[b] = [("ve", ve_tt)]
                        src = Sd[:, :]
                        srcw = [("ve", ve_tt), ("se", r._mst_guard)]
                    else:
                        src = psA[:, b]
                        srcw = [("pe", pe_s), ("se", r._mst_guard)]
                    r.op("vector",
                         lambda e, c=c, src=src: e.reduce_max(
                             B.mst[:, c:c + 1], src, axis=AX.X),
                         srcw, (("ve", 1),))
                    r.op("gpsimd",
                         lambda e, c=c: e.tensor_scalar_mul(
                             B.bst[:, c:c + 1], B.mst[:, c:c + 1], -0.125),
                         [("ve", r.val("ve"))], (("po", 1),))
                    r.op("scalar",
                         lambda e, c=c, src=src: e.activation(
                             P[:, c * 512:(c + 1) * 512], src, AF.Exp,
                             scale=0.125, bias=B.bst[:, c:c + 1],
                             accum_out=B.sst[:, c:c + 1]),
                         [("po", r.val("po"))], (("se", 1),))
                    if diag:
                        r._Sd_free = [("se", r.val("se"))]
                    else:
                        r._psSc_free[b] = [("se", r.val("se"))]
                se_exp = r.val("se")
                r.op("vector",
                     lambda e, n=nc5: e.reduce_max(
                         B.negMx[:, :], B.mst[:, 0:n], axis=AX.X, negate=True),
                     [("se", se_exp)], (("ve", 1),))
                r.op("gpsimd",
                     lambda e: e.tensor_scalar_mul(
                         B.negMx[:, :], B.negMx[:, :], 0.125),
                     [("ve", r.val("ve"))], (("po", 1),))
                r.op("scalar",
                     lambda e, n=nc5: e.activation(
                         B.corr[:, 0:n], B.mst[:, 0:n], AF.Exp,
                         scale=0.125, bias=B.negMx[:, :]),
                     [("po", r.val("po"))], (("se", 1),))
                r._mst_guard = r.val("se")
                r.op("vector",
                     lambda e, n=nc5: e.tensor_mul(
                         B.tmp8[:, 0:n], B.corr[:, 0:n], B.sst[:, 0:n]),
                     [("se", r.val("se"))], (("ve", 1),))
                r.op("vector",
                     lambda e, n=nc5: e.reduce_sum(
                         B.ssum[:, :], B.tmp8[:, 0:n], axis=AX.X),
                     (), (("ve", 1),))
                r.op("vector",
                     lambda e: e.reciprocal(B.rs[:, :], B.ssum[:, :]),
                     (), (("ve", 1),))
                r.op("gpsimd",
                     lambda e, n=nc5: e.tensor_scalar_mul(
                         B.pscale[:, 0:n], B.corr[:, 0:n], B.rs[:, :]),
                     [("ve", r.val("ve"))], (("po", 1),))
                po_ps = r.val("po")
                for c in range(nc5):
                    r.op("vector",
                         lambda e, c=c: e.tensor_scalar_mul(
                             P[:, c * 512:(c + 1) * 512],
                             P[:, c * 512:(c + 1) * 512], B.pscale[:, c:c + 1]),
                         [("po", po_ps)], (("ve", 1),))
                p_done = r.val("ve")
                for k in range(nkc):
                    tb = k % 2
                    prev = r._psT_free.get(("psT", tb), ())
                    r.op("tensor",
                         lambda e, k=k, tb=tb: e.transpose(
                             psT[:, tb, 0:128], P[:, k * 128:(k + 1) * 128],
                             B.ident[:, :]),
                         [("ve", p_done)] + list(prev) + r._PT_free.get(k, []),
                         (("pe", 1),))
                    eng = "vector" if k % 2 == 0 else "scalar"
                    sem = "ve" if k % 2 == 0 else "se"
                    r.op(eng,
                         lambda e, k=k, tb=tb, eng=eng: (
                             e.tensor_copy(PT[:, k], psT[:, tb, 0:128])
                             if eng == "vector" else
                             e.activation(PT[:, k], psT[:, tb, 0:128], AF.Copy)),
                         [("pe", r.val("pe"))], ((sem, 1),))
                    r._psT_free[("psT", tb)] = ((sem, r.val(sem)),)
                pt_done = (("ve", r.val("ve")), ("se", r.val("se")))
                for k in range(nkc):
                    r.op("tensor",
                         lambda e, k=k, h8=h8, st=(k == 0), sp=(k == nkc - 1):
                         e.matmul(
                             psC[0:64, 0, 0:128],
                             A.v0[:, k, h8 * 64:(h8 + 1) * 64],
                             PT[:, k],
                             start=st, stop=sp),
                         (list(pt_done) + [("ldx", kv_ld)] + r._psO_free)
                         if k == 0 else (),
                         (("pe", 1),) if k == nkc - 1 else ())
                for k in range(nkc):
                    r._PT_free[k] = [("pe", r.val("pe"))]
                r.op("vector",
                     lambda e, hb=hb, hc=hc: e.tensor_copy(
                         OT[hb:hb + 64, hc, :], psC[0:64, 0, 0:128]),
                     [("pe", r.val("pe"))], (("ve", 1),))
                r._psO_free = [("ve", r.val("ve"))]
            ot_done = r.val("ve")
            for oh in range(2):
                w0 = [("ve", ot_done)] + list(cfg.wo_wait) + \
                     r._psOP_free.get(oh, [])
                for c in range(4):
                    r.op("tensor",
                         lambda e, c=c, oh=oh, hg=hg: e.matmul(
                             psB[:, oh], OT[:, c, :],
                             wo[:, hg * 4 + c, oh * 512:(oh + 1) * 512],
                             start=(c == 0), stop=(c == 3)),
                         w0 if c == 0 else (),
                         (("pe", 1),) if c == 3 else ())
                pe_op = r.val("pe")
                xw = [("pe", pe_op), ("ldx", pf[i])] + r._xnw_free.get(qb, [])
                if cfg.out_bias is not None and hg == 0:
                    r.op("vector",
                         lambda e, oh=oh, qb=qb: e.tensor_add(
                             xnw[:, qb, oh * 512:(oh + 1) * 512], psB[:, oh],
                             cfg.out_bias_sb[:, oh * 512:(oh + 1) * 512]),
                         xw, (("ve", 1),))
                    r.op("vector",
                         lambda e, oh=oh, qb=qb: e.tensor_add(
                             xnw[:, qb, oh * 512:(oh + 1) * 512],
                             xnw[:, qb, oh * 512:(oh + 1) * 512],
                             xsr[:, qb, oh * 512:(oh + 1) * 512]),
                         (), (("ve", 1),))
                else:
                    r.op("vector",
                         lambda e, oh=oh, qb=qb: e.tensor_add(
                             xnw[:, qb, oh * 512:(oh + 1) * 512], psB[:, oh],
                             xsr[:, qb, oh * 512:(oh + 1) * 512]),
                         xw, (("ve", 1),))
                r._psOP_free[oh] = [("ve", r.val("ve"))]
            r._xsr_free[qb] = [("ve", r.val("ve"))]
            r.dma("sync", xout[i * 128:(i + 1) * 128, :], xnw[:, qb],
                  "stx", waits=[("ve", r.val("ve"))])
            r._xnw_free[qb] = [("stx", r.val("stx"))]
            r._qts_free[qb] = [("pe", r.val("pe"))]
    ar.release()
    r.op("sync", None, (("stx", r.val("stx")),), ())
    r.barrier()


def emit_attention(r, B, ar, cfg, nc, xin, xout, ps, cap, tag):
    A = AttnBufs(r, ar, nc, tag)
    fresh_attn_state(r)
    cfg.ps = ps
    cfg.cap = cap
    cfg.prep(r, B, ar, A, ps)
    emit_phaseA(r, B, ar, A, cfg, xin, ps)
    emit_phaseB(r, B, ar, A, cfg, xin, xout, ps)


def emit_cast_transpose(r, B, psA, psT, src_dram, nout, nin, nat, dst,
                        fold_mt=None, scale_tile=None):
    for oc in range(nout // 128):
        r.dma("gpsimd", nat[:, 0:nin], src_dram[oc * 128:(oc + 1) * 128, :],
              "wdma", waits=r._nat_free)
        wld = r.val("wdma")
        nat_src = nat
        if scale_tile is not None:
            r.op("vector",
                 lambda e, oc=oc: e.tensor_scalar_mul(
                     nat[:, 0:nin], nat[:, 0:nin], scale_tile[:, oc:oc + 1]),
                 [("wdma", wld)], (("ve", 1),))
        if fold_mt is not None:
            for nb in range((nin + 511) // 512):
                w0 = min(512, nin - nb * 512)
                bb = nb % 2
                r.op("tensor",
                     lambda e, oc=oc, nb=nb, w0=w0, bb=bb: e.matmul(
                         psA[:, bb, 0:w0], fold_mt[:, oc, :],
                         nat[:, nb * 512:nb * 512 + w0],
                         start=True, stop=True),
                     [("wdma", wld)] + r._psA_free.get(bb, []), (("pe", 1),))
                r.op("scalar",
                     lambda e, oc=oc, nb=nb, w0=w0, bb=bb: e.activation(
                         nat[:, nb * 512:nb * 512 + w0], psA[:, bb, 0:w0],
                         AF.Copy),
                     [("pe", r.val("pe"))], (("se", 1),))
                r._psA_free[bb] = [("se", r.val("se"))]
            r._nat_guard = [("se", r.val("se"))]
        else:
            r._nat_guard = [("wdma", wld)] + \
                ([("ve", r.val("ve"))] if scale_tile is not None else [])
        last_read = []
        for ic in range(nin // 128):
            tb = ic % 2
            prev = r._psT_free.get(("psT", tb), ())
            r.op("tensor",
                 lambda e, ic=ic, tb=tb: e.transpose(
                     psT[:, tb, 0:128], nat[:, ic * 128:(ic + 1) * 128],
                     B.ident[:, :]),
                 list(prev) + r._nat_guard + r._dst_guard,
                 (("pe", 1),))
            eng = "vector" if ic % 2 == 0 else "scalar"
            sem = "ve" if ic % 2 == 0 else "se"
            r.op(eng,
                 lambda e, ic=ic, oc=oc, tb=tb, eng=eng: (
                     e.tensor_copy(dst[:, ic, oc * 128:(oc + 1) * 128],
                                   psT[:, tb, 0:128])
                     if eng == "vector" else
                     e.activation(dst[:, ic, oc * 128:(oc + 1) * 128],
                                  psT[:, tb, 0:128], AF.Copy)),
                 [("pe", r.val("pe"))], ((sem, 1),))
            r._psT_free[("psT", tb)] = ((sem, r.val(sem)),)
            last_read = [("pe", r.val("pe"))]
        r._nat_free = last_read
    return [("ve", r.val("ve")), ("se", r.val("se"))]


class CfgL1AttnA:
    causal = True
    out_bias = True

    def declare(self, nc):
        g = {}
        for w in ("Wq", "Wk", "Wv", "Wo"):
            g[w] = nc.dram_tensor("l1a_" + w, [D, D], F32, kind="ExternalInput").ap()
        g["bq2d"] = nc.dram_tensor("l1a_bq2d", [128, 8], F32, kind="ExternalInput").ap()
        g["bk2d"] = nc.dram_tensor("l1a_bk2d", [128, 8], F32, kind="ExternalInput").ap()
        g["bv2d"] = nc.dram_tensor("l1a_bv2d", [128, 8], F32, kind="ExternalInput").ap()
        g["bo_bc"] = nc.dram_tensor("l1a_bo_bc", [128, D], F32, kind="ExternalInput").ap()
        self.g = g

    @staticmethod
    def host_inputs(ri, pre="L1_attn_a_"):
        return {
            "l1a_Wq": ri[pre + "Wq"], "l1a_Wk": ri[pre + "Wk"],
            "l1a_Wv": ri[pre + "Wv"], "l1a_Wo": ri[pre + "Wo"],
            "l1a_bq2d": reshape_col_chunks(ri[pre + "bq"]),
            "l1a_bk2d": reshape_col_chunks(ri[pre + "bk"]),
            "l1a_bv2d": reshape_col_chunks(ri[pre + "bv"]),
            "l1a_bo_bc": np.ascontiguousarray(
                np.broadcast_to(ri[pre + "bo"], (128, D))),
        }

    def prep(self, r, B, ar, A, ps):
        psA, psB, psC, psT = ps
        self.wqT = ar.alloc("l1a_wqT", [128, 8, D], BF16)
        self.wkT = ar.alloc("l1a_wkT", [128, 8, D], BF16)
        self.wvT = ar.alloc("l1a_wvT", [128, 8, D], BF16)
        self.bq = ar.alloc("l1a_bq", [128, 8], F32)
        self.bk = ar.alloc("l1a_bk", [128, 8], F32)
        self.bv = ar.alloc("l1a_bv", [128, 8], F32)
        self.r128 = ar.alloc("l1a_r128", [128, 128], BF16)
        r.dma("sync", self.bq[:, :], self.g["bq2d"][:, :], "ldx")
        r.dma("sync", self.bk[:, :], self.g["bk2d"][:, :], "ldx")
        r.dma("sync", self.bv[:, :], self.g["bv2d"][:, :], "ldx")
        r.dma("sync", self.r128[:, :], self.cap["R128T"][:, :], "ldx")
        ar.mark()
        nat = ar.alloc("l1a_nat", [128, 1024], BF16)
        r._nat_free = []
        r._dst_guard = []
        r._psA_free = {}
        for src, dst in ((self.g["Wq"], self.wqT), (self.g["Wk"], self.wkT),
                         (self.g["Wv"], self.wvT)):
            emit_cast_transpose(r, B, psA, psT, src, D, D, nat, dst)
        ar.release()
        r.barrier()

    def alloc_phaseA(self, r, B, ar, A):
        self.tabc = ar.alloc("l1a_tabc", [128, 512], BF16)
        self.tabs = ar.alloc("l1a_tabs", [128, 512], BF16)
        self.qraw = ar.alloc("l1a_qraw", [128, 512], BF16)
        self.ropet = ar.alloc("l1a_ropet", [128, 512], BF16)
        self._tab_free = []
        self._qraw_free = []
        A.q_done = {}
        A.k_done = {}
        A.v_done = {}
        r._psA_free = {}
        r._psC_free = {}

    def _proj_rope(self, r, B, A, ib, yT_done, wT, bias, dst_sel, done_dict,
                   rope, psA, psC):
        for oc in range(8):
            b = oc % 2
            w = list(yT_done) + r._psA_free.get(b, [])
            for kc in range(8):
                r.op("tensor",
                     lambda e, b=b, kc=kc, oc=oc: e.matmul(
                         psA[:, b], wT[:, kc, oc * 128:(oc + 1) * 128],
                         A.yT[:, kc, :], start=(kc == 0), stop=(kc == 7)),
                     w if kc == 0 else (), (("pe", 1),) if kc == 7 else ())
            pe_p = r.val("pe")
            if not rope:
                dst = dst_sel(oc)
                r.op("scalar",
                     lambda e, b=b, oc=oc, dst=dst: e.activation(
                         dst, psA[:, b], AF.Identity, scale=1.0,
                         bias=bias[:, oc:oc + 1]),
                     [("pe", pe_p)] + done_dict.get(("dst_guard", oc), []),
                     (("se", 1),))
                r._psA_free[b] = [("se", r.val("se"))]
                done_dict[oc] = [("se", r.val("se"))]
            else:
                r.op("scalar",
                     lambda e, b=b, oc=oc: e.activation(
                         self.qraw[:, :], psA[:, b], AF.Identity, scale=1.0,
                         bias=bias[:, oc:oc + 1]),
                     [("pe", pe_p)] + self._qraw_free, (("se", 1),))
                qr_se = r.val("se")
                r._psA_free[b] = [("se", qr_se)]
                b2 = oc % 2
                r.op("tensor",
                     lambda e, b2=b2: e.matmul(
                         psC[:, b2], self.r128[:, :], self.qraw[:, :],
                         start=True, stop=True),
                     [("se", qr_se)] + r._psC_free.get(b2, []), (("pe", 1),))
                pe_rot = r.val("pe")
                dst = dst_sel(oc)
                r.op("vector",
                     lambda e, dst=dst: e.tensor_mul(
                         dst, self.qraw[:, :], self.tabc[:, :]),
                     [("se", qr_se)] + self._tab_free +
                     done_dict.get(("dst_guard", oc), []),
                     (("ve", 1),))
                r.op("vector",
                     lambda e, b2=b2: e.tensor_mul(
                         self.ropet[:, :], psC[:, b2], self.tabs[:, :]),
                     [("pe", pe_rot)], (("ve", 1),))
                r._psC_free[b2] = [("ve", r.val("ve"))]
                r.op("vector",
                     lambda e, dst=dst: e.tensor_add(dst, dst, self.ropet[:, :]),
                     (), (("ve", 1),))
                self._qraw_free = [("ve", r.val("ve"))]
                done_dict[oc] = [("ve", r.val("ve"))]

    def qkv(self, r, B, A, ib, yT_done):
        psA, psB, psC, psT = self.ps
        r.dma("sync", self.tabc[:, :], self.cap["cosN1"][:, ib * 512:(ib + 1) * 512],
              "ldx", waits=self._tab_free)
        r.dma("sync", self.tabs[:, :], self.cap["sinN1"][:, ib * 512:(ib + 1) * 512],
              "ldx", waits=self._tab_free)
        tl = r.val("ldx")
        ydw = list(yT_done) + [("ldx", tl)]
        for oc in range(8):
            A.q_done[("dst_guard", oc)] = r._qstage_free
            A.k_done[("dst_guard", oc)] = r._qstage_free
            A.v_done[("dst_guard", oc)] = r._vt_free
        self._proj_rope(r, B, A, ib, ydw, self.wqT, self.bq,
                        lambda oc: A.qstage[:, oc], A.q_done, True, psA, psC)
        kdst = lambda oc: (A.kt0[:, oc, ib * 512:(ib + 1) * 512] if oc < 4
                           else A.kstage[:, oc - 4])
        self._proj_rope(r, B, A, ib, ydw, self.wkT, self.bk,
                        kdst, A.k_done, True, psA, psC)
        self._proj_rope(r, B, A, ib, ydw, self.wvT, self.bv,
                        lambda e_oc: A.vt[:, e_oc], A.v_done, False, psA, psC)
        self._tab_free = [("ve", r.val("ve"))]

    def alloc_phaseB(self, r, B, ar, A):
        psA, psB, psC, psT = self.ps
        woT = ar.alloc("l1a_woT", [128, 8, D], BF16)
        self.out_bias_sb = ar.alloc("l1a_bo", [128, D], F32)
        r.dma("sync", self.out_bias_sb[:, :], self.g["bo_bc"][:, :], "ldx")
        ar.mark()
        nat = ar.alloc("l1a_natB", [128, 1024], BF16)
        r._nat_free = []
        r._dst_guard = []
        r._psA_free = {}
        self.wo_wait = emit_cast_transpose(r, B, psA, psT, self.g["Wo"],
                                           D, D, nat, woT)
        self.wo_wait = [self.wo_wait[0], self.wo_wait[1],
                        ("ldx", r.val("ldx"))]
        ar.release()
        return woT


def fresh_attn_state(r):
    r._mst_guard = 0
    r._xs_free = []
    r._vstage_free = []
    r._vt_free = []
    r._qstage_free = []
    r._psSc_free = {}
    r._Sd_free = []
    r._psT_free = {}
    r._PT_free = {}
    r._psO_free = []
    r._psOP_free = {}
    r._qts_free = {}
    r._xsr_free = {}
    r._xnw_free = {}
    r._ystage_free = {}


class CfgL0AttnB:
    causal = False
    out_bias = None

    def declare(self, nc):
        g = {}
        g["wq"] = nc.dram_tensor("l0b_wq", [D, D], F32, kind="ExternalInput").ap()
        g["wk_a"] = nc.dram_tensor("l0b_wk_a", [256, D], F32, kind="ExternalInput").ap()
        g["wk_b"] = nc.dram_tensor("l0b_wk_b", [D, 256], F32, kind="ExternalInput").ap()
        g["wv_a"] = nc.dram_tensor("l0b_wv_a", [256, D], F32, kind="ExternalInput").ap()
        g["wv_b"] = nc.dram_tensor("l0b_wv_b", [D, 256], F32, kind="ExternalInput").ap()
        g["wo"] = nc.dram_tensor("l0b_wo", [D, D], F32, kind="ExternalInput").ap()
        self.g = g

    @staticmethod
    def host_inputs(ri, pre="L0_attn_b_"):
        return {"l0b_wq": ri[pre + "wq"], "l0b_wk_a": ri[pre + "wk_a"],
                "l0b_wk_b": ri[pre + "wk_b"], "l0b_wv_a": ri[pre + "wv_a"],
                "l0b_wv_b": ri[pre + "wv_b"], "l0b_wo": ri[pre + "wo"]}

    def prep(self, r, B, ar, A, ps):
        psA, psB, psC, psT = ps
        self.wqT = ar.alloc("l0b_wqT", [128, 8, D], BF16)
        self.wkaT = ar.alloc("l0b_wkaT", [128, 8, 256], BF16)
        self.wkbT = ar.alloc("l0b_wkbT", [128, 2, D], BF16)
        self.wvaT = ar.alloc("l0b_wvaT", [128, 8, 256], BF16)
        self.wvbT = ar.alloc("l0b_wvbT", [128, 2, D], BF16)
        self.mallT = ar.alloc("l0b_mallT", [128, 8, 128], BF16)
        r.dma("sync", self.mallT[:, :, :], self.cap["MallT"][:, :, :], "ldx")
        r.op("sync", None, (("ldx", r.val("ldx")),), ())
        ar.mark()
        nat = ar.alloc("l0b_nat", [128, 1024], BF16)
        r._nat_free = []
        r._dst_guard = []
        r._psA_free = {}
        emit_cast_transpose(r, B, psA, psT, self.g["wq"], D, D, nat, self.wqT,
                            fold_mt=self.mallT)
        emit_cast_transpose(r, B, psA, psT, self.g["wk_a"], 256, D, nat, self.wkaT)
        emit_cast_transpose(r, B, psA, psT, self.g["wk_b"], D, 256, nat, self.wkbT,
                            fold_mt=self.mallT)
        emit_cast_transpose(r, B, psA, psT, self.g["wv_a"], 256, D, nat, self.wvaT)
        emit_cast_transpose(r, B, psA, psT, self.g["wv_b"], D, 256, nat, self.wvbT)
        ar.release()
        r.barrier()

    def alloc_phaseA(self, r, B, ar, A):
        self.latb = ar.alloc("l0b_lat", [128, 2, 512], BF16)
        A.q_done = {}
        A.k_done = {}
        A.v_done = {}
        r._psA_free = {}
        r._psB_free = {}
        r._psC_free = {}
        self._lat_free = []

    def _proj(self, r, A, wT, nkc_w, dst_sel, done_dict, rhs_sel, nrhs, ydw,
              psA):
        for oc in range(8):
            b = oc % 2
            w = list(ydw) + r._psA_free.get(b, [])
            for kc in range(nrhs):
                r.op("tensor",
                     lambda e, b=b, kc=kc, oc=oc: e.matmul(
                         psA[:, b], wT[:, kc, oc * 128:(oc + 1) * 128],
                         rhs_sel(kc), start=(kc == 0), stop=(kc == nkc_w - 1)),
                     w if kc == 0 else (), (("pe", 1),) if kc == nkc_w - 1 else ())
            dst = dst_sel(oc)
            r.op("scalar",
                 lambda e, b=b, dst=dst: e.activation(dst, psA[:, b], AF.Copy),
                 [("pe", r.val("pe"))] + done_dict.get(("dst_guard", oc), []),
                 (("se", 1),))
            r._psA_free[b] = [("se", r.val("se"))]
            done_dict[oc] = [("se", r.val("se"))]

    def qkv(self, r, B, A, ib, yT_done):
        psA, psB, psC, psT = self.ps
        for oc in range(8):
            A.q_done[("dst_guard", oc)] = r._qstage_free
            A.k_done[("dst_guard", oc)] = r._qstage_free
            A.v_done[("dst_guard", oc)] = r._vt_free
        ydw = list(yT_done)
        self._proj(r, A, self.wqT, 8, lambda oc: A.qstage[:, oc], A.q_done,
                   lambda kc: A.yT[:, kc, :], 8, ydw, psA)
        kdst = lambda oc: (A.kt0[:, oc, ib * 512:(ib + 1) * 512] if oc < 4
                           else A.kstage[:, oc - 4])
        for stage2, vt_mode in ((kdst, False), (lambda oc: A.vt[:, oc], True)):
            waT = self.wvaT if vt_mode else self.wkaT
            wbT = self.wvbT if vt_mode else self.wkbT
            dd = A.v_done if vt_mode else A.k_done
            for lc in range(2):
                b = lc % 2
                w = list(ydw) + r._psC_free.get(b, []) + self._lat_free
                for kc in range(8):
                    r.op("tensor",
                         lambda e, b=b, kc=kc, lc=lc, waT=waT: e.matmul(
                             psC[:, b], waT[:, kc, lc * 128:(lc + 1) * 128],
                             A.yT[:, kc, :],
                             start=(kc == 0), stop=(kc == 7)),
                         w if kc == 0 else (), (("pe", 1),) if kc == 7 else ())
                r.op("scalar",
                     lambda e, b=b, lc=lc: e.activation(
                         self.latb[:, lc], psC[:, b], AF.Copy),
                     [("pe", r.val("pe"))], (("se", 1),))
                r._psC_free[b] = [("se", r.val("se"))]
            lat_ready = [("se", r.val("se"))]
            self._proj(r, A, wbT, 2, stage2, dd,
                       lambda lc: self.latb[:, lc], 2, lat_ready, psA)
            self._lat_free = [("pe", r.val("pe"))]

    def alloc_phaseB(self, r, B, ar, A):
        psA, psB, psC, psT = self.ps
        woT = ar.alloc("l0b_woT", [128, 8, D], BF16)
        ar.mark()
        nat = ar.alloc("l0b_natB", [128, 1024], BF16)
        r._nat_free = []
        r._dst_guard = []
        r._psA_free = {}
        self.wo_wait = emit_cast_transpose(r, B, psA, psT, self.g["wo"],
                                           D, D, nat, woT)
        ar.release()
        return woT


class CfgL1AttnB:
    causal = True
    out_bias = None

    def declare(self, nc):
        g = {}
        g["W_q"] = nc.dram_tensor("l1b_Wq", [D, D], F32, kind="ExternalInput").ap()
        g["W_dkv"] = nc.dram_tensor("l1b_Wdkv", [D, 256], F32, kind="ExternalInput").ap()
        g["W_uk"] = nc.dram_tensor("l1b_Wuk", [256, D], F32, kind="ExternalInput").ap()
        g["W_uv"] = nc.dram_tensor("l1b_Wuv", [256, D], F32, kind="ExternalInput").ap()
        g["W_o"] = nc.dram_tensor("l1b_Wo", [D, D], F32, kind="ExternalInput").ap()
        g["gam2d"] = nc.dram_tensor("l1b_gam2d", [128, 2], F32, kind="ExternalInput").ap()
        g["bet2d"] = nc.dram_tensor("l1b_bet2d", [128, 2], F32, kind="ExternalInput").ap()
        self.kb_dram = nc.dram_tensor("l1b_kb", [2, D], F32, kind="Internal").ap()
        self.g = g

    @staticmethod
    def host_inputs(ri, pre="L1_attn_b_"):
        return {"l1b_Wq": ri[pre + "W_q"], "l1b_Wdkv": ri[pre + "W_dkv"],
                "l1b_Wuk": ri[pre + "W_uk"], "l1b_Wuv": ri[pre + "W_uv"],
                "l1b_Wo": ri[pre + "W_o"],
                "l1b_gam2d": reshape_col_chunks(ri[pre + "ln_gamma"]),
                "l1b_bet2d": reshape_col_chunks(ri[pre + "ln_beta"])}

    def prep(self, r, B, ar, A, ps):
        psA, psB, psC, psT = ps
        self.wq = ar.alloc("l1b_wq", [128, 8, D], BF16)
        self.wdkv = ar.alloc("l1b_wdkv", [128, 8, 256], BF16)
        self.wuk = ar.alloc("l1b_wuk", [128, 2, D], BF16)
        self.wuv = ar.alloc("l1b_wuv", [128, 2, D], BF16)
        self.gam = ar.alloc("l1b_gam", [128, 2], F32)
        self.bet = ar.alloc("l1b_bet", [128, 2], F32)
        self.betb = ar.alloc("l1b_betb", [128, 2], BF16)
        self.kb = ar.alloc("l1b_kb", [128, 8], F32)
        self.vb = ar.alloc("l1b_vb", [128, 8], F32)
        self.kbrow = ar.alloc("l1b_kbrow", [1, D], F32)
        self.r128 = ar.alloc("l1b_r128", [128, 128], BF16)
        g = self.g
        for kc in range(8):
            r.dma("gpsimd", self.wq[:, kc], g["W_q"][kc * 128:(kc + 1) * 128, :],
                  "wdma")
            r.dma("gpsimd", self.wdkv[:, kc], g["W_dkv"][kc * 128:(kc + 1) * 128, :],
                  "wdma")
        for lc in range(2):
            r.dma("gpsimd", self.wuk[:, lc], g["W_uk"][lc * 128:(lc + 1) * 128, :],
                  "wdma")
            r.dma("gpsimd", self.wuv[:, lc], g["W_uv"][lc * 128:(lc + 1) * 128, :],
                  "wdma")
        r.dma("sync", self.gam[:, :], g["gam2d"][:, :], "ldx")
        r.dma("sync", self.bet[:, :], g["bet2d"][:, :], "ldx")
        r.dma("sync", self.r128[:, :], self.cap["R128T"][:, :], "ldx")
        wl = r.val("wdma")
        ll = r.val("ldx")
        r.op("vector",
             lambda e: e.tensor_copy(self.betb[:, :], self.bet[:, :]),
             [("ldx", ll)], (("ve", 1),))
        ve_fold = r.val("ve")
        for w_i, wsb in ((0, self.wuk), (1, self.wuv)):
            for nb in range(2):
                r.op("tensor",
                     lambda e, wsb=wsb, nb=nb: e.matmul(
                         psA[0:1, nb, 0:512], self.betb[:, 0:1],
                         wsb[:, 0, nb * 512:(nb + 1) * 512],
                         start=True, stop=False),
                     [("ve", ve_fold), ("se", r.val("se")), ("wdma", wl)], ())
                r.op("tensor",
                     lambda e, wsb=wsb, nb=nb: e.matmul(
                         psA[0:1, nb, 0:512], self.betb[:, 1:2],
                         wsb[:, 1, nb * 512:(nb + 1) * 512],
                         start=False, stop=True),
                     (), (("pe", 1),))
                r.op("scalar",
                     lambda e, nb=nb: e.activation(
                         self.kbrow[0:1, nb * 512:(nb + 1) * 512],
                         psA[0:1, nb, 0:512], AF.Copy),
                     [("pe", r.val("pe")), ("stx", r.val("stx"))], (("se", 1),))
            r.dma("sync", self.kb_dram[w_i:w_i + 1, :], self.kbrow[0:1, :],
                  "stx", waits=[("se", r.val("se"))])
            r.op("sync", None, (("stx", r.val("stx")),), ())
        pe_bias = r.val("pe")
        for wsb in (self.wuk, self.wuv):
            for lc in range(2):
                r.op("vector",
                     lambda e, wsb=wsb, lc=lc: e.tensor_scalar_mul(
                         wsb[:, lc], wsb[:, lc], self.gam[:, lc:lc + 1]),
                     [("wdma", wl), ("ldx", ll), ("pe", pe_bias)], (("ve", 1),))
        for w_i, dst in ((0, self.kb), (1, self.vb)):
            for oc in range(8):
                def fn(e, w_i=w_i, dst=dst, oc=oc):
                    return e.dma_start(
                        out=dst[:, oc:oc + 1],
                        in_=self.kb_dram[w_i, oc * 128:(oc + 1) * 128]
                        .rearrange("(p one) -> p one", one=1),
                        allow_slow_non_contiguous=True)
                r.op("gpsimd", fn, [("wdma", r.last_wait["wdma"]),
                                    ("stx", r.val("stx"))],
                     (("wdma", 16),), selfsync=False)
        r.op("gpsimd", None, (("wdma", r.val("wdma")),), ())
        r.barrier()

    def alloc_phaseA(self, r, B, ar, A):
        self.tabc = ar.alloc("l1b_tabc", [128, 512], BF16)
        self.tabs = ar.alloc("l1b_tabs", [128, 512], BF16)
        self.qraw = ar.alloc("l1b_qraw", [128, 512], BF16)
        self.ropet = ar.alloc("l1b_ropet", [128, 512], BF16)
        self.ln0 = ar.alloc("l1b_ln0", [128, 4, 256], BF16)
        self.latT = ar.alloc("l1b_latT", [128, 2, 512], BF16)
        self._tab_free = []
        self._qraw_free = []
        self._latT_free = []
        A.q_done = {}
        A.k_done = {}
        A.v_done = {}
        r._psA_free = {}
        r._psB_free = {}
        r._psC_free = {}

    def _proj_rope(self, r, B, A, ydw, wT, nk, rhs_sel, bias, dst_sel,
                   done_dict, rope, psA, psC):
        for oc in range(8):
            b = oc % 2
            w = list(ydw) + r._psA_free.get(b, [])
            for kc in range(nk):
                r.op("tensor",
                     lambda e, b=b, kc=kc, oc=oc: e.matmul(
                         psA[:, b], wT[:, kc, oc * 128:(oc + 1) * 128],
                         rhs_sel(kc), start=(kc == 0), stop=(kc == nk - 1)),
                     w if kc == 0 else (), (("pe", 1),) if kc == nk - 1 else ())
            pe_p = r.val("pe")
            guard = done_dict.get(("dst_guard", oc), [])
            if not rope:
                dst = dst_sel(oc)
                if bias is None:
                    r.op("scalar",
                         lambda e, b=b, dst=dst: e.activation(
                             dst, psA[:, b], AF.Copy),
                         [("pe", pe_p)] + guard, (("se", 1),))
                else:
                    r.op("scalar",
                         lambda e, b=b, oc=oc, dst=dst: e.activation(
                             dst, psA[:, b], AF.Identity, scale=1.0,
                             bias=bias[:, oc:oc + 1]),
                         [("pe", pe_p)] + guard, (("se", 1),))
                r._psA_free[b] = [("se", r.val("se"))]
                done_dict[oc] = [("se", r.val("se"))]
            else:
                if bias is None:
                    r.op("scalar",
                         lambda e, b=b: e.activation(
                             self.qraw[:, :], psA[:, b], AF.Copy),
                         [("pe", pe_p)] + self._qraw_free, (("se", 1),))
                else:
                    r.op("scalar",
                         lambda e, b=b, oc=oc: e.activation(
                             self.qraw[:, :], psA[:, b], AF.Identity, scale=1.0,
                             bias=bias[:, oc:oc + 1]),
                         [("pe", pe_p)] + self._qraw_free, (("se", 1),))
                qr_se = r.val("se")
                r._psA_free[b] = [("se", qr_se)]
                b2 = oc % 2
                r.op("tensor",
                     lambda e, b2=b2: e.matmul(
                         psC[:, b2], self.r128[:, :], self.qraw[:, :],
                         start=True, stop=True),
                     [("se", qr_se)] + r._psC_free.get(b2, []), (("pe", 1),))
                pe_rot = r.val("pe")
                dst = dst_sel(oc)
                r.op("vector",
                     lambda e, dst=dst: e.tensor_mul(
                         dst, self.qraw[:, :], self.tabc[:, :]),
                     [("se", qr_se)] + self._tab_free + guard, (("ve", 1),))
                r.op("vector",
                     lambda e, b2=b2: e.tensor_mul(
                         self.ropet[:, :], psC[:, b2], self.tabs[:, :]),
                     [("pe", pe_rot)], (("ve", 1),))
                r._psC_free[b2] = [("ve", r.val("ve"))]
                r.op("vector",
                     lambda e, dst=dst: e.tensor_add(dst, dst, self.ropet[:, :]),
                     (), (("ve", 1),))
                self._qraw_free = [("ve", r.val("ve"))]
                done_dict[oc] = [("ve", r.val("ve"))]

    def qkv(self, r, B, A, ib, yT_done):
        psA, psB, psC, psT = self.ps
        r.dma("sync", self.tabc[:, :], self.cap["cosN2"][:, ib * 512:(ib + 1) * 512],
              "ldx", waits=self._tab_free)
        r.dma("sync", self.tabs[:, :], self.cap["sinN2"][:, ib * 512:(ib + 1) * 512],
              "ldx", waits=self._tab_free)
        tl = r.val("ldx")
        ydw = list(yT_done) + [("ldx", tl)]
        for oc in range(8):
            A.q_done[("dst_guard", oc)] = r._qstage_free
            A.k_done[("dst_guard", oc)] = r._qstage_free
            A.v_done[("dst_guard", oc)] = r._vt_free
        self._proj_rope(r, B, A, ydw, self.wq, 8, lambda kc: A.yT[:, kc, :],
                        None, lambda oc: A.qstage[:, oc], A.q_done, True,
                        psA, psC)
        self.lat_ln(r, B, A, ydw, psB, psT)
        lat_ready = [("ve", r.val("ve")), ("se", r.val("se"))]
        kdst = lambda oc: (A.kt0[:, oc, ib * 512:(ib + 1) * 512] if oc < 4
                           else A.kstage[:, oc - 4])
        self._proj_rope(r, B, A, lat_ready, self.wuk, 2,
                        lambda lc: self.latT[:, lc, :], self.kb, kdst,
                        A.k_done, True, psA, psC)
        self._proj_rope(r, B, A, lat_ready, self.wuv, 2,
                        lambda lc: self.latT[:, lc, :], self.vb,
                        lambda oc: A.vt[:, oc], A.v_done, False, psA, psC)
        self._latT_free = [("pe", r.val("pe"))]
        self._tab_free = [("ve", r.val("ve"))]

    def lat_ln(self, r, B, A, ydw, psB, psT):
        for tt in range(4):
            b = tt % 2
            w = list(ydw) + r._psB_free.get(b, []) + self._latT_free
            for kc in range(8):
                r.op("tensor",
                     lambda e, b=b, kc=kc, tt=tt: e.matmul(
                         psB[:, b, 0:256], A.yT[:, kc, tt * 128:(tt + 1) * 128],
                         self.wdkv[:, kc, :], start=(kc == 0), stop=(kc == 7)),
                     w if kc == 0 else (), (("pe", 1),) if kc == 7 else ())
            pe_lat = r.val("pe")
            jc = 12 + tt
            r.op("vector",
                 lambda e, b=b, jc=jc: e.reduce_sum(
                     B.mu_raw[:, jc:jc + 1], psB[:, b, 0:256], axis=AX.X),
                 [("pe", pe_lat), ("po", r.val("po"))], (("ve", 1),))
            r.op("scalar",
                 lambda e, b=b, jc=jc: e.activation(
                     B.trash[:, 0:256], psB[:, b, 0:256], AF.Square,
                     accum_out=B.s2[:, jc:jc + 1]),
                 [("pe", pe_lat), ("ve", r.val("ve"))], (("se", 1),))
            r.op("gpsimd",
                 lambda e, jc=jc: e.tensor_scalar_mul(
                     B.mu[:, jc:jc + 1], B.mu_raw[:, jc:jc + 1], 1.0 / 256),
                 [("ve", r.val("ve"))], (("po", 1),))
            r.op("gpsimd",
                 lambda e, jc=jc: e.tensor_mul(
                     B.mu2[:, jc:jc + 1], B.mu[:, jc:jc + 1], B.mu[:, jc:jc + 1]),
                 (), (("po", 1),))
            r.op("vector",
                 lambda e, jc=jc: e.scalar_tensor_tensor(
                     B.var[:, jc:jc + 1], B.s2[:, jc:jc + 1], 1.0 / 256,
                     B.mu2[:, jc:jc + 1], ALU.mult, ALU.subtract),
                 (("se", r.val("se")), ("po", r.val("po"))), (("ve", 1),))
            r.op("gpsimd",
                 lambda e, jc=jc: e.tensor_scalar_add(
                     B.var[:, jc:jc + 1], B.var[:, jc:jc + 1], B.ln_eps),
                 (("ve", r.val("ve")),), (("po", 1),))
            r.op("scalar",
                 lambda e, jc=jc: e.activation(
                     B.sd[:, jc:jc + 1], B.var[:, jc:jc + 1], AF.Sqrt),
                 (("po", r.val("po")),), (("se", 1),))
            r.op("vector",
                 lambda e, jc=jc: e.reciprocal(
                     B.rstd[:, jc:jc + 1], B.sd[:, jc:jc + 1]),
                 (("se", r.val("se")),), (("ve", 1),))
            r.op("vector",
                 lambda e, jc=jc: e.scalar_tensor_tensor(
                     B.nmr[:, jc:jc + 1], B.mu[:, jc:jc + 1], -1.0,
                     B.rstd[:, jc:jc + 1], ALU.mult, ALU.mult),
                 (), (("ve", 1),))
            r.op("scalar",
                 lambda e, b=b, tt=tt, jc=jc: e.activation(
                     self.ln0[:, tt], psB[:, b, 0:256], AF.Identity,
                     scale=B.rstd[:, jc:jc + 1], bias=B.nmr[:, jc:jc + 1]),
                 (("ve", r.val("ve")), ("po", r.val("po"))), (("se", 1),))
            r._psB_free[b] = [("se", r.val("se"))]
            for lc in range(2):
                emit_transpose(r, B, psT, self.ln0[:, tt, lc * 128:(lc + 1) * 128],
                               self.latT[:, lc, tt * 128:(tt + 1) * 128],
                               (tt * 2 + lc) % 2,
                               [("se", r.val("se"))],
                               evac_eng="vector" if lc == 0 else "scalar")

    def alloc_phaseB(self, r, B, ar, A):
        wo = ar.alloc("l1b_wo", [128, 8, D], BF16)
        for kc in range(8):
            r.dma("gpsimd", wo[:, kc], self.g["W_o"][kc * 128:(kc + 1) * 128, :],
                  "wdma")
        self.wo_wait = [("wdma", r.val("wdma"))]
        return wo


class CfgL0AttnA:
    causal = True
    out_bias = None

    def declare(self, nc):
        g = {}
        g["Wq_down"] = nc.dram_tensor("l0a_Wqd", [D, D], F32, kind="ExternalInput").ap()
        g["Wk_down"] = nc.dram_tensor("l0a_Wkd", [D, D], F32, kind="ExternalInput").ap()
        g["Wv_down"] = nc.dram_tensor("l0a_Wvd", [D, D], F32, kind="ExternalInput").ap()
        g["Wq_up_c"] = nc.dram_tensor("l0a_Wquc", [64, 32], F32, kind="ExternalInput").ap()
        g["Wq_up_e"] = nc.dram_tensor("l0a_Wque", [64, 32], F32, kind="ExternalInput").ap()
        g["Wk_up_c"] = nc.dram_tensor("l0a_Wkuc", [64, 32], F32, kind="ExternalInput").ap()
        g["Wk_up_e"] = nc.dram_tensor("l0a_Wkue", [64, 32], F32, kind="ExternalInput").ap()
        g["Wv_up"] = nc.dram_tensor("l0a_Wvu", [64, 64], F32, kind="ExternalInput").ap()
        g["Wc"] = nc.dram_tensor("l0a_Wc", [D, D], F32, kind="ExternalInput").ap()
        self.g = g

    @staticmethod
    def host_inputs(ri, pre="L0_attn_a_"):
        return {"l0a_Wqd": ri[pre + "Wq_down"], "l0a_Wkd": ri[pre + "Wk_down"],
                "l0a_Wvd": ri[pre + "Wv_down"],
                "l0a_Wquc": ri[pre + "Wq_up_c"], "l0a_Wque": ri[pre + "Wq_up_e"],
                "l0a_Wkuc": ri[pre + "Wk_up_c"], "l0a_Wkue": ri[pre + "Wk_up_e"],
                "l0a_Wvu": ri[pre + "Wv_up"], "l0a_Wc": ri[pre + "Wc"]}

    def prep(self, r, B, ar, A, ps):
        self.wqd = ar.alloc("l0a_wqd", [128, 8, D], BF16)
        self.wkd = ar.alloc("l0a_wkd", [128, 8, D], BF16)
        self.wvd = ar.alloc("l0a_wvd", [128, 8, D], BF16)
        self.uqc = ar.alloc("l0a_uqc", [128, 32], BF16)
        self.uqe = ar.alloc("l0a_uqe", [128, 32], BF16)
        self.ukc = ar.alloc("l0a_ukc", [128, 32], BF16)
        self.uke = ar.alloc("l0a_uke", [128, 32], BF16)
        self.uv = ar.alloc("l0a_uv", [128, 64], BF16)
        self.rhat = ar.alloc("l0a_rhat", [128, 128], BF16)
        g = self.g
        for src, dst in ((g["Wq_down"], self.wqd), (g["Wk_down"], self.wkd),
                         (g["Wv_down"], self.wvd)):
            for kc in range(8):
                r.dma("gpsimd", dst[:, kc], src[kc * 128:(kc + 1) * 128, :],
                      "wdma")
        for src, dst in ((g["Wq_up_c"], self.uqc), (g["Wq_up_e"], self.uqe),
                         (g["Wk_up_c"], self.ukc), (g["Wk_up_e"], self.uke),
                         (g["Wv_up"], self.uv)):
            r.dma("gpsimd", dst[0:64, :], src[:, :], "wdma")
            r.dma("gpsimd", dst[64:128, :], src[:, :], "wdma")
        r.dma("sync", self.rhat[:, :], self.cap["RhatAT"][:, :], "ldx")
        r.op("sync", None, (("ldx", r.val("ldx")),), ())
        r.op("gpsimd", None, (("wdma", r.val("wdma")),), ())
        r.barrier()

    def alloc_phaseA(self, r, B, ar, A):
        self.latb = ar.alloc("l0a_lat", [128, 8, 512], BF16)
        self.estage = ar.alloc("l0a_est", [128, 512], BF16)
        self.ropet = ar.alloc("l0a_rt", [128, 512], BF16)
        self.tabc = ar.alloc("l0a_tabc", [128, 512], BF16)
        self.tabs = ar.alloc("l0a_tabs", [128, 512], BF16)
        r.op("gpsimd", lambda e: e.memset(self.estage[:, :], 0.0), (),
             (("po", 1),))
        self._tab_free = []
        self._est_free = []
        self._lat_free = []
        A.q_done = {}
        A.k_done = {}
        A.v_done = {}
        r._psA_free = {}
        r._psB_free = {}
        r._psC_free = {}

    def qkv(self, r, B, A, ib, yT_done):
        psA, psB, psC, psT = self.ps
        r.dma("sync", self.tabc[:, :], self.cap["cosA"][:, ib * 512:(ib + 1) * 512],
              "ldx", waits=self._tab_free)
        r.dma("sync", self.tabs[:, :], self.cap["sinA"][:, ib * 512:(ib + 1) * 512],
              "ldx", waits=self._tab_free)
        tl = r.val("ldx")
        ydw = list(yT_done) + [("ldx", tl)]
        for oc in range(8):
            A.q_done[("dst_guard", oc)] = r._qstage_free
            A.k_done[("dst_guard", oc)] = r._qstage_free
            A.v_done[("dst_guard", oc)] = r._vt_free
        kdst = lambda oc: (A.kt0[:, oc, ib * 512:(ib + 1) * 512] if oc < 4
                           else A.kstage[:, oc - 4])
        for mode, wd, uc, ue, dst_sel, dd in (
                ("qk", self.wqd, self.uqc, self.uqe,
                 lambda hp: A.qstage[:, hp], A.q_done),
                ("qk", self.wkd, self.ukc, self.uke, kdst, A.k_done),
                ("v", self.wvd, None, None, lambda hp: A.vt[:, hp], A.v_done)):
            for oc in range(8):
                b = oc % 2
                w = list(ydw) + r._psA_free.get(b, []) + self._lat_free
                for kc in range(8):
                    r.op("tensor",
                         lambda e, b=b, kc=kc, oc=oc, wd=wd: e.matmul(
                             psA[:, b], wd[:, kc, oc * 128:(oc + 1) * 128],
                             A.yT[:, kc, :], start=(kc == 0), stop=(kc == 7)),
                         w if kc == 0 else (), (("pe", 1),) if kc == 7 else ())
                r.op("scalar",
                     lambda e, b=b, oc=oc: e.activation(
                         self.latb[:, oc], psA[:, b], AF.Copy),
                     [("pe", r.val("pe"))], (("se", 1),))
                r._psA_free[b] = [("se", r.val("se"))]
            lat_rdy = r.val("se")
            self._lat_free = []
            for hp in range(8):
                b = hp % 2
                w = [("se", lat_rdy)] + r._psB_free.get(b, [])
                guard = dd.get(("dst_guard", hp), [])
                if mode == "v":
                    r.op("tensor",
                         lambda e, b=b, hp=hp: e.matmul(
                             psB[0:64, b], self.uv[0:64, :],
                             self.latb[0:64, hp, :], start=True, stop=True,
                             tile_position=(0, 0)),
                         w, ())
                    r.op("tensor",
                         lambda e, b=b, hp=hp: e.matmul(
                             psB[64:128, b], self.uv[64:128, :],
                             self.latb[64:128, hp, :], start=True, stop=True,
                             tile_position=(64, 64)),
                         (), (("pe", 1),))
                    dst = dst_sel(hp)
                    r.op("scalar",
                         lambda e, b=b, dst=dst: e.activation(
                             dst, psB[:, b], AF.Copy),
                         [("pe", r.val("pe"))] + guard, (("se", 1),))
                    r._psB_free[b] = [("se", r.val("se"))]
                    dd[hp] = [("se", r.val("se"))]
                    continue
                r.op("tensor",
                     lambda e, b=b, hp=hp, uc=uc: e.matmul(
                         psB[0:32, b], uc[0:64, :], self.latb[0:64, hp, :],
                         start=True, stop=True, tile_position=(0, 0)),
                     w, ())
                r.op("tensor",
                     lambda e, b=b, hp=hp, ue=ue: e.matmul(
                         psB[32:64, b], ue[0:64, :], self.latb[0:64, hp, :],
                         start=True, stop=True, tile_position=(0, 32)),
                     (), ())
                r.op("tensor",
                     lambda e, b=b, hp=hp, uc=uc: e.matmul(
                         psB[64:96, b], uc[64:128, :], self.latb[64:128, hp, :],
                         start=True, stop=True, tile_position=(64, 64)),
                     (), ())
                r.op("tensor",
                     lambda e, b=b, hp=hp, ue=ue: e.matmul(
                         psB[96:128, b], ue[64:128, :], self.latb[64:128, hp, :],
                         start=True, stop=True, tile_position=(64, 96)),
                     (), (("pe", 1),))
                pe_up = r.val("pe")
                dst = dst_sel(hp)
                r.op("scalar",
                     lambda e, b=b, dst=dst: e.activation(
                         dst[0:32, :], psB[0:32, b], AF.Copy),
                     [("pe", pe_up)] + guard, (("se", 1),))
                r.op("scalar",
                     lambda e, b=b, dst=dst: e.activation(
                         dst[64:96, :], psB[64:96, b], AF.Copy),
                     (), (("se", 1),))
                r.op("scalar",
                     lambda e, b=b: e.activation(
                         self.estage[32:64, :], psB[32:64, b], AF.Copy),
                     self._est_free, (("se", 1),))
                r.op("scalar",
                     lambda e, b=b: e.activation(
                         self.estage[96:128, :], psB[96:128, b], AF.Copy),
                     (), (("se", 1),))
                est_se = r.val("se")
                r._psB_free[b] = [("se", est_se)]
                b2 = hp % 2
                r.op("tensor",
                     lambda e, b2=b2: e.matmul(
                         psC[:, b2], self.rhat[:, :], self.estage[:, :],
                         start=True, stop=True),
                     [("se", est_se), ("po", 1)] + r._psC_free.get(b2, []),
                     (("pe", 1),))
                pe_rot = r.val("pe")
                for base in (32, 96):
                    r.op("vector",
                         lambda e, dst=dst, base=base: e.tensor_mul(
                             dst[base:base + 32, :], self.estage[base:base + 32, :],
                             self.tabc[base:base + 32, :]),
                         [("se", est_se), ("ldx", tl)] + guard, (("ve", 1),))
                    r.op("vector",
                         lambda e, b2=b2, base=base: e.tensor_mul(
                             self.ropet[base:base + 32, :], psC[base:base + 32, b2],
                             self.tabs[base:base + 32, :]),
                         [("pe", pe_rot)], (("ve", 1),))
                    r.op("vector",
                         lambda e, dst=dst, base=base: e.tensor_add(
                             dst[base:base + 32, :], dst[base:base + 32, :],
                             self.ropet[base:base + 32, :]),
                         (), (("ve", 1),))
                r._psC_free[b2] = [("ve", r.val("ve"))]
                self._est_free = [("pe", pe_rot), ("ve", r.val("ve"))]
                dd[hp] = [("ve", r.val("ve")), ("se", r.val("se"))]
            self._lat_free = [("pe", r.val("pe"))]
        self._tab_free = [("ve", r.val("ve"))]

    def alloc_phaseB(self, r, B, ar, A):
        wo = ar.alloc("l0a_wc", [128, 8, D], BF16)
        for kc in range(8):
            r.dma("gpsimd", wo[:, kc], self.g["Wc"][kc * 128:(kc + 1) * 128, :],
                  "wdma")
        self.wo_wait = [("wdma", r.val("wdma"))]
        return wo


class FfnL1:
    def declare(self, nc):
        g = {}
        g["w1"] = nc.dram_tensor("f1_w1", [4096, D], F32, kind="ExternalInput").ap()
        g["w2"] = nc.dram_tensor("f1_w2", [4096, D], F32, kind="ExternalInput").ap()
        g["h1"] = nc.dram_tensor("f1_h1", [4096, 4096], F32, kind="ExternalInput").ap()
        g["h2"] = nc.dram_tensor("f1_h2", [4096, 4096], F32, kind="ExternalInput").ap()
        g["w3"] = nc.dram_tensor("f1_w3", [D, 4096], F32, kind="ExternalInput").ap()
        g["s1"] = nc.dram_tensor("f1_s1", [128, 32], F32, kind="ExternalInput").ap()
        g["s2"] = nc.dram_tensor("f1_s2", [128, 32], F32, kind="ExternalInput").ap()
        g["sh1"] = nc.dram_tensor("f1_sh1", [128, 32], F32, kind="ExternalInput").ap()
        g["sh2"] = nc.dram_tensor("f1_sh2", [128, 32], F32, kind="ExternalInput").ap()
        g["s3"] = nc.dram_tensor("f1_s3", [128, 8], F32, kind="ExternalInput").ap()
        self.w1t = nc.dram_tensor("f1_w1t", [D, 4096], BF16, kind="Internal").ap()
        self.w2t = nc.dram_tensor("f1_w2t", [D, 4096], BF16, kind="Internal").ap()
        self.h1t = nc.dram_tensor("f1_h1t", [4096, 4096], BF16, kind="Internal").ap()
        self.h2t = nc.dram_tensor("f1_h2t", [4096, 4096], BF16, kind="Internal").ap()
        self.w3t = nc.dram_tensor("f1_w3t", [4096, D], BF16, kind="Internal").ap()
        self.g = g

    @staticmethod
    def host_inputs(ri, pre="L1_ffn_"):
        return {"f1_w1": ri[pre + "w1_w"], "f1_w2": ri[pre + "w2_w"],
                "f1_h1": ri[pre + "h1_w"], "f1_h2": ri[pre + "h2_w"],
                "f1_w3": ri[pre + "w3_w"],
                "f1_s1": reshape_col_chunks(ri[pre + "w1_s"]),
                "f1_s2": reshape_col_chunks(ri[pre + "w2_s"]),
                "f1_sh1": reshape_col_chunks(ri[pre + "h1_s"]),
                "f1_sh2": reshape_col_chunks(ri[pre + "h2_s"]),
                "f1_s3": reshape_col_chunks(ri[pre + "w3_s"])}

    def prep(self, r, B, ar, ps):
        psA, psB, psC, psT = ps
        ar.mark()
        scl = ar.alloc("f1_scl", [128, 32], F32)
        natg = ar.alloc("f1_natg", [128, 8, 4096], BF16)
        tst = ar.alloc("f1_tst", [128, 2, 1024], BF16)
        r._psT_free = {}
        jobs = (
            (self.g["w1"], self.g["s1"], self.w1t, 4096, D),
            (self.g["w2"], self.g["s2"], self.w2t, 4096, D),
            (self.g["h1"], self.g["sh1"], self.h1t, 4096, 4096),
            (self.g["h2"], self.g["sh2"], self.h2t, 4096, 4096),
            (self.g["w3"], self.g["s3"], self.w3t, D, 4096),
        )
        nat_free = []
        tst_free = {}
        for (wsrc, ssrc, wdst, nout, nin) in jobs:
            r.dma("sync", scl[:, 0:nout // 128], ssrc[:, 0:nout // 128], "ldx",
                  waits=[("ve", r.val("ve"))])
            sl = r.val("ldx")
            for grp in range(nout // 1024):
                for j in range(8):
                    oc = grp * 8 + j
                    r.dma("gpsimd", natg[:, j, 0:nin],
                          wsrc[oc * 128:(oc + 1) * 128, :], "wdma",
                          waits=nat_free)
                wl = r.val("wdma")
                for j in range(8):
                    oc = grp * 8 + j
                    r.op("vector",
                         lambda e, j=j, oc=oc, nin=nin: e.tensor_scalar_mul(
                             natg[:, j, 0:nin], natg[:, j, 0:nin],
                             scl[:, oc:oc + 1]),
                         [("wdma", wl), ("ldx", sl)], (("ve", 1),))
                sc_ve = r.val("ve")
                for ic in range(nin // 128):
                    for j in range(8):
                        tb = j % 2
                        prev = r._psT_free.get(("psT", tb), ())
                        r.op("tensor",
                             lambda e, j=j, ic=ic, tb=tb: e.transpose(
                                 psT[:, tb, 0:128],
                                 natg[:, j, ic * 128:(ic + 1) * 128],
                                 B.ident[:, :]),
                             [("ve", sc_ve)] + list(prev) +
                             tst_free.get(ic % 2, []),
                             (("pe", 1),))
                        eng = "vector" if j % 2 == 0 else "scalar"
                        sem = "ve" if j % 2 == 0 else "se"
                        r.op(eng,
                             lambda e, j=j, ic=ic, tb=tb, eng=eng: (
                                 e.tensor_copy(tst[:, ic % 2, j * 128:(j + 1) * 128],
                                               psT[:, tb, 0:128])
                                 if eng == "vector" else
                                 e.activation(tst[:, ic % 2, j * 128:(j + 1) * 128],
                                              psT[:, tb, 0:128], AF.Copy)),
                             [("pe", r.val("pe"))], ((sem, 1),))
                        r._psT_free[("psT", tb)] = ((sem, r.val(sem)),)
                    r.dma("sync",
                          wdst[ic * 128:(ic + 1) * 128,
                               grp * 1024:(grp + 1) * 1024],
                          tst[:, ic % 2], "stx",
                          waits=[("ve", r.val("ve")), ("se", r.val("se"))])
                    tst_free[ic % 2] = [("stx", r.val("stx"))]
                nat_free = [("pe", r.val("pe"))]
        r.op("sync", None, (("stx", r.val("stx")),), ())
        ar.release()
        r.barrier()

    def emit(self, r, B, ar, ps, xin, xout):
        psA, psB, psC, psT = ps
        TB = 1024
        ar.mark()
        ystage = ar.alloc("f1_y", [128, 2, 1024], BF16)
        yT = ar.alloc("f1_yT", [128, 8, TB], BF16)
        xs = ar.alloc("f1_xs", [128, 2, 1024], F32)
        hblk = ar.alloc("f1_h", [128, 32, TB], BF16)
        gblk = ar.alloc("f1_g", [128, 32, TB], BF16)
        wbuf = ar.alloc("f1_wb", [128, 2, 2, 8, 128], BF16)
        hbuf = ar.alloc("f1_hb", [128, 2, 32, 128], BF16)
        silu = ar.alloc("f1_silu", [128, 2, 512], BF16)
        xnw = ar.alloc("f1_xn", [128, 2, 1024], F32)

        w1r = self.w1t.rearrange("(a p) n -> p a n", p=128)
        w2r = self.w2t.rearrange("(a p) n -> p a n", p=128)
        h1r = self.h1t.rearrange("(a p) n -> p a n", p=128)
        h2r = self.h2t.rearrange("(a p) n -> p a n", p=128)
        w3r = self.w3t.rearrange("(a p) n -> p a n", p=128)

        state = {"xs_free": {}, "y_free": {}, "psA": {}, "psB": {},
                 "w_free": {}, "h_free": {}, "silu_free": {}, "xnw_free": {},
                 "blk_free": [], "w3_free": []}
        r._ystage_free = {}
        r._psT_free = {}

        for tb in range(4):
            t0 = tb * TB
            for j in range(8):
                jb = j % 2
                r.dma("sync", xs[:, jb], xin[t0 + j * 128: t0 + (j + 1) * 128, :],
                      "ldx", waits=state["xs_free"].get(jb, []))
                ld = r.val("ldx")
                yd = emit_ln(r, B, [(xs[:, jb], ystage[:, jb])], [j % 4],
                             [("ldx", ld)] + state["y_free"].get(jb, []))
                for c in range(8):
                    emit_transpose(r, B, psT, ystage[:, jb, c * 128:(c + 1) * 128],
                                   yT[:, c, j * 128:(j + 1) * 128],
                                   (j * 8 + c) % 2, [("se", yd)],
                                   evac_eng="vector" if c % 2 == 0 else "scalar")
                state["y_free"][jb] = [("pe", r.val("pe"))]
                state["xs_free"][jb] = [("pe", r.val("pe"))]
            yT_rdy = (("ve", r.val("ve")), ("se", r.val("se")))

            for oc in range(32):
                wb = oc % 2
                r.dma("sync", wbuf[:, wb, 0], w1r[:, :, oc * 128:(oc + 1) * 128],
                      "ldx", waits=state["w_free"].get(wb, []))
                r.dma("sync", wbuf[:, wb, 1], w2r[:, :, oc * 128:(oc + 1) * 128],
                      "ldx", waits=[])
                wl = r.val("ldx")
                for nb in range(2):
                    b = nb
                    wA = [("ldx", wl)] + list(yT_rdy) + state["psA"].get(b, [])
                    for kc in range(8):
                        r.op("tensor",
                             lambda e, b=b, kc=kc, nb=nb, wb=wb: e.matmul(
                                 psA[:, b], wbuf[:, wb, 0, kc, :],
                                 yT[:, kc, nb * 512:(nb + 1) * 512],
                                 start=(kc == 0), stop=(kc == 7)),
                             wA if kc == 0 else (),
                             (("pe", 1),) if kc == 7 else ())
                    wB = state["psB"].get(b, [])
                    for kc in range(8):
                        r.op("tensor",
                             lambda e, b=b, kc=kc, nb=nb, wb=wb: e.matmul(
                                 psB[:, b], wbuf[:, wb, 1, kc, :],
                                 yT[:, kc, nb * 512:(nb + 1) * 512],
                                 start=(kc == 0), stop=(kc == 7)),
                             wB if kc == 0 else (),
                             (("pe", 1),) if kc == 7 else ())
                    pe_g = r.val("pe")
                    r.op("scalar",
                         lambda e, b=b: e.activation(
                             silu[:, b], psA[:, b], AF.Silu),
                         [("pe", pe_g - 1)] + state["silu_free"].get(b, []),
                         (("se", 1),))
                    state["psA"][b] = [("se", r.val("se"))]
                    r.op("vector",
                         lambda e, b=b, oc=oc, nb=nb: e.tensor_mul(
                             hblk[:, oc, nb * 512:(nb + 1) * 512],
                             psB[:, b], silu[:, b]),
                         [("pe", pe_g), ("se", r.val("se"))] +
                         state["blk_free"],
                         (("ve", 1),))
                    state["psB"][b] = [("ve", r.val("ve"))]
                    state["silu_free"][b] = [("ve", r.val("ve"))]
                state["w_free"][wb] = [("pe", r.val("pe"))]
            state["blk_free"] = []

            for stage, (hsrc, src_blk, dst_blk) in enumerate(
                    ((h1r, hblk, gblk), (h2r, gblk, hblk))):
                src_rdy = (("ve", r.val("ve")), ("se", r.val("se")))
                for oc in range(32):
                    wb = oc % 2
                    r.dma("sync", hbuf[:, wb], hsrc[:, :, oc * 128:(oc + 1) * 128],
                          "ldx", waits=state["h_free"].get(wb, []))
                    wl = r.val("ldx")
                    for nb in range(2):
                        b = nb
                        wA = [("ldx", wl)] + list(src_rdy) + \
                            state["psA"].get(b, [])
                        for kc in range(32):
                            r.op("tensor",
                                 lambda e, b=b, kc=kc, nb=nb, wb=wb,
                                 src_blk=src_blk: e.matmul(
                                     psA[:, b], hbuf[:, wb, kc, :],
                                     src_blk[:, kc, nb * 512:(nb + 1) * 512],
                                     start=(kc == 0), stop=(kc == 31)),
                                 wA if kc == 0 else (),
                                 (("pe", 1),) if kc == 31 else ())
                        r.op("scalar",
                             lambda e, b=b, oc=oc, nb=nb, dst_blk=dst_blk:
                             e.activation(
                                 dst_blk[:, oc, nb * 512:(nb + 1) * 512],
                                 psA[:, b], AF.Gelu),
                             [("pe", r.val("pe"))], (("se", 1),))
                        state["psA"][b] = [("se", r.val("se"))]
                    state["h_free"][wb] = [("pe", r.val("pe"))]

            hc_rdy = (("se", r.val("se")),)
            pe_c = r.val("pe")
            for q4 in range(4):
                r.dma("sync", gblk[:, q4 * 8:(q4 + 1) * 8, :],
                      w3r[:, q4 * 8:(q4 + 1) * 8, :], "ldx",
                      waits=[("pe", pe_c)] if q4 == 0 else [])
            w3l = r.val("ldx")
            for j in range(8):
                jb = j % 2
                r.dma("sync", xs[:, jb], xin[t0 + j * 128: t0 + (j + 1) * 128, :],
                      "ldx", waits=state["xs_free"].get(jb, []))
                xl = r.val("ldx")
                for oh in range(2):
                    b = oh
                    wD = [("ldx", w3l)] + list(hc_rdy) + state["psA"].get(b, [])
                    for kc in range(32):
                        r.op("tensor",
                             lambda e, b=b, kc=kc, j=j, oh=oh: e.matmul(
                                 psA[:, b], hblk[:, kc, j * 128:(j + 1) * 128],
                                 gblk[:, kc, oh * 512:(oh + 1) * 512],
                                 start=(kc == 0), stop=(kc == 31)),
                             wD if kc == 0 else (),
                             (("pe", 1),) if kc == 31 else ())
                    r.op("vector",
                         lambda e, b=b, jb=jb, oh=oh: e.tensor_add(
                             xnw[:, jb, oh * 512:(oh + 1) * 512], psA[:, b],
                             xs[:, jb, oh * 512:(oh + 1) * 512]),
                         [("pe", r.val("pe")), ("ldx", xl)] +
                         state["xnw_free"].get(jb, []),
                         (("ve", 1),))
                    state["psA"][b] = [("ve", r.val("ve"))]
                r.dma("sync", xout[t0 + j * 128: t0 + (j + 1) * 128, :],
                      xnw[:, jb], "stx", waits=[("ve", r.val("ve"))])
                state["xnw_free"][jb] = [("stx", r.val("stx"))]
                state["xs_free"][jb] = [("ve", r.val("ve"))]
            state["h_free"] = {0: [("pe", r.val("pe"))], 1: [("pe", r.val("pe"))]}
            state["blk_free"] = [("pe", r.val("pe"))]
        ar.release()
        r.op("sync", None, (("stx", r.val("stx")),), ())
        r.barrier()


def fresh_state(r):
    r._psA_free = {}
    r._psB_free = {}
    r._psC_free = {}
    r._psT_free = {}
    r._ystage_free = {}
    r._silu_free = {}
    r._h_free = {}
    r._wgwu_free = []
    r._wd_free = []


def declare_consts(nc, consts):
    cap = {}
    for k, v in consts.items():
        dt = BF16 if v.dtype == ml_dtypes.bfloat16 else F32
        cap[k] = nc.dram_tensor("c_" + k, list(v.shape), dt,
                                kind="ExternalInput").ap()
    return cap


def build_attn_program(cfg, tag):
    nc = bass.Bass("TRN2", target_bir_lowering=False)
    consts = host_consts()
    cap = declare_consts(nc, consts)
    xin = nc.dram_tensor("x_in", [T, D], F32, kind="ExternalInput").ap()
    xout = nc.dram_tensor("x_out", [T, D], F32, kind="ExternalOutput").ap()
    cfg.declare(nc)

    ar = Arena(nc)
    B = Bufs(ar)
    psA = nc.alloc_psum_tensor("psA", [128, 2, 512], F32).ap()
    psB = nc.alloc_psum_tensor("psB", [128, 2, 512], F32).ap()
    psC = nc.alloc_psum_tensor("psC", [128, 2, 512], F32).ap()
    psT = nc.alloc_psum_tensor("psT", [128, 2, 1024], BF16).ap()
    ps = (psA, psB, psC, psT)

    r = Rec(nc)
    for s in ("ldx", "stx", "wdma", "pe", "ve", "se", "po", "bar"):
        r.sem(s)
    fresh_state(r)
    emit_load_consts(r, B, cap)
    emit_attention(r, B, ar, cfg, nc, xin, xout, ps, cap, tag)
    r.op("sync", None, (("stx", r.val("stx")),), ())
    r.emit()
    return nc, consts


def build_ffn_l1_program():
    nc = bass.Bass("TRN2", target_bir_lowering=False)
    consts = host_consts()
    cap = declare_consts(nc, consts)
    xin = nc.dram_tensor("x_in", [T, D], F32, kind="ExternalInput").ap()
    xout = nc.dram_tensor("x_out", [T, D], F32, kind="ExternalOutput").ap()
    f1 = FfnL1()
    f1.declare(nc)

    ar = Arena(nc)
    B = Bufs(ar)
    psA = nc.alloc_psum_tensor("psA", [128, 2, 512], F32).ap()
    psB = nc.alloc_psum_tensor("psB", [128, 2, 512], F32).ap()
    psC = nc.alloc_psum_tensor("psC", [128, 2, 512], F32).ap()
    psT = nc.alloc_psum_tensor("psT", [128, 2, 1024], BF16).ap()
    ps = (psA, psB, psC, psT)

    r = Rec(nc)
    for s in ("ldx", "stx", "wdma", "pe", "ve", "se", "po", "bar"):
        r.sem(s)
    fresh_state(r)
    emit_load_consts(r, B, cap)
    f1.prep(r, B, ar, ps)
    f1.emit(r, B, ar, ps, xin, xout)
    r.op("sync", None, (("stx", r.val("stx")),), ())
    r.emit()
    return nc, consts


def build_ffn_l0_program():
    nc = bass.Bass("TRN2", target_bir_lowering=False)
    consts = host_consts()
    cap = declare_consts(nc, consts)
    xin = nc.dram_tensor("x_in", [T, D], F32, kind="ExternalInput").ap()
    wg = nc.dram_tensor("Wg", [8, D, 4096], F32, kind="ExternalInput").ap()
    wu = nc.dram_tensor("Wu", [8, D, 4096], F32, kind="ExternalInput").ap()
    wd = nc.dram_tensor("Wd", [8, 4096, D], F32, kind="ExternalInput").ap()
    xout = nc.dram_tensor("x_out", [T, D], F32, kind="ExternalOutput").ap()

    ar = Arena(nc)
    B = Bufs(ar)
    psA = nc.alloc_psum_tensor("psA", [128, 2, 512], F32).ap()
    psB = nc.alloc_psum_tensor("psB", [128, 2, 512], F32).ap()
    psC = nc.alloc_psum_tensor("psC", [128, 2, 512], F32).ap()
    psT = nc.alloc_psum_tensor("psT", [128, 2, 1024], BF16).ap()

    r = Rec(nc)
    for s in ("ldx", "stx", "wdma", "pe", "ve", "se", "po", "bar"):
        r.sem(s)
    fresh_state(r)
    emit_load_consts(r, B, cap)
    emit_ffn_l0(r, B, ar, {"Wg": wg, "Wu": wu, "Wd": wd}, xin, xout,
                psA, psB, psC, psT)
    r.op("sync", None, (("stx", r.val("stx")),), ())
    r.emit()
    return nc, consts


_PROG_CACHE = {}


def _get_programs():
    if "progs" in _PROG_CACHE:
        return _PROG_CACHE["progs"]
    progs = {}
    progs["l0a"] = build_attn_program(CfgL0AttnA(), "l0a")
    progs["ffn0"] = build_ffn_l0_program()
    progs["l0b"] = build_attn_program(CfgL0AttnB(), "l0b")
    progs["l1a"] = build_attn_program(CfgL1AttnA(), "l1a")
    progs["ffn1"] = build_ffn_l1_program()
    progs["l1b"] = build_attn_program(CfgL1AttnB(), "l1b")
    _PROG_CACHE["progs"] = progs
    return progs


def kernel(**inputs):
    progs = _get_programs()
    consts = host_consts()
    cmap = {"c_" + k: np.asarray(v) for k, v in consts.items()}

    wmaps = {
        "l0a": CfgL0AttnA.host_inputs(inputs),
        "ffn0": {"Wg": inputs["L0_ffn_Wg"], "Wu": inputs["L0_ffn_Wu"],
                 "Wd": inputs["L0_ffn_Wd"]},
        "l0b": CfgL0AttnB.host_inputs(inputs),
        "l1a": CfgL1AttnA.host_inputs(inputs),
        "ffn1": FfnL1.host_inputs(inputs),
        "l1b": CfgL1AttnB.host_inputs(inputs),
    }

    x = np.ascontiguousarray(np.asarray(inputs["L0_attn_a_x"], np.float32))
    shards = [np.ascontiguousarray(x[c]) for c in range(8)]

    seq = ["l0a", "ffn0", "l0b", "ffn0", "l1a", "ffn1", "l1b", "ffn1"]
    for tag in seq:
        nc, _ = progs[tag]
        in_maps = []
        for c in range(8):
            im = dict(cmap)
            im.update({k: np.asarray(v) for k, v in wmaps[tag].items()})
            im["x_in"] = shards[c]
            in_maps.append(im)
        res = run_bass_kernel_spmd(nc, in_maps, core_ids=list(range(8)))
        shards = [res.results[c]["x_out"] for c in range(8)]

    return np.stack(shards, axis=0)


# version 03
# speedup vs baseline: 1.0293x; 1.0293x over previous
import math
import numpy as np
import ml_dtypes
import concourse.bass as bass
import concourse.mybir as mybir
from concourse.bass_utils import run_bass_kernel_spmd

F32 = mybir.dt.float32
BF16 = mybir.dt.bfloat16
AF = mybir.ActivationFunctionType
ALU = mybir.AluOpType
AX = mybir.AxisListType

T = 4096
D = 1024
NH = 16
DH = 64
NT = T // 128
NEG = -1e30

SBUF_BASE = 16640
SBUF_END = 229248

ENGINES = ("sync", "tensor", "vector", "scalar", "gpsimd")


def host_consts():
    c = {}
    bf = ml_dtypes.bfloat16
    c["ident"] = np.eye(128, dtype=np.float32).astype(bf)

    m = np.zeros((128, 4, 512), dtype=np.float32)
    for v in range(4):
        q = np.arange(128)[:, None]
        k = np.arange(512)[None, :]
        m[:, v, :] = np.where(k <= v * 128 + q, 0.0, NEG)
    c["cmask"] = m.astype(bf)

    dhe = 32
    inv = 1.0 / 10000.0 ** (np.arange(0, dhe, 2, dtype=np.float64) / dhe)
    tpos = np.arange(T, dtype=np.float64)
    fr = tpos[:, None] * inv[None, :]
    emb = np.concatenate([fr, fr], axis=1)
    cosA32 = np.cos(emb).T
    sinA32 = np.sin(emb).T
    c["cosA"] = np.tile(cosA32, (4, 1)).astype(bf)
    c["sinA"] = np.tile(sinA32, (4, 1)).astype(bf)
    R32 = np.zeros((32, 32), dtype=np.float32)
    for i in range(16):
        R32[2 * i, 2 * i + 1] = -1.0
        R32[2 * i + 1, 2 * i] = 1.0
    c["R32T"] = np.tile(R32.T, (4, 1)).astype(bf)
    Rhat = np.zeros((128, 128), dtype=np.float32)
    Rhat[32:64, 32:64] = R32
    Rhat[96:128, 96:128] = R32
    c["RhatAT"] = Rhat.T.copy().astype(bf)

    half = 32
    invf = 1.0 / 10000.0 ** (np.arange(half, dtype=np.float64) / half)
    for name, poss in (("N1", np.arange(1, T + 1, dtype=np.float64)),
                       ("N2", np.arange(T, dtype=np.float64))):
        ang = poss[:, None] * invf[None, :]
        cc = np.concatenate([np.cos(ang), np.cos(ang)], axis=1).T
        ss = np.concatenate([np.sin(ang), np.sin(ang)], axis=1).T
        c["cos" + name] = np.tile(cc, (2, 1)).astype(bf)
        c["sin" + name] = np.tile(ss, (2, 1)).astype(bf)
    R64 = np.zeros((64, 64), dtype=np.float32)
    R64[:32, 32:] = -np.eye(32)
    R64[32:, :32] = np.eye(32)
    R128 = np.zeros((128, 128), dtype=np.float32)
    R128[:64, :64] = R64
    R128[64:, 64:] = R64
    c["R128T"] = R128.T.copy().astype(bf)

    hd = 64
    invb = 1.0 / 10000.0 ** (np.arange(0, hd, 2, dtype=np.float64) / hd)
    Mall = np.zeros((D, D), dtype=np.float64)
    for h in range(NH):
        Mh = np.eye(hd)
        for f in range(8):
            a = h * invb[f]
            cv, sv = np.cos(a), np.sin(a)
            Mh[2 * f, 2 * f] = cv
            Mh[2 * f, 2 * f + 1] = -sv
            Mh[2 * f + 1, 2 * f] = sv
            Mh[2 * f + 1, 2 * f + 1] = cv
        Mall[h * hd:(h + 1) * hd, h * hd:(h + 1) * hd] = Mh
    MT = np.zeros((128, 8, 128), dtype=np.float32)
    for cc in range(8):
        blk = Mall[cc * 128:(cc + 1) * 128, cc * 128:(cc + 1) * 128]
        MT[:, cc, :] = blk.T
    c["MallT"] = MT.astype(bf)
    return c


def reshape_col_chunks(v):
    n = v.shape[0]
    return np.ascontiguousarray(v.reshape(n // 128, 128).T)


class Rec:
    def __init__(self, nc):
        self.nc = nc
        self.ops = []
        self.cnt = {}
        self.sems = {}

    def sem(self, name):
        ctx = self.nc.semaphore(name)
        h = ctx.__enter__()
        self.sems[name] = h
        self.cnt[name] = 0
        self.last_wait = getattr(self, "last_wait", {})
        self.last_wait[name] = 0
        return h

    def val(self, name):
        return self.cnt[name]

    ENG_SEM = {"vector": "ve", "scalar": "se", "gpsimd": "po"}

    def op(self, eng, fn, waits=(), incs=(), selfsync=True):
        waits = list(waits)
        if selfsync and fn is not None and eng in self.ENG_SEM:
            s = self.ENG_SEM[eng]
            if self.cnt[s] > 0:
                waits.append((s, self.cnt[s]))
        for (s, v) in waits:
            assert v <= self.cnt[s], f"wait on future: {s} {v} > {self.cnt[s]}"
            if v > self.last_wait[s]:
                self.last_wait[s] = v
        for (s, k) in incs:
            if k == 16 and self.last_wait[s] > 0:
                have = dict(waits).get(s, -1)
                if have < self.last_wait[s]:
                    waits.append((s, self.last_wait[s]))
        for (s, k) in incs:
            self.cnt[s] += k
        self.ops.append((eng, fn, tuple(waits), tuple(incs)))

    def dma(self, queue, out, in_, sem, waits=()):
        waits = list(waits)
        if sem == "stx" and self.cnt[sem] > 0:
            waits.append((sem, self.cnt[sem]))
        def fn(e, out=out, in_=in_):
            return e.dma_start(out=out, in_=in_)
        self.op(queue, fn, waits, ((sem, 16),), selfsync=False)

    def barrier(self):
        if "bar" not in self.cnt:
            self.sem("bar")
        base = self.cnt["bar"]
        for eng in ENGINES:
            self.op(eng, lambda e: e.nop(), (), (("bar", 1),), selfsync=False)
        tgt = base + 5
        for eng in ENGINES:
            self.op(eng, None, (("bar", tgt),), ())

    def emit(self):
        nc = self.nc
        with nc.Block() as block:
            for eng_name in ENGINES:
                def make(eng_name=eng_name):
                    def f(eng):
                        guaranteed = {}
                        for (en, fn, waits, incs) in self.ops:
                            if en != eng_name:
                                continue
                            for (s, v) in waits:
                                if guaranteed.get(s, -1) >= v:
                                    continue
                                eng.wait_ge(self.sems[s], v)
                                guaranteed[s] = v
                            if fn is None:
                                if incs:
                                    ins = eng.nop()
                                    for (s, k) in incs:
                                        ins.then_inc(self.sems[s], k)
                                continue
                            ins = fn(eng)
                            for (s, k) in incs:
                                ins.then_inc(self.sems[s], k)
                    return f
                getattr(block, eng_name)(make())


class Arena:
    def __init__(self, nc, base=SBUF_BASE, end=SBUF_END):
        self.nc = nc
        self.base = base
        self.end = end
        self.off = base
        self.marks = []
        self.n = 0

    def alloc(self, name, shape, dtype, align=64):
        self.off = (self.off + align - 1) // align * align
        nbytes = int(np.prod(shape[1:])) * mybir.dt.size(dtype)
        assert shape[0] <= 128
        t = self.nc.alloc_sbuf_tensor_at(
            f"{name}_{self.n}", list(shape), dtype, offset=self.off)
        self.n += 1
        self.off += nbytes
        assert self.off <= self.end, f"SBUF overflow at {name}: {self.off}"
        return t.ap()

    def mark(self):
        self.marks.append(self.off)

    def release(self):
        self.off = self.marks.pop()


class Bufs:
    def __init__(self, ar):
        self.cpool = ar.alloc("cpool", [128, 4], F32)
        self.ident = ar.alloc("ident", [128, 128], BF16)
        self.cmask = ar.alloc("cmask", [128, 4, 512], BF16)
        self.trash = ar.alloc("trash", [128, 1024], F32)
        self.mu_raw = ar.alloc("mu_raw", [128, 16], F32)
        self.s2 = ar.alloc("s2", [128, 16], F32)
        self.mu = ar.alloc("mu", [128, 16], F32)
        self.mu2 = ar.alloc("mu2", [128, 16], F32)
        self.var = ar.alloc("var", [128, 16], F32)
        self.sd = ar.alloc("sd", [128, 16], F32)
        self.rstd = ar.alloc("rstd", [128, 16], F32)
        self.nmr = ar.alloc("nmr", [128, 16], F32)
        self.mst = ar.alloc("mst", [128, 8], F32)
        self.sst = ar.alloc("sst", [128, 8], F32)
        self.bst = ar.alloc("bst", [128, 8], F32)
        self.corr = ar.alloc("corr", [128, 8], F32)
        self.negMx = ar.alloc("negMx", [128, 1], F32)
        self.ssum = ar.alloc("ssum", [128, 1], F32)
        self.rs = ar.alloc("rs", [128, 1], F32)
        self.pscale = ar.alloc("pscale", [128, 8], F32)
        self.tmp8 = ar.alloc("tmp8", [128, 8], F32)
        self.ln_eps = 1e-5


ACT_CONSTS = (0.125,)


def emit_load_consts(r, B, cap):
    r.dma("sync", B.ident[:, :], cap["ident"][:, :], "ldx")
    r.dma("sync", B.cmask[:, :, :], cap["cmask"][:, :, :], "ldx")
    for i, v in enumerate(ACT_CONSTS):
        r.op("gpsimd",
             lambda e, i=i, v=v: e.memset(B.cpool[:, i:i + 1], v),
             (), (("po", 1),))
        r.nc.const_aps.register(F32, v, B.cpool[:, i:i + 1])
    r.op("sync", None, (("ldx", r.val("ldx")),), ())
    r.barrier()


def emit_ln(r, B, xtiles, jcols, ln_wait):
    nt = len(xtiles)
    ln_wait = list(ln_wait) + [("po", r.val("po"))]
    for (xt, _), j in zip(xtiles, jcols):
        r.op("vector",
             lambda e, xt=xt, j=j: e.reduce_sum(B.mu_raw[:, j:j + 1], xt, axis=AX.X),
             ln_wait, (("ve", 1),))
        r.op("scalar",
             lambda e, xt=xt, j=j: e.activation(
                 B.trash[:, :], xt, AF.Square, accum_out=B.s2[:, j:j + 1]),
             ln_wait, (("se", 1),))
    ve_after = r.val("ve")
    se_after = r.val("se")
    j0, j1 = jcols[0], jcols[0] + nt
    r.op("gpsimd",
         lambda e: e.tensor_scalar_mul(B.mu[:, j0:j1], B.mu_raw[:, j0:j1], 1.0 / D),
         (("ve", ve_after),), (("po", 1),))
    r.op("gpsimd",
         lambda e: e.tensor_mul(B.mu2[:, j0:j1], B.mu[:, j0:j1], B.mu[:, j0:j1]),
         (), (("po", 1),))
    r.op("vector",
         lambda e: e.scalar_tensor_tensor(
             B.var[:, j0:j1], B.s2[:, j0:j1], 1.0 / D, B.mu2[:, j0:j1],
             ALU.mult, ALU.subtract),
         (("se", se_after), ("po", r.val("po"))), (("ve", 1),))
    r.op("gpsimd",
         lambda e: e.tensor_scalar_add(B.var[:, j0:j1], B.var[:, j0:j1],
                                       B.ln_eps),
         (("ve", r.val("ve")),), (("po", 1),))
    r.op("scalar",
         lambda e: e.activation(B.sd[:, j0:j1], B.var[:, j0:j1], AF.Sqrt),
         (("po", r.val("po")),), (("se", 1),))
    r.op("vector",
         lambda e: e.reciprocal(B.rstd[:, j0:j1], B.sd[:, j0:j1]),
         (("se", r.val("se")),), (("ve", 1),))
    r.op("vector",
         lambda e: e.scalar_tensor_tensor(
             B.nmr[:, j0:j1], B.mu[:, j0:j1], -1.0, B.rstd[:, j0:j1],
             ALU.mult, ALU.mult),
         (), (("ve", 1),))
    stats_wait = (("po", r.val("po")), ("ve", r.val("ve")))
    for (xt, yt), j in zip(xtiles, jcols):
        r.op("scalar",
             lambda e, xt=xt, yt=yt, j=j: e.activation(
                 yt, xt, AF.Identity, scale=B.rstd[:, j:j + 1],
                 bias=B.nmr[:, j:j + 1]),
             stats_wait, (("se", 1),))
    return r.val("se")


def emit_transpose(r, B, psT, src, dst, pidx, src_wait, evac_eng="vector"):
    key = ("psT", pidx)
    prev = r._psT_free.get(key, ())
    r.op("tensor",
         lambda e, src=src, pidx=pidx: e.transpose(psT[:, pidx, 0:128], src, B.ident[:, :]),
         tuple(src_wait) + tuple(prev), (("pe", 1),))
    sem = {"vector": "ve", "scalar": "se"}[evac_eng]
    r.op(evac_eng,
         lambda e, dst=dst, pidx=pidx: e.tensor_copy(dst, psT[:, pidx, 0:128])
         if evac_eng == "vector" else e.activation(dst, psT[:, pidx, 0:128], AF.Copy),
         (("pe", r.val("pe")),), ((sem, 1),))
    r._psT_free[key] = ((sem, r.val(sem)),)


def emit_ffn_l0(r, B, ar, wap, xin, xout, psA, psB, psC, psT):
    TH = 2048
    ar.mark()
    x_half = ar.alloc("ffn_x", [128, 16, 1024], F32)
    yT = ar.alloc("ffn_yT", [128, 8, TH], BF16)
    ystage = ar.alloc("ffn_y", [128, 2, 1024], BF16)
    wg = ar.alloc("ffn_wg", [128, 8, 1024], BF16)
    wu = ar.alloc("ffn_wu", [128, 8, 1024], BF16)
    wd = ar.alloc("ffn_wd", [128, 8, 1024], BF16)
    hbuf = ar.alloc("ffn_h", [128, 8, 512], BF16)
    silu_st = ar.alloc("ffn_silu", [128, 2, 512], BF16)

    Wg = wap["Wg"]
    Wu = wap["Wu"]
    Wd = wap["Wd"]

    for half in range(2):
        t0 = half * TH
        for j in range(16):
            r.dma("sync", x_half[:, j], xin[t0 + j * 128: t0 + (j + 1) * 128, :],
                  "ldx")
        ldx_all = r.val("ldx")
        for j in range(16):
            yt_done = emit_ln(r, B, [(x_half[:, j], ystage[:, j % 2])], [j],
                              [("ldx", ldx_all)] + r._ystage_free.get(j % 2, []))
            for c in range(8):
                emit_transpose(
                    r, B, psT,
                    ystage[:, j % 2, c * 128:(c + 1) * 128],
                    yT[:, c, j * 128:(j + 1) * 128],
                    (j * 8 + c) % 2,
                    [("se", yt_done)],
                    evac_eng="vector" if c % 2 == 0 else "scalar")
            r._ystage_free[j % 2] = [("pe", r.val("pe"))]
        yT_ready = (("ve", r.val("ve")), ("se", r.val("se")))

        for e8 in range(8):
            for g in range(4):
                gc = g * 1024
                wg_wait = r._wgwu_free
                for kc in range(8):
                    r.dma("gpsimd", wg[:, kc], Wg[e8, kc * 128:(kc + 1) * 128,
                                                  gc:gc + 1024], "wdma",
                          waits=wg_wait)
                    r.dma("gpsimd", wu[:, kc], Wu[e8, kc * 128:(kc + 1) * 128,
                                                  gc:gc + 1024], "wdma",
                          waits=wg_wait)
                wd_wait = r._wd_free
                for jj in range(8):
                    r.dma("gpsimd", wd[:, jj],
                          Wd[e8, gc + jj * 128:gc + (jj + 1) * 128, :], "wdma",
                          waits=wd_wait)
                w_ld = r.val("wdma")

                for tb in range(4):
                    for jj in range(8):
                        b = (tb * 8 + jj) % 2
                        g_wait = [("wdma", w_ld)] + r._psA_free.get(b, []) + \
                                 list(yT_ready)
                        for kc in range(8):
                            r.op("tensor",
                                 lambda e, b=b, kc=kc, jj=jj, tb=tb: e.matmul(
                                     psA[:, b], wg[:, kc, jj * 128:(jj + 1) * 128],
                                     yT[:, kc, tb * 512:(tb + 1) * 512],
                                     start=(kc == 0), stop=(kc == 7)),
                                 g_wait if kc == 0 else (),
                                 (("pe", 1),) if kc == 7 else ())
                        u_wait = r._psB_free.get(b, [])
                        for kc in range(8):
                            r.op("tensor",
                                 lambda e, b=b, kc=kc, jj=jj, tb=tb: e.matmul(
                                     psB[:, b], wu[:, kc, jj * 128:(jj + 1) * 128],
                                     yT[:, kc, tb * 512:(tb + 1) * 512],
                                     start=(kc == 0), stop=(kc == 7)),
                                 u_wait if kc == 0 else (),
                                 (("pe", 1),) if kc == 7 else ())
                        pe_now = r.val("pe")
                        r.op("scalar",
                             lambda e, b=b: e.activation(
                                 silu_st[:, b], psA[:, b], AF.Silu),
                             [("pe", pe_now - 1)] + r._silu_free.get(b, []),
                             (("se", 1),))
                        r._psA_free[b] = [("se", r.val("se"))]
                        r.op("vector",
                             lambda e, b=b, jj=jj: e.tensor_mul(
                                 hbuf[:, jj], psB[:, b], silu_st[:, b]),
                             [("pe", pe_now), ("se", r.val("se"))] +
                             r._h_free.get(jj, []),
                             (("ve", 1),))
                        r._psB_free[b] = [("ve", r.val("ve"))]
                        r._silu_free[b] = [("ve", r.val("ve"))]
                    h_done = r.val("ve")
                    for tsub in range(4):
                        tt = tb * 4 + tsub
                        for oh in range(2):
                            b = (tt * 2 + oh) % 2
                            d_wait = [("ve", h_done), ("wdma", w_ld)] + \
                                     r._psC_free.get(b, [])
                            for jj in range(8):
                                r.op("tensor",
                                     lambda e, b=b, jj=jj, tsub=tsub, oh=oh:
                                     e.matmul(
                                         psC[:, b],
                                         hbuf[:, jj, tsub * 128:(tsub + 1) * 128],
                                         wd[:, jj, oh * 512:(oh + 1) * 512],
                                         start=(jj == 0), stop=(jj == 7)),
                                     d_wait if jj == 0 else (),
                                     (("pe", 1),) if jj == 7 else ())
                            r.op("vector",
                                 lambda e, b=b, tt=tt, oh=oh: e.tensor_add(
                                     x_half[:, tt, oh * 512:(oh + 1) * 512],
                                     psC[:, b],
                                     x_half[:, tt, oh * 512:(oh + 1) * 512]),
                                 [("pe", r.val("pe"))], (("ve", 1),))
                            r._psC_free[b] = [("ve", r.val("ve"))]
                    for jj in range(8):
                        r._h_free[jj] = [("pe", r.val("pe"))]
                r._wgwu_free = [("pe", r.val("pe"))]
                r._wd_free = [("pe", r.val("pe"))]

        ve_done = r.val("ve")
        for j in range(16):
            r.dma("sync", xout[t0 + j * 128: t0 + (j + 1) * 128, :], x_half[:, j],
                  "stx", waits=[("ve", ve_done)])
        r.op("sync", None, (("stx", r.val("stx")),), ())
        r.barrier()
    ar.release()


class AttnBufs:
    def __init__(self, r, ar, nc, tag):
        self.kt0 = ar.alloc(tag + "kt0", [128, 4, T], BF16)
        self.v0 = ar.alloc(tag + "v0", [128, NT, 512], BF16)
        self.qt_dram = nc.dram_tensor(tag + "qt", [D, T], BF16, kind="Internal").ap()
        self.kt_dram = nc.dram_tensor(tag + "kt", [512, T], BF16, kind="Internal").ap()
        self.v_dram = nc.dram_tensor(tag + "v", [T, 512], BF16, kind="Internal").ap()


def emit_phaseA(r, B, ar, A, cfg, xin, ps):
    psA, psB, psC, psT = ps
    ar.mark()
    A.xs = ar.alloc("pa_xs", [128, 4, 1024], F32)
    A.ystage = ar.alloc("pa_y", [128, 2, 1024], BF16)
    A.yT = ar.alloc("pa_yT", [128, 8, 512], BF16)
    A.qstage = ar.alloc("pa_qs", [128, 8, 512], BF16)
    A.kstage = ar.alloc("pa_ks", [128, 4, 512], BF16)
    A.vt = ar.alloc("pa_vt", [128, 8, 512], BF16)
    A.vstage = ar.alloc("pa_vs", [128, 4, 128], BF16)
    cfg.alloc_phaseA(r, B, ar, A)

    for ib in range(8):
        for j in range(4):
            r.dma("sync", A.xs[:, j], xin[ib * 512 + j * 128: ib * 512 + (j + 1) * 128, :],
                  "ldx", waits=r._xs_free)
        ld = r.val("ldx")
        for j in range(4):
            yd = emit_ln(r, B, [(A.xs[:, j], A.ystage[:, j % 2])], [j],
                         [("ldx", ld)] + r._ystage_free.get(j % 2, []))
            for c in range(8):
                emit_transpose(r, B, psT, A.ystage[:, j % 2, c * 128:(c + 1) * 128],
                               A.yT[:, c, j * 128:(j + 1) * 128], (j * 8 + c) % 2,
                               [("se", yd)],
                               evac_eng="vector" if c % 2 == 0 else "scalar")
            r._ystage_free[j % 2] = [("pe", r.val("pe"))]
        r._xs_free = [("pe", r.val("pe"))]
        yT_done = (("ve", r.val("ve")), ("se", r.val("se")))

        cfg.qkv(r, B, A, ib, yT_done)

        for rc in range(8):
            r.dma("sync", A.qt_dram[rc * 128:(rc + 1) * 128, ib * 512:(ib + 1) * 512],
                  A.qstage[:, rc], "stx", waits=A.q_done[rc])
        for rc in range(4):
            r.dma("sync", A.kt_dram[rc * 128:(rc + 1) * 128, ib * 512:(ib + 1) * 512],
                  A.kstage[:, rc], "stx", waits=A.k_done[rc + 4])
        for tt in range(4):
            trow = ib * 4 + tt
            for rc in range(8):
                src = A.vt[:, rc, tt * 128:(tt + 1) * 128]
                if rc < 4:
                    dst = A.v0[:, trow, rc * 128:(rc + 1) * 128]
                else:
                    dst = A.vstage[:, rc - 4]
                emit_transpose(r, B, psT, src, dst, (tt * 8 + rc) % 2,
                               A.v_done[rc] + r._vstage_free,
                               evac_eng="vector" if rc % 2 == 0 else "scalar")
            vs_done = [("ve", r.val("ve")), ("se", r.val("se"))]
            r.dma("sync", A.v_dram[trow * 128:(trow + 1) * 128, :],
                  A.vstage[:, 0:4].rearrange("p a b -> p (a b)"), "stx",
                  waits=vs_done)
            r._vstage_free = [("stx", r.val("stx"))]
        r._vt_free = [("pe", r.val("pe"))]
        r._qstage_free = [("stx", r.val("stx"))]
    ar.release()
    r.barrier()


def emit_phaseB(r, B, ar, A, cfg, xin, xout, ps):
    psA, psB, psC, psT = ps
    ar.mark()
    wo = cfg.alloc_phaseB(r, B, ar, A)
    P = ar.alloc("pb_P", [128, T], BF16)
    PT = ar.alloc("pb_PT", [128, NT, 128], BF16)
    Sd = ar.alloc("pb_Sd", [128, 512], F32)
    OT = ar.alloc("pb_OT", [128, 4, 128], BF16)
    qts = ar.alloc("pb_qts", [128, 2, 4, 128], BF16)
    xsr = ar.alloc("pb_xs", [128, 2, 1024], F32)
    xnw = ar.alloc("pb_xn", [128, 2, 1024], F32)

    qt_r = A.qt_dram.rearrange("(n p) t -> p n t", p=128)
    kt_r = A.kt_dram.rearrange("(n p) t -> p n t", p=128)
    v_r = A.v_dram.rearrange("(n p) c -> p n c", p=128)

    for hg in range(2):
        if hg == 1:
            r.barrier()
            r.dma("sync", A.kt0[:, :, :], kt_r[:, :, :], "ldx")
            r.dma("sync", A.v0[:, :, :], v_r[:, :, :], "ldx")
        kv_ld = r.val("ldx")
        xsrc = xin if hg == 0 else xout

        r.op("sync", None, (("ldx", kv_ld),), ())
        r.dma("sync", qts[:, 0], qt_r[:, hg * 4:hg * 4 + 4, 0:128], "ldx",
              waits=r._qts_free.get(0, []))
        r.dma("sync", xsr[:, 0], xsrc[0:128, :], "ldx",
              waits=[("stx", r.val("stx"))] + r._xsr_free.get(0, []))
        pf = {0: r.val("ldx")}

        for i in range(NT):
            qb = i % 2
            if i + 1 < NT:
                r.op("sync", None, (("ldx", pf[i]),), ())
                r.dma("sync", qts[:, (i + 1) % 2],
                      qt_r[:, hg * 4:hg * 4 + 4, (i + 1) * 128:(i + 2) * 128],
                      "ldx", waits=r._qts_free.get((i + 1) % 2, []))
                r.dma("sync", xsr[:, (i + 1) % 2],
                      xsrc[(i + 1) * 128:(i + 2) * 128, :], "ldx",
                      waits=r._xsr_free.get((i + 1) % 2, []))
                pf[i + 1] = r.val("ldx")
            nkc = (i + 1) if cfg.causal else NT
            nc5 = (nkc * 128 + 511) // 512
            for h8 in range(8):
                hc = h8 // 2
                hb = (h8 % 2) * 64
                for c in range(nc5):
                    b = c % 2
                    diag = cfg.causal and (c == nc5 - 1)
                    w = [("ldx", max(pf[i], kv_ld))] + r._psSc_free.get(b, [])
                    r.op("tensor",
                         lambda e, b=b, c=c, hc=hc, hb=hb, qb=qb: e.matmul(
                             psA[:, b],
                             qts[hb:hb + 64, qb, hc, :],
                             A.kt0[hb:hb + 64, hc, c * 512:(c + 1) * 512],
                             start=True, stop=True, tile_position=(hb, 0)),
                         w, (("pe", 1),))
                    pe_s = r.val("pe")
                    if diag:
                        r.op("vector",
                             lambda e, b=b, i=i: e.tensor_add(
                                 Sd[:, :], psA[:, b], B.cmask[:, i % 4]),
                             [("pe", pe_s)] + r._Sd_free, (("ve", 1),))
                        ve_tt = r.val("ve")
                        r._psSc_free[b] = [("ve", ve_tt)]
                        src = Sd[:, :]
                        srcw = [("ve", ve_tt), ("se", r._mst_guard)]
                    else:
                        src = psA[:, b]
                        srcw = [("pe", pe_s), ("se", r._mst_guard)]
                    r.op("vector",
                         lambda e, c=c, src=src: e.reduce_max(
                             B.mst[:, c:c + 1], src, axis=AX.X),
                         srcw, (("ve", 1),))
                    r.op("gpsimd",
                         lambda e, c=c: e.tensor_scalar_mul(
                             B.bst[:, c:c + 1], B.mst[:, c:c + 1], -0.125),
                         [("ve", r.val("ve"))], (("po", 1),))
                    r.op("scalar",
                         lambda e, c=c, src=src: e.activation(
                             P[:, c * 512:(c + 1) * 512], src, AF.Exp,
                             scale=0.125, bias=B.bst[:, c:c + 1],
                             accum_out=B.sst[:, c:c + 1]),
                         [("po", r.val("po"))], (("se", 1),))
                    if diag:
                        r._Sd_free = [("se", r.val("se"))]
                    else:
                        r._psSc_free[b] = [("se", r.val("se"))]
                se_exp = r.val("se")
                r.op("vector",
                     lambda e, n=nc5: e.reduce_max(
                         B.negMx[:, :], B.mst[:, 0:n], axis=AX.X, negate=True),
                     [("se", se_exp)], (("ve", 1),))
                r.op("gpsimd",
                     lambda e: e.tensor_scalar_mul(
                         B.negMx[:, :], B.negMx[:, :], 0.125),
                     [("ve", r.val("ve"))], (("po", 1),))
                r.op("scalar",
                     lambda e, n=nc5: e.activation(
                         B.corr[:, 0:n], B.mst[:, 0:n], AF.Exp,
                         scale=0.125, bias=B.negMx[:, :]),
                     [("po", r.val("po"))], (("se", 1),))
                r._mst_guard = r.val("se")
                r.op("vector",
                     lambda e, n=nc5: e.tensor_mul(
                         B.tmp8[:, 0:n], B.corr[:, 0:n], B.sst[:, 0:n]),
                     [("se", r.val("se"))], (("ve", 1),))
                r.op("vector",
                     lambda e, n=nc5: e.reduce_sum(
                         B.ssum[:, :], B.tmp8[:, 0:n], axis=AX.X),
                     (), (("ve", 1),))
                r.op("vector",
                     lambda e: e.reciprocal(B.rs[:, :], B.ssum[:, :]),
                     (), (("ve", 1),))
                r.op("gpsimd",
                     lambda e, n=nc5: e.tensor_scalar_mul(
                         B.pscale[:, 0:n], B.corr[:, 0:n], B.rs[:, :]),
                     [("ve", r.val("ve"))], (("po", 1),))
                po_ps = r.val("po")
                for c in range(nc5):
                    r.op("vector",
                         lambda e, c=c: e.tensor_scalar_mul(
                             P[:, c * 512:(c + 1) * 512],
                             P[:, c * 512:(c + 1) * 512], B.pscale[:, c:c + 1]),
                         [("po", po_ps)], (("ve", 1),))
                p_done = r.val("ve")
                for k in range(nkc):
                    tb = k % 2
                    prev = r._psT_free.get(("psT", tb), ())
                    r.op("tensor",
                         lambda e, k=k, tb=tb: e.transpose(
                             psT[:, tb, 0:128], P[:, k * 128:(k + 1) * 128],
                             B.ident[:, :]),
                         [("ve", p_done)] + list(prev) + r._PT_free.get(k, []),
                         (("pe", 1),))
                    eng = "vector" if k % 2 == 0 else "scalar"
                    sem = "ve" if k % 2 == 0 else "se"
                    r.op(eng,
                         lambda e, k=k, tb=tb, eng=eng: (
                             e.tensor_copy(PT[:, k], psT[:, tb, 0:128])
                             if eng == "vector" else
                             e.activation(PT[:, k], psT[:, tb, 0:128], AF.Copy)),
                         [("pe", r.val("pe"))], ((sem, 1),))
                    r._psT_free[("psT", tb)] = ((sem, r.val(sem)),)
                pt_done = (("ve", r.val("ve")), ("se", r.val("se")))
                for k in range(nkc):
                    r.op("tensor",
                         lambda e, k=k, h8=h8, st=(k == 0), sp=(k == nkc - 1):
                         e.matmul(
                             psC[0:64, 0, 0:128],
                             A.v0[:, k, h8 * 64:(h8 + 1) * 64],
                             PT[:, k],
                             start=st, stop=sp),
                         (list(pt_done) + [("ldx", kv_ld)] + r._psO_free)
                         if k == 0 else (),
                         (("pe", 1),) if k == nkc - 1 else ())
                for k in range(nkc):
                    r._PT_free[k] = [("pe", r.val("pe"))]
                r.op("vector",
                     lambda e, hb=hb, hc=hc: e.tensor_copy(
                         OT[hb:hb + 64, hc, :], psC[0:64, 0, 0:128]),
                     [("pe", r.val("pe"))], (("ve", 1),))
                r._psO_free = [("ve", r.val("ve"))]
            ot_done = r.val("ve")
            for oh in range(2):
                w0 = [("ve", ot_done)] + list(cfg.wo_wait) + \
                     r._psOP_free.get(oh, [])
                for c in range(4):
                    r.op("tensor",
                         lambda e, c=c, oh=oh, hg=hg: e.matmul(
                             psB[:, oh], OT[:, c, :],
                             wo[:, hg * 4 + c, oh * 512:(oh + 1) * 512],
                             start=(c == 0), stop=(c == 3)),
                         w0 if c == 0 else (),
                         (("pe", 1),) if c == 3 else ())
                pe_op = r.val("pe")
                xw = [("pe", pe_op), ("ldx", pf[i])] + r._xnw_free.get(qb, [])
                if cfg.out_bias is not None and hg == 0:
                    r.op("vector",
                         lambda e, oh=oh, qb=qb: e.tensor_add(
                             xnw[:, qb, oh * 512:(oh + 1) * 512], psB[:, oh],
                             cfg.out_bias_sb[:, oh * 512:(oh + 1) * 512]),
                         xw, (("ve", 1),))
                    r.op("vector",
                         lambda e, oh=oh, qb=qb: e.tensor_add(
                             xnw[:, qb, oh * 512:(oh + 1) * 512],
                             xnw[:, qb, oh * 512:(oh + 1) * 512],
                             xsr[:, qb, oh * 512:(oh + 1) * 512]),
                         (), (("ve", 1),))
                else:
                    r.op("vector",
                         lambda e, oh=oh, qb=qb: e.tensor_add(
                             xnw[:, qb, oh * 512:(oh + 1) * 512], psB[:, oh],
                             xsr[:, qb, oh * 512:(oh + 1) * 512]),
                         xw, (("ve", 1),))
                r._psOP_free[oh] = [("ve", r.val("ve"))]
            r._xsr_free[qb] = [("ve", r.val("ve"))]
            r.dma("sync", xout[i * 128:(i + 1) * 128, :], xnw[:, qb],
                  "stx", waits=[("ve", r.val("ve"))])
            r._xnw_free[qb] = [("stx", r.val("stx"))]
            r._qts_free[qb] = [("pe", r.val("pe"))]
    ar.release()
    r.op("sync", None, (("stx", r.val("stx")),), ())
    r.barrier()


def emit_attention(r, B, ar, cfg, nc, xin, xout, ps, cap, tag):
    A = AttnBufs(r, ar, nc, tag)
    fresh_attn_state(r)
    cfg.ps = ps
    cfg.cap = cap
    cfg.prep(r, B, ar, A, ps)
    emit_phaseA(r, B, ar, A, cfg, xin, ps)
    emit_phaseB(r, B, ar, A, cfg, xin, xout, ps)


def emit_cast_transpose(r, B, psA, psT, src_dram, nout, nin, nat, dst,
                        fold_mt=None, scale_tile=None):
    for oc in range(nout // 128):
        r.dma("gpsimd", nat[:, 0:nin], src_dram[oc * 128:(oc + 1) * 128, :],
              "wdma", waits=r._nat_free)
        wld = r.val("wdma")
        nat_src = nat
        if scale_tile is not None:
            r.op("vector",
                 lambda e, oc=oc: e.tensor_scalar_mul(
                     nat[:, 0:nin], nat[:, 0:nin], scale_tile[:, oc:oc + 1]),
                 [("wdma", wld)], (("ve", 1),))
        if fold_mt is not None:
            for nb in range((nin + 511) // 512):
                w0 = min(512, nin - nb * 512)
                bb = nb % 2
                r.op("tensor",
                     lambda e, oc=oc, nb=nb, w0=w0, bb=bb: e.matmul(
                         psA[:, bb, 0:w0], fold_mt[:, oc, :],
                         nat[:, nb * 512:nb * 512 + w0],
                         start=True, stop=True),
                     [("wdma", wld)] + r._psA_free.get(bb, []), (("pe", 1),))
                r.op("scalar",
                     lambda e, oc=oc, nb=nb, w0=w0, bb=bb: e.activation(
                         nat[:, nb * 512:nb * 512 + w0], psA[:, bb, 0:w0],
                         AF.Copy),
                     [("pe", r.val("pe"))], (("se", 1),))
                r._psA_free[bb] = [("se", r.val("se"))]
            r._nat_guard = [("se", r.val("se"))]
        else:
            r._nat_guard = [("wdma", wld)] + \
                ([("ve", r.val("ve"))] if scale_tile is not None else [])
        last_read = []
        for ic in range(nin // 128):
            tb = ic % 2
            prev = r._psT_free.get(("psT", tb), ())
            r.op("tensor",
                 lambda e, ic=ic, tb=tb: e.transpose(
                     psT[:, tb, 0:128], nat[:, ic * 128:(ic + 1) * 128],
                     B.ident[:, :]),
                 list(prev) + r._nat_guard + r._dst_guard,
                 (("pe", 1),))
            eng = "vector" if ic % 2 == 0 else "scalar"
            sem = "ve" if ic % 2 == 0 else "se"
            r.op(eng,
                 lambda e, ic=ic, oc=oc, tb=tb, eng=eng: (
                     e.tensor_copy(dst[:, ic, oc * 128:(oc + 1) * 128],
                                   psT[:, tb, 0:128])
                     if eng == "vector" else
                     e.activation(dst[:, ic, oc * 128:(oc + 1) * 128],
                                  psT[:, tb, 0:128], AF.Copy)),
                 [("pe", r.val("pe"))], ((sem, 1),))
            r._psT_free[("psT", tb)] = ((sem, r.val(sem)),)
            last_read = [("pe", r.val("pe"))]
        r._nat_free = last_read
    return [("ve", r.val("ve")), ("se", r.val("se"))]


class CfgL1AttnA:
    causal = True
    out_bias = True

    def declare(self, nc):
        g = {}
        for w in ("Wq", "Wk", "Wv", "Wo"):
            g[w] = nc.dram_tensor("l1a_" + w, [D, D], F32, kind="ExternalInput").ap()
        g["bq2d"] = nc.dram_tensor("l1a_bq2d", [128, 8], F32, kind="ExternalInput").ap()
        g["bk2d"] = nc.dram_tensor("l1a_bk2d", [128, 8], F32, kind="ExternalInput").ap()
        g["bv2d"] = nc.dram_tensor("l1a_bv2d", [128, 8], F32, kind="ExternalInput").ap()
        g["bo_bc"] = nc.dram_tensor("l1a_bo_bc", [128, D], F32, kind="ExternalInput").ap()
        self.g = g

    @staticmethod
    def host_inputs(ri, pre="L1_attn_a_"):
        return {
            "l1a_Wq": ri[pre + "Wq"], "l1a_Wk": ri[pre + "Wk"],
            "l1a_Wv": ri[pre + "Wv"], "l1a_Wo": ri[pre + "Wo"],
            "l1a_bq2d": reshape_col_chunks(ri[pre + "bq"]),
            "l1a_bk2d": reshape_col_chunks(ri[pre + "bk"]),
            "l1a_bv2d": reshape_col_chunks(ri[pre + "bv"]),
            "l1a_bo_bc": np.ascontiguousarray(
                np.broadcast_to(ri[pre + "bo"], (128, D))),
        }

    def prep(self, r, B, ar, A, ps):
        psA, psB, psC, psT = ps
        self.wqT = ar.alloc("l1a_wqT", [128, 8, D], BF16)
        self.wkT = ar.alloc("l1a_wkT", [128, 8, D], BF16)
        self.wvT = ar.alloc("l1a_wvT", [128, 8, D], BF16)
        self.bq = ar.alloc("l1a_bq", [128, 8], F32)
        self.bk = ar.alloc("l1a_bk", [128, 8], F32)
        self.bv = ar.alloc("l1a_bv", [128, 8], F32)
        self.r128 = ar.alloc("l1a_r128", [128, 128], BF16)
        r.dma("sync", self.bq[:, :], self.g["bq2d"][:, :], "ldx")
        r.dma("sync", self.bk[:, :], self.g["bk2d"][:, :], "ldx")
        r.dma("sync", self.bv[:, :], self.g["bv2d"][:, :], "ldx")
        r.dma("sync", self.r128[:, :], self.cap["R128T"][:, :], "ldx")
        ar.mark()
        nat = ar.alloc("l1a_nat", [128, 1024], BF16)
        r._nat_free = []
        r._dst_guard = []
        r._psA_free = {}
        for src, dst in ((self.g["Wq"], self.wqT), (self.g["Wk"], self.wkT),
                         (self.g["Wv"], self.wvT)):
            emit_cast_transpose(r, B, psA, psT, src, D, D, nat, dst)
        ar.release()
        r.barrier()

    def alloc_phaseA(self, r, B, ar, A):
        self.tabc = ar.alloc("l1a_tabc", [128, 512], BF16)
        self.tabs = ar.alloc("l1a_tabs", [128, 512], BF16)
        self.qraw = ar.alloc("l1a_qraw", [128, 512], BF16)
        self.ropet = ar.alloc("l1a_ropet", [128, 512], BF16)
        self._tab_free = []
        self._qraw_free = []
        A.q_done = {}
        A.k_done = {}
        A.v_done = {}
        r._psA_free = {}
        r._psC_free = {}

    def _proj_rope(self, r, B, A, ib, yT_done, wT, bias, dst_sel, done_dict,
                   rope, psA, psC):
        for oc in range(8):
            b = oc % 2
            w = list(yT_done) + r._psA_free.get(b, [])
            for kc in range(8):
                r.op("tensor",
                     lambda e, b=b, kc=kc, oc=oc: e.matmul(
                         psA[:, b], wT[:, kc, oc * 128:(oc + 1) * 128],
                         A.yT[:, kc, :], start=(kc == 0), stop=(kc == 7)),
                     w if kc == 0 else (), (("pe", 1),) if kc == 7 else ())
            pe_p = r.val("pe")
            if not rope:
                dst = dst_sel(oc)
                r.op("scalar",
                     lambda e, b=b, oc=oc, dst=dst: e.activation(
                         dst, psA[:, b], AF.Identity, scale=1.0,
                         bias=bias[:, oc:oc + 1]),
                     [("pe", pe_p)] + done_dict.get(("dst_guard", oc), []),
                     (("se", 1),))
                r._psA_free[b] = [("se", r.val("se"))]
                done_dict[oc] = [("se", r.val("se"))]
            else:
                r.op("scalar",
                     lambda e, b=b, oc=oc: e.activation(
                         self.qraw[:, :], psA[:, b], AF.Identity, scale=1.0,
                         bias=bias[:, oc:oc + 1]),
                     [("pe", pe_p)] + self._qraw_free, (("se", 1),))
                qr_se = r.val("se")
                r._psA_free[b] = [("se", qr_se)]
                b2 = oc % 2
                r.op("tensor",
                     lambda e, b2=b2: e.matmul(
                         psC[:, b2], self.r128[:, :], self.qraw[:, :],
                         start=True, stop=True),
                     [("se", qr_se)] + r._psC_free.get(b2, []), (("pe", 1),))
                pe_rot = r.val("pe")
                dst = dst_sel(oc)
                r.op("vector",
                     lambda e, dst=dst: e.tensor_mul(
                         dst, self.qraw[:, :], self.tabc[:, :]),
                     [("se", qr_se)] + self._tab_free +
                     done_dict.get(("dst_guard", oc), []),
                     (("ve", 1),))
                r.op("vector",
                     lambda e, b2=b2: e.tensor_mul(
                         self.ropet[:, :], psC[:, b2], self.tabs[:, :]),
                     [("pe", pe_rot)], (("ve", 1),))
                r._psC_free[b2] = [("ve", r.val("ve"))]
                r.op("vector",
                     lambda e, dst=dst: e.tensor_add(dst, dst, self.ropet[:, :]),
                     (), (("ve", 1),))
                self._qraw_free = [("ve", r.val("ve"))]
                done_dict[oc] = [("ve", r.val("ve"))]

    def qkv(self, r, B, A, ib, yT_done):
        psA, psB, psC, psT = self.ps
        r.dma("sync", self.tabc[:, :], self.cap["cosN1"][:, ib * 512:(ib + 1) * 512],
              "ldx", waits=self._tab_free)
        r.dma("sync", self.tabs[:, :], self.cap["sinN1"][:, ib * 512:(ib + 1) * 512],
              "ldx", waits=self._tab_free)
        tl = r.val("ldx")
        ydw = list(yT_done) + [("ldx", tl)]
        for oc in range(8):
            A.q_done[("dst_guard", oc)] = r._qstage_free
            A.k_done[("dst_guard", oc)] = r._qstage_free
            A.v_done[("dst_guard", oc)] = r._vt_free
        self._proj_rope(r, B, A, ib, ydw, self.wqT, self.bq,
                        lambda oc: A.qstage[:, oc], A.q_done, True, psA, psC)
        kdst = lambda oc: (A.kt0[:, oc, ib * 512:(ib + 1) * 512] if oc < 4
                           else A.kstage[:, oc - 4])
        self._proj_rope(r, B, A, ib, ydw, self.wkT, self.bk,
                        kdst, A.k_done, True, psA, psC)
        self._proj_rope(r, B, A, ib, ydw, self.wvT, self.bv,
                        lambda e_oc: A.vt[:, e_oc], A.v_done, False, psA, psC)
        self._tab_free = [("ve", r.val("ve"))]

    def alloc_phaseB(self, r, B, ar, A):
        psA, psB, psC, psT = self.ps
        woT = ar.alloc("l1a_woT", [128, 8, D], BF16)
        self.out_bias_sb = ar.alloc("l1a_bo", [128, D], F32)
        r.dma("sync", self.out_bias_sb[:, :], self.g["bo_bc"][:, :], "ldx")
        ar.mark()
        nat = ar.alloc("l1a_natB", [128, 1024], BF16)
        r._nat_free = []
        r._dst_guard = []
        r._psA_free = {}
        self.wo_wait = emit_cast_transpose(r, B, psA, psT, self.g["Wo"],
                                           D, D, nat, woT)
        self.wo_wait = [self.wo_wait[0], self.wo_wait[1],
                        ("ldx", r.val("ldx"))]
        ar.release()
        return woT


def fresh_attn_state(r):
    r._mst_guard = 0
    r._xs_free = []
    r._vstage_free = []
    r._vt_free = []
    r._qstage_free = []
    r._psSc_free = {}
    r._Sd_free = []
    r._psT_free = {}
    r._PT_free = {}
    r._psO_free = []
    r._psOP_free = {}
    r._qts_free = {}
    r._xsr_free = {}
    r._xnw_free = {}
    r._ystage_free = {}


class CfgL0AttnB:
    causal = False
    out_bias = None

    def declare(self, nc):
        g = {}
        g["wq"] = nc.dram_tensor("l0b_wq", [D, D], F32, kind="ExternalInput").ap()
        g["wk_a"] = nc.dram_tensor("l0b_wk_a", [256, D], F32, kind="ExternalInput").ap()
        g["wk_b"] = nc.dram_tensor("l0b_wk_b", [D, 256], F32, kind="ExternalInput").ap()
        g["wv_a"] = nc.dram_tensor("l0b_wv_a", [256, D], F32, kind="ExternalInput").ap()
        g["wv_b"] = nc.dram_tensor("l0b_wv_b", [D, 256], F32, kind="ExternalInput").ap()
        g["wo"] = nc.dram_tensor("l0b_wo", [D, D], F32, kind="ExternalInput").ap()
        self.g = g

    @staticmethod
    def host_inputs(ri, pre="L0_attn_b_"):
        return {"l0b_wq": ri[pre + "wq"], "l0b_wk_a": ri[pre + "wk_a"],
                "l0b_wk_b": ri[pre + "wk_b"], "l0b_wv_a": ri[pre + "wv_a"],
                "l0b_wv_b": ri[pre + "wv_b"], "l0b_wo": ri[pre + "wo"]}

    def prep(self, r, B, ar, A, ps):
        psA, psB, psC, psT = ps
        self.wqT = ar.alloc("l0b_wqT", [128, 8, D], BF16)
        self.wkaT = ar.alloc("l0b_wkaT", [128, 8, 256], BF16)
        self.wkbT = ar.alloc("l0b_wkbT", [128, 2, D], BF16)
        self.wvaT = ar.alloc("l0b_wvaT", [128, 8, 256], BF16)
        self.wvbT = ar.alloc("l0b_wvbT", [128, 2, D], BF16)
        self.mallT = ar.alloc("l0b_mallT", [128, 8, 128], BF16)
        r.dma("sync", self.mallT[:, :, :], self.cap["MallT"][:, :, :], "ldx")
        r.op("sync", None, (("ldx", r.val("ldx")),), ())
        ar.mark()
        nat = ar.alloc("l0b_nat", [128, 1024], BF16)
        r._nat_free = []
        r._dst_guard = []
        r._psA_free = {}
        emit_cast_transpose(r, B, psA, psT, self.g["wq"], D, D, nat, self.wqT,
                            fold_mt=self.mallT)
        emit_cast_transpose(r, B, psA, psT, self.g["wk_a"], 256, D, nat, self.wkaT)
        emit_cast_transpose(r, B, psA, psT, self.g["wk_b"], D, 256, nat, self.wkbT,
                            fold_mt=self.mallT)
        emit_cast_transpose(r, B, psA, psT, self.g["wv_a"], 256, D, nat, self.wvaT)
        emit_cast_transpose(r, B, psA, psT, self.g["wv_b"], D, 256, nat, self.wvbT)
        ar.release()
        r.barrier()

    def alloc_phaseA(self, r, B, ar, A):
        self.latb = ar.alloc("l0b_lat", [128, 2, 512], BF16)
        A.q_done = {}
        A.k_done = {}
        A.v_done = {}
        r._psA_free = {}
        r._psB_free = {}
        r._psC_free = {}
        self._lat_free = []

    def _proj(self, r, A, wT, nkc_w, dst_sel, done_dict, rhs_sel, nrhs, ydw,
              psA):
        for oc in range(8):
            b = oc % 2
            w = list(ydw) + r._psA_free.get(b, [])
            for kc in range(nrhs):
                r.op("tensor",
                     lambda e, b=b, kc=kc, oc=oc: e.matmul(
                         psA[:, b], wT[:, kc, oc * 128:(oc + 1) * 128],
                         rhs_sel(kc), start=(kc == 0), stop=(kc == nkc_w - 1)),
                     w if kc == 0 else (), (("pe", 1),) if kc == nkc_w - 1 else ())
            dst = dst_sel(oc)
            r.op("scalar",
                 lambda e, b=b, dst=dst: e.activation(dst, psA[:, b], AF.Copy),
                 [("pe", r.val("pe"))] + done_dict.get(("dst_guard", oc), []),
                 (("se", 1),))
            r._psA_free[b] = [("se", r.val("se"))]
            done_dict[oc] = [("se", r.val("se"))]

    def qkv(self, r, B, A, ib, yT_done):
        psA, psB, psC, psT = self.ps
        for oc in range(8):
            A.q_done[("dst_guard", oc)] = r._qstage_free
            A.k_done[("dst_guard", oc)] = r._qstage_free
            A.v_done[("dst_guard", oc)] = r._vt_free
        ydw = list(yT_done)
        self._proj(r, A, self.wqT, 8, lambda oc: A.qstage[:, oc], A.q_done,
                   lambda kc: A.yT[:, kc, :], 8, ydw, psA)
        kdst = lambda oc: (A.kt0[:, oc, ib * 512:(ib + 1) * 512] if oc < 4
                           else A.kstage[:, oc - 4])
        for stage2, vt_mode in ((kdst, False), (lambda oc: A.vt[:, oc], True)):
            waT = self.wvaT if vt_mode else self.wkaT
            wbT = self.wvbT if vt_mode else self.wkbT
            dd = A.v_done if vt_mode else A.k_done
            for lc in range(2):
                b = lc % 2
                w = list(ydw) + r._psC_free.get(b, []) + self._lat_free
                for kc in range(8):
                    r.op("tensor",
                         lambda e, b=b, kc=kc, lc=lc, waT=waT: e.matmul(
                             psC[:, b], waT[:, kc, lc * 128:(lc + 1) * 128],
                             A.yT[:, kc, :],
                             start=(kc == 0), stop=(kc == 7)),
                         w if kc == 0 else (), (("pe", 1),) if kc == 7 else ())
                r.op("scalar",
                     lambda e, b=b, lc=lc: e.activation(
                         self.latb[:, lc], psC[:, b], AF.Copy),
                     [("pe", r.val("pe"))], (("se", 1),))
                r._psC_free[b] = [("se", r.val("se"))]
            lat_ready = [("se", r.val("se"))]
            self._proj(r, A, wbT, 2, stage2, dd,
                       lambda lc: self.latb[:, lc], 2, lat_ready, psA)
            self._lat_free = [("pe", r.val("pe"))]

    def alloc_phaseB(self, r, B, ar, A):
        psA, psB, psC, psT = self.ps
        woT = ar.alloc("l0b_woT", [128, 8, D], BF16)
        ar.mark()
        nat = ar.alloc("l0b_natB", [128, 1024], BF16)
        r._nat_free = []
        r._dst_guard = []
        r._psA_free = {}
        self.wo_wait = emit_cast_transpose(r, B, psA, psT, self.g["wo"],
                                           D, D, nat, woT)
        ar.release()
        return woT


class CfgL1AttnB:
    causal = True
    out_bias = None

    def declare(self, nc):
        g = {}
        g["W_q"] = nc.dram_tensor("l1b_Wq", [D, D], F32, kind="ExternalInput").ap()
        g["W_dkv"] = nc.dram_tensor("l1b_Wdkv", [D, 256], F32, kind="ExternalInput").ap()
        g["W_uk"] = nc.dram_tensor("l1b_Wuk", [256, D], F32, kind="ExternalInput").ap()
        g["W_uv"] = nc.dram_tensor("l1b_Wuv", [256, D], F32, kind="ExternalInput").ap()
        g["W_o"] = nc.dram_tensor("l1b_Wo", [D, D], F32, kind="ExternalInput").ap()
        g["gam2d"] = nc.dram_tensor("l1b_gam2d", [128, 2], F32, kind="ExternalInput").ap()
        g["bet2d"] = nc.dram_tensor("l1b_bet2d", [128, 2], F32, kind="ExternalInput").ap()
        self.kb_dram = nc.dram_tensor("l1b_kb", [2, D], F32, kind="Internal").ap()
        self.g = g

    @staticmethod
    def host_inputs(ri, pre="L1_attn_b_"):
        return {"l1b_Wq": ri[pre + "W_q"], "l1b_Wdkv": ri[pre + "W_dkv"],
                "l1b_Wuk": ri[pre + "W_uk"], "l1b_Wuv": ri[pre + "W_uv"],
                "l1b_Wo": ri[pre + "W_o"],
                "l1b_gam2d": reshape_col_chunks(ri[pre + "ln_gamma"]),
                "l1b_bet2d": reshape_col_chunks(ri[pre + "ln_beta"])}

    def prep(self, r, B, ar, A, ps):
        psA, psB, psC, psT = ps
        self.wq = ar.alloc("l1b_wq", [128, 8, D], BF16)
        self.wdkv = ar.alloc("l1b_wdkv", [128, 8, 256], BF16)
        self.wuk = ar.alloc("l1b_wuk", [128, 2, D], BF16)
        self.wuv = ar.alloc("l1b_wuv", [128, 2, D], BF16)
        self.gam = ar.alloc("l1b_gam", [128, 2], F32)
        self.bet = ar.alloc("l1b_bet", [128, 2], F32)
        self.betb = ar.alloc("l1b_betb", [128, 2], BF16)
        self.kb = ar.alloc("l1b_kb", [128, 8], F32)
        self.vb = ar.alloc("l1b_vb", [128, 8], F32)
        self.kbrow = ar.alloc("l1b_kbrow", [1, D], F32)
        self.r128 = ar.alloc("l1b_r128", [128, 128], BF16)
        g = self.g
        for kc in range(8):
            r.dma("gpsimd", self.wq[:, kc], g["W_q"][kc * 128:(kc + 1) * 128, :],
                  "wdma")
            r.dma("gpsimd", self.wdkv[:, kc], g["W_dkv"][kc * 128:(kc + 1) * 128, :],
                  "wdma")
        for lc in range(2):
            r.dma("gpsimd", self.wuk[:, lc], g["W_uk"][lc * 128:(lc + 1) * 128, :],
                  "wdma")
            r.dma("gpsimd", self.wuv[:, lc], g["W_uv"][lc * 128:(lc + 1) * 128, :],
                  "wdma")
        r.dma("sync", self.gam[:, :], g["gam2d"][:, :], "ldx")
        r.dma("sync", self.bet[:, :], g["bet2d"][:, :], "ldx")
        r.dma("sync", self.r128[:, :], self.cap["R128T"][:, :], "ldx")
        wl = r.val("wdma")
        ll = r.val("ldx")
        r.op("vector",
             lambda e: e.tensor_copy(self.betb[:, :], self.bet[:, :]),
             [("ldx", ll)], (("ve", 1),))
        ve_fold = r.val("ve")
        for w_i, wsb in ((0, self.wuk), (1, self.wuv)):
            for nb in range(2):
                r.op("tensor",
                     lambda e, wsb=wsb, nb=nb: e.matmul(
                         psA[0:1, nb, 0:512], self.betb[:, 0:1],
                         wsb[:, 0, nb * 512:(nb + 1) * 512],
                         start=True, stop=False),
                     [("ve", ve_fold), ("se", r.val("se")), ("wdma", wl)], ())
                r.op("tensor",
                     lambda e, wsb=wsb, nb=nb: e.matmul(
                         psA[0:1, nb, 0:512], self.betb[:, 1:2],
                         wsb[:, 1, nb * 512:(nb + 1) * 512],
                         start=False, stop=True),
                     (), (("pe", 1),))
                r.op("scalar",
                     lambda e, nb=nb: e.activation(
                         self.kbrow[0:1, nb * 512:(nb + 1) * 512],
                         psA[0:1, nb, 0:512], AF.Copy),
                     [("pe", r.val("pe")), ("stx", r.val("stx"))], (("se", 1),))
            r.dma("sync", self.kb_dram[w_i:w_i + 1, :], self.kbrow[0:1, :],
                  "stx", waits=[("se", r.val("se"))])
            r.op("sync", None, (("stx", r.val("stx")),), ())
        pe_bias = r.val("pe")
        for wsb in (self.wuk, self.wuv):
            for lc in range(2):
                r.op("vector",
                     lambda e, wsb=wsb, lc=lc: e.tensor_scalar_mul(
                         wsb[:, lc], wsb[:, lc], self.gam[:, lc:lc + 1]),
                     [("wdma", wl), ("ldx", ll), ("pe", pe_bias)], (("ve", 1),))
        for w_i, dst in ((0, self.kb), (1, self.vb)):
            for oc in range(8):
                def fn(e, w_i=w_i, dst=dst, oc=oc):
                    return e.dma_start(
                        out=dst[:, oc:oc + 1],
                        in_=self.kb_dram[w_i, oc * 128:(oc + 1) * 128]
                        .rearrange("(p one) -> p one", one=1),
                        allow_slow_non_contiguous=True)
                r.op("gpsimd", fn, [("wdma", r.last_wait["wdma"]),
                                    ("stx", r.val("stx"))],
                     (("wdma", 16),), selfsync=False)
        r.op("gpsimd", None, (("wdma", r.val("wdma")),), ())
        r.barrier()

    def alloc_phaseA(self, r, B, ar, A):
        self.tabc = ar.alloc("l1b_tabc", [128, 512], BF16)
        self.tabs = ar.alloc("l1b_tabs", [128, 512], BF16)
        self.qraw = ar.alloc("l1b_qraw", [128, 512], BF16)
        self.ropet = ar.alloc("l1b_ropet", [128, 512], BF16)
        self.ln0 = ar.alloc("l1b_ln0", [128, 4, 256], BF16)
        self.latT = ar.alloc("l1b_latT", [128, 2, 512], BF16)
        self._tab_free = []
        self._qraw_free = []
        self._latT_free = []
        A.q_done = {}
        A.k_done = {}
        A.v_done = {}
        r._psA_free = {}
        r._psB_free = {}
        r._psC_free = {}

    def _proj_rope(self, r, B, A, ydw, wT, nk, rhs_sel, bias, dst_sel,
                   done_dict, rope, psA, psC):
        for oc in range(8):
            b = oc % 2
            w = list(ydw) + r._psA_free.get(b, [])
            for kc in range(nk):
                r.op("tensor",
                     lambda e, b=b, kc=kc, oc=oc: e.matmul(
                         psA[:, b], wT[:, kc, oc * 128:(oc + 1) * 128],
                         rhs_sel(kc), start=(kc == 0), stop=(kc == nk - 1)),
                     w if kc == 0 else (), (("pe", 1),) if kc == nk - 1 else ())
            pe_p = r.val("pe")
            guard = done_dict.get(("dst_guard", oc), [])
            if not rope:
                dst = dst_sel(oc)
                if bias is None:
                    r.op("scalar",
                         lambda e, b=b, dst=dst: e.activation(
                             dst, psA[:, b], AF.Copy),
                         [("pe", pe_p)] + guard, (("se", 1),))
                else:
                    r.op("scalar",
                         lambda e, b=b, oc=oc, dst=dst: e.activation(
                             dst, psA[:, b], AF.Identity, scale=1.0,
                             bias=bias[:, oc:oc + 1]),
                         [("pe", pe_p)] + guard, (("se", 1),))
                r._psA_free[b] = [("se", r.val("se"))]
                done_dict[oc] = [("se", r.val("se"))]
            else:
                if bias is None:
                    r.op("scalar",
                         lambda e, b=b: e.activation(
                             self.qraw[:, :], psA[:, b], AF.Copy),
                         [("pe", pe_p)] + self._qraw_free, (("se", 1),))
                else:
                    r.op("scalar",
                         lambda e, b=b, oc=oc: e.activation(
                             self.qraw[:, :], psA[:, b], AF.Identity, scale=1.0,
                             bias=bias[:, oc:oc + 1]),
                         [("pe", pe_p)] + self._qraw_free, (("se", 1),))
                qr_se = r.val("se")
                r._psA_free[b] = [("se", qr_se)]
                b2 = oc % 2
                r.op("tensor",
                     lambda e, b2=b2: e.matmul(
                         psC[:, b2], self.r128[:, :], self.qraw[:, :],
                         start=True, stop=True),
                     [("se", qr_se)] + r._psC_free.get(b2, []), (("pe", 1),))
                pe_rot = r.val("pe")
                dst = dst_sel(oc)
                r.op("vector",
                     lambda e, dst=dst: e.tensor_mul(
                         dst, self.qraw[:, :], self.tabc[:, :]),
                     [("se", qr_se)] + self._tab_free + guard, (("ve", 1),))
                r.op("vector",
                     lambda e, b2=b2: e.tensor_mul(
                         self.ropet[:, :], psC[:, b2], self.tabs[:, :]),
                     [("pe", pe_rot)], (("ve", 1),))
                r._psC_free[b2] = [("ve", r.val("ve"))]
                r.op("vector",
                     lambda e, dst=dst: e.tensor_add(dst, dst, self.ropet[:, :]),
                     (), (("ve", 1),))
                self._qraw_free = [("ve", r.val("ve"))]
                done_dict[oc] = [("ve", r.val("ve"))]

    def qkv(self, r, B, A, ib, yT_done):
        psA, psB, psC, psT = self.ps
        r.dma("sync", self.tabc[:, :], self.cap["cosN2"][:, ib * 512:(ib + 1) * 512],
              "ldx", waits=self._tab_free)
        r.dma("sync", self.tabs[:, :], self.cap["sinN2"][:, ib * 512:(ib + 1) * 512],
              "ldx", waits=self._tab_free)
        tl = r.val("ldx")
        ydw = list(yT_done) + [("ldx", tl)]
        for oc in range(8):
            A.q_done[("dst_guard", oc)] = r._qstage_free
            A.k_done[("dst_guard", oc)] = r._qstage_free
            A.v_done[("dst_guard", oc)] = r._vt_free
        self._proj_rope(r, B, A, ydw, self.wq, 8, lambda kc: A.yT[:, kc, :],
                        None, lambda oc: A.qstage[:, oc], A.q_done, True,
                        psA, psC)
        self.lat_ln(r, B, A, ydw, psB, psT)
        lat_ready = [("ve", r.val("ve")), ("se", r.val("se"))]
        kdst = lambda oc: (A.kt0[:, oc, ib * 512:(ib + 1) * 512] if oc < 4
                           else A.kstage[:, oc - 4])
        self._proj_rope(r, B, A, lat_ready, self.wuk, 2,
                        lambda lc: self.latT[:, lc, :], self.kb, kdst,
                        A.k_done, True, psA, psC)
        self._proj_rope(r, B, A, lat_ready, self.wuv, 2,
                        lambda lc: self.latT[:, lc, :], self.vb,
                        lambda oc: A.vt[:, oc], A.v_done, False, psA, psC)
        self._latT_free = [("pe", r.val("pe"))]
        self._tab_free = [("ve", r.val("ve"))]

    def lat_ln(self, r, B, A, ydw, psB, psT):
        for tt in range(4):
            b = tt % 2
            w = list(ydw) + r._psB_free.get(b, []) + self._latT_free
            for kc in range(8):
                r.op("tensor",
                     lambda e, b=b, kc=kc, tt=tt: e.matmul(
                         psB[:, b, 0:256], A.yT[:, kc, tt * 128:(tt + 1) * 128],
                         self.wdkv[:, kc, :], start=(kc == 0), stop=(kc == 7)),
                     w if kc == 0 else (), (("pe", 1),) if kc == 7 else ())
            pe_lat = r.val("pe")
            jc = 12 + tt
            r.op("vector",
                 lambda e, b=b, jc=jc: e.reduce_sum(
                     B.mu_raw[:, jc:jc + 1], psB[:, b, 0:256], axis=AX.X),
                 [("pe", pe_lat), ("po", r.val("po"))], (("ve", 1),))
            r.op("scalar",
                 lambda e, b=b, jc=jc: e.activation(
                     B.trash[:, 0:256], psB[:, b, 0:256], AF.Square,
                     accum_out=B.s2[:, jc:jc + 1]),
                 [("pe", pe_lat), ("ve", r.val("ve"))], (("se", 1),))
            r.op("gpsimd",
                 lambda e, jc=jc: e.tensor_scalar_mul(
                     B.mu[:, jc:jc + 1], B.mu_raw[:, jc:jc + 1], 1.0 / 256),
                 [("ve", r.val("ve"))], (("po", 1),))
            r.op("gpsimd",
                 lambda e, jc=jc: e.tensor_mul(
                     B.mu2[:, jc:jc + 1], B.mu[:, jc:jc + 1], B.mu[:, jc:jc + 1]),
                 (), (("po", 1),))
            r.op("vector",
                 lambda e, jc=jc: e.scalar_tensor_tensor(
                     B.var[:, jc:jc + 1], B.s2[:, jc:jc + 1], 1.0 / 256,
                     B.mu2[:, jc:jc + 1], ALU.mult, ALU.subtract),
                 (("se", r.val("se")), ("po", r.val("po"))), (("ve", 1),))
            r.op("gpsimd",
                 lambda e, jc=jc: e.tensor_scalar_add(
                     B.var[:, jc:jc + 1], B.var[:, jc:jc + 1], B.ln_eps),
                 (("ve", r.val("ve")),), (("po", 1),))
            r.op("scalar",
                 lambda e, jc=jc: e.activation(
                     B.sd[:, jc:jc + 1], B.var[:, jc:jc + 1], AF.Sqrt),
                 (("po", r.val("po")),), (("se", 1),))
            r.op("vector",
                 lambda e, jc=jc: e.reciprocal(
                     B.rstd[:, jc:jc + 1], B.sd[:, jc:jc + 1]),
                 (("se", r.val("se")),), (("ve", 1),))
            r.op("vector",
                 lambda e, jc=jc: e.scalar_tensor_tensor(
                     B.nmr[:, jc:jc + 1], B.mu[:, jc:jc + 1], -1.0,
                     B.rstd[:, jc:jc + 1], ALU.mult, ALU.mult),
                 (), (("ve", 1),))
            r.op("scalar",
                 lambda e, b=b, tt=tt, jc=jc: e.activation(
                     self.ln0[:, tt], psB[:, b, 0:256], AF.Identity,
                     scale=B.rstd[:, jc:jc + 1], bias=B.nmr[:, jc:jc + 1]),
                 (("ve", r.val("ve")), ("po", r.val("po"))), (("se", 1),))
            r._psB_free[b] = [("se", r.val("se"))]
            for lc in range(2):
                emit_transpose(r, B, psT, self.ln0[:, tt, lc * 128:(lc + 1) * 128],
                               self.latT[:, lc, tt * 128:(tt + 1) * 128],
                               (tt * 2 + lc) % 2,
                               [("se", r.val("se"))],
                               evac_eng="vector" if lc == 0 else "scalar")

    def alloc_phaseB(self, r, B, ar, A):
        wo = ar.alloc("l1b_wo", [128, 8, D], BF16)
        for kc in range(8):
            r.dma("gpsimd", wo[:, kc], self.g["W_o"][kc * 128:(kc + 1) * 128, :],
                  "wdma")
        self.wo_wait = [("wdma", r.val("wdma"))]
        return wo


class CfgL0AttnA:
    causal = True
    out_bias = None

    def declare(self, nc):
        g = {}
        g["Wq_down"] = nc.dram_tensor("l0a_Wqd", [D, D], F32, kind="ExternalInput").ap()
        g["Wk_down"] = nc.dram_tensor("l0a_Wkd", [D, D], F32, kind="ExternalInput").ap()
        g["Wv_down"] = nc.dram_tensor("l0a_Wvd", [D, D], F32, kind="ExternalInput").ap()
        g["Wq_up_c"] = nc.dram_tensor("l0a_Wquc", [64, 32], F32, kind="ExternalInput").ap()
        g["Wq_up_e"] = nc.dram_tensor("l0a_Wque", [64, 32], F32, kind="ExternalInput").ap()
        g["Wk_up_c"] = nc.dram_tensor("l0a_Wkuc", [64, 32], F32, kind="ExternalInput").ap()
        g["Wk_up_e"] = nc.dram_tensor("l0a_Wkue", [64, 32], F32, kind="ExternalInput").ap()
        g["Wv_up"] = nc.dram_tensor("l0a_Wvu", [64, 64], F32, kind="ExternalInput").ap()
        g["Wc"] = nc.dram_tensor("l0a_Wc", [D, D], F32, kind="ExternalInput").ap()
        self.g = g

    @staticmethod
    def host_inputs(ri, pre="L0_attn_a_"):
        return {"l0a_Wqd": ri[pre + "Wq_down"], "l0a_Wkd": ri[pre + "Wk_down"],
                "l0a_Wvd": ri[pre + "Wv_down"],
                "l0a_Wquc": ri[pre + "Wq_up_c"], "l0a_Wque": ri[pre + "Wq_up_e"],
                "l0a_Wkuc": ri[pre + "Wk_up_c"], "l0a_Wkue": ri[pre + "Wk_up_e"],
                "l0a_Wvu": ri[pre + "Wv_up"], "l0a_Wc": ri[pre + "Wc"]}

    def prep(self, r, B, ar, A, ps):
        self.wqd = ar.alloc("l0a_wqd", [128, 8, D], BF16)
        self.wkd = ar.alloc("l0a_wkd", [128, 8, D], BF16)
        self.wvd = ar.alloc("l0a_wvd", [128, 8, D], BF16)
        self.uqc = ar.alloc("l0a_uqc", [128, 32], BF16)
        self.uqe = ar.alloc("l0a_uqe", [128, 32], BF16)
        self.ukc = ar.alloc("l0a_ukc", [128, 32], BF16)
        self.uke = ar.alloc("l0a_uke", [128, 32], BF16)
        self.uv = ar.alloc("l0a_uv", [128, 64], BF16)
        self.rhat = ar.alloc("l0a_rhat", [128, 128], BF16)
        g = self.g
        for src, dst in ((g["Wq_down"], self.wqd), (g["Wk_down"], self.wkd),
                         (g["Wv_down"], self.wvd)):
            for kc in range(8):
                r.dma("gpsimd", dst[:, kc], src[kc * 128:(kc + 1) * 128, :],
                      "wdma")
        for src, dst in ((g["Wq_up_c"], self.uqc), (g["Wq_up_e"], self.uqe),
                         (g["Wk_up_c"], self.ukc), (g["Wk_up_e"], self.uke),
                         (g["Wv_up"], self.uv)):
            r.dma("gpsimd", dst[0:64, :], src[:, :], "wdma")
            r.dma("gpsimd", dst[64:128, :], src[:, :], "wdma")
        r.dma("sync", self.rhat[:, :], self.cap["RhatAT"][:, :], "ldx")
        r.op("sync", None, (("ldx", r.val("ldx")),), ())
        r.op("gpsimd", None, (("wdma", r.val("wdma")),), ())
        r.barrier()

    def alloc_phaseA(self, r, B, ar, A):
        self.latb = ar.alloc("l0a_lat", [128, 8, 512], BF16)
        self.estage = ar.alloc("l0a_est", [128, 512], BF16)
        self.ropet = ar.alloc("l0a_rt", [128, 512], BF16)
        self.tabc = ar.alloc("l0a_tabc", [128, 512], BF16)
        self.tabs = ar.alloc("l0a_tabs", [128, 512], BF16)
        r.op("gpsimd", lambda e: e.memset(self.estage[:, :], 0.0), (),
             (("po", 1),))
        self._tab_free = []
        self._est_free = []
        self._lat_free = []
        A.q_done = {}
        A.k_done = {}
        A.v_done = {}
        r._psA_free = {}
        r._psB_free = {}
        r._psC_free = {}

    def qkv(self, r, B, A, ib, yT_done):
        psA, psB, psC, psT = self.ps
        r.dma("sync", self.tabc[:, :], self.cap["cosA"][:, ib * 512:(ib + 1) * 512],
              "ldx", waits=self._tab_free)
        r.dma("sync", self.tabs[:, :], self.cap["sinA"][:, ib * 512:(ib + 1) * 512],
              "ldx", waits=self._tab_free)
        tl = r.val("ldx")
        ydw = list(yT_done) + [("ldx", tl)]
        for oc in range(8):
            A.q_done[("dst_guard", oc)] = r._qstage_free
            A.k_done[("dst_guard", oc)] = r._qstage_free
            A.v_done[("dst_guard", oc)] = r._vt_free
        kdst = lambda oc: (A.kt0[:, oc, ib * 512:(ib + 1) * 512] if oc < 4
                           else A.kstage[:, oc - 4])
        for mode, wd, uc, ue, dst_sel, dd in (
                ("qk", self.wqd, self.uqc, self.uqe,
                 lambda hp: A.qstage[:, hp], A.q_done),
                ("qk", self.wkd, self.ukc, self.uke, kdst, A.k_done),
                ("v", self.wvd, None, None, lambda hp: A.vt[:, hp], A.v_done)):
            for oc in range(8):
                b = oc % 2
                w = list(ydw) + r._psA_free.get(b, []) + self._lat_free
                for kc in range(8):
                    r.op("tensor",
                         lambda e, b=b, kc=kc, oc=oc, wd=wd: e.matmul(
                             psA[:, b], wd[:, kc, oc * 128:(oc + 1) * 128],
                             A.yT[:, kc, :], start=(kc == 0), stop=(kc == 7)),
                         w if kc == 0 else (), (("pe", 1),) if kc == 7 else ())
                r.op("scalar",
                     lambda e, b=b, oc=oc: e.activation(
                         self.latb[:, oc], psA[:, b], AF.Copy),
                     [("pe", r.val("pe"))], (("se", 1),))
                r._psA_free[b] = [("se", r.val("se"))]
            lat_rdy = r.val("se")
            self._lat_free = []
            for hp in range(8):
                b = hp % 2
                w = [("se", lat_rdy)] + r._psB_free.get(b, [])
                guard = dd.get(("dst_guard", hp), [])
                if mode == "v":
                    r.op("tensor",
                         lambda e, b=b, hp=hp: e.matmul(
                             psB[0:64, b], self.uv[0:64, :],
                             self.latb[0:64, hp, :], start=True, stop=True,
                             tile_position=(0, 0)),
                         w, ())
                    r.op("tensor",
                         lambda e, b=b, hp=hp: e.matmul(
                             psB[64:128, b], self.uv[64:128, :],
                             self.latb[64:128, hp, :], start=True, stop=True,
                             tile_position=(64, 64)),
                         (), (("pe", 1),))
                    dst = dst_sel(hp)
                    r.op("scalar",
                         lambda e, b=b, dst=dst: e.activation(
                             dst, psB[:, b], AF.Copy),
                         [("pe", r.val("pe"))] + guard, (("se", 1),))
                    r._psB_free[b] = [("se", r.val("se"))]
                    dd[hp] = [("se", r.val("se"))]
                    continue
                r.op("tensor",
                     lambda e, b=b, hp=hp, uc=uc: e.matmul(
                         psB[0:32, b], uc[0:64, :], self.latb[0:64, hp, :],
                         start=True, stop=True, tile_position=(0, 0)),
                     w, ())
                r.op("tensor",
                     lambda e, b=b, hp=hp, ue=ue: e.matmul(
                         psB[32:64, b], ue[0:64, :], self.latb[0:64, hp, :],
                         start=True, stop=True, tile_position=(0, 32)),
                     (), ())
                r.op("tensor",
                     lambda e, b=b, hp=hp, uc=uc: e.matmul(
                         psB[64:96, b], uc[64:128, :], self.latb[64:128, hp, :],
                         start=True, stop=True, tile_position=(64, 64)),
                     (), ())
                r.op("tensor",
                     lambda e, b=b, hp=hp, ue=ue: e.matmul(
                         psB[96:128, b], ue[64:128, :], self.latb[64:128, hp, :],
                         start=True, stop=True, tile_position=(64, 96)),
                     (), (("pe", 1),))
                pe_up = r.val("pe")
                dst = dst_sel(hp)
                r.op("scalar",
                     lambda e, b=b, dst=dst: e.activation(
                         dst[0:32, :], psB[0:32, b], AF.Copy),
                     [("pe", pe_up)] + guard, (("se", 1),))
                r.op("scalar",
                     lambda e, b=b, dst=dst: e.activation(
                         dst[64:96, :], psB[64:96, b], AF.Copy),
                     (), (("se", 1),))
                r.op("scalar",
                     lambda e, b=b: e.activation(
                         self.estage[32:64, :], psB[32:64, b], AF.Copy),
                     self._est_free, (("se", 1),))
                r.op("scalar",
                     lambda e, b=b: e.activation(
                         self.estage[96:128, :], psB[96:128, b], AF.Copy),
                     (), (("se", 1),))
                est_se = r.val("se")
                r._psB_free[b] = [("se", est_se)]
                b2 = hp % 2
                r.op("tensor",
                     lambda e, b2=b2: e.matmul(
                         psC[:, b2], self.rhat[:, :], self.estage[:, :],
                         start=True, stop=True),
                     [("se", est_se), ("po", 1)] + r._psC_free.get(b2, []),
                     (("pe", 1),))
                pe_rot = r.val("pe")
                for base in (32, 96):
                    r.op("vector",
                         lambda e, dst=dst, base=base: e.tensor_mul(
                             dst[base:base + 32, :], self.estage[base:base + 32, :],
                             self.tabc[base:base + 32, :]),
                         [("se", est_se), ("ldx", tl)] + guard, (("ve", 1),))
                    r.op("vector",
                         lambda e, b2=b2, base=base: e.tensor_mul(
                             self.ropet[base:base + 32, :], psC[base:base + 32, b2],
                             self.tabs[base:base + 32, :]),
                         [("pe", pe_rot)], (("ve", 1),))
                    r.op("vector",
                         lambda e, dst=dst, base=base: e.tensor_add(
                             dst[base:base + 32, :], dst[base:base + 32, :],
                             self.ropet[base:base + 32, :]),
                         (), (("ve", 1),))
                r._psC_free[b2] = [("ve", r.val("ve"))]
                self._est_free = [("pe", pe_rot), ("ve", r.val("ve"))]
                dd[hp] = [("ve", r.val("ve")), ("se", r.val("se"))]
            self._lat_free = [("pe", r.val("pe"))]
        self._tab_free = [("ve", r.val("ve"))]

    def alloc_phaseB(self, r, B, ar, A):
        wo = ar.alloc("l0a_wc", [128, 8, D], BF16)
        for kc in range(8):
            r.dma("gpsimd", wo[:, kc], self.g["Wc"][kc * 128:(kc + 1) * 128, :],
                  "wdma")
        self.wo_wait = [("wdma", r.val("wdma"))]
        return wo


class FfnL1:
    def declare(self, nc):
        g = {}
        g["w1"] = nc.dram_tensor("f1_w1", [4096, D], F32, kind="ExternalInput").ap()
        g["w2"] = nc.dram_tensor("f1_w2", [4096, D], F32, kind="ExternalInput").ap()
        g["h1"] = nc.dram_tensor("f1_h1", [4096, 4096], F32, kind="ExternalInput").ap()
        g["h2"] = nc.dram_tensor("f1_h2", [4096, 4096], F32, kind="ExternalInput").ap()
        g["w3"] = nc.dram_tensor("f1_w3", [D, 4096], F32, kind="ExternalInput").ap()
        g["s1"] = nc.dram_tensor("f1_s1", [128, 32], F32, kind="ExternalInput").ap()
        g["s2"] = nc.dram_tensor("f1_s2", [128, 32], F32, kind="ExternalInput").ap()
        g["sh1"] = nc.dram_tensor("f1_sh1", [128, 32], F32, kind="ExternalInput").ap()
        g["sh2"] = nc.dram_tensor("f1_sh2", [128, 32], F32, kind="ExternalInput").ap()
        g["s3"] = nc.dram_tensor("f1_s3", [128, 8], F32, kind="ExternalInput").ap()
        self.w1t = nc.dram_tensor("f1_w1t", [D, 4096], BF16, kind="Internal").ap()
        self.w2t = nc.dram_tensor("f1_w2t", [D, 4096], BF16, kind="Internal").ap()
        self.h1t = nc.dram_tensor("f1_h1t", [4096, 4096], BF16, kind="Internal").ap()
        self.h2t = nc.dram_tensor("f1_h2t", [4096, 4096], BF16, kind="Internal").ap()
        self.w3t = nc.dram_tensor("f1_w3t", [4096, D], BF16, kind="Internal").ap()
        self.g = g

    @staticmethod
    def host_inputs(ri, pre="L1_ffn_"):
        return {"f1_w1": ri[pre + "w1_w"], "f1_w2": ri[pre + "w2_w"],
                "f1_h1": ri[pre + "h1_w"], "f1_h2": ri[pre + "h2_w"],
                "f1_w3": ri[pre + "w3_w"],
                "f1_s1": reshape_col_chunks(ri[pre + "w1_s"]),
                "f1_s2": reshape_col_chunks(ri[pre + "w2_s"]),
                "f1_sh1": reshape_col_chunks(ri[pre + "h1_s"]),
                "f1_sh2": reshape_col_chunks(ri[pre + "h2_s"]),
                "f1_s3": reshape_col_chunks(ri[pre + "w3_s"])}

    def prep(self, r, B, ar, ps):
        psA, psB, psC, psT = ps
        ar.mark()
        scl = ar.alloc("f1_scl", [128, 32], F32)
        natg = ar.alloc("f1_natg", [128, 8, 4096], BF16)
        tst = ar.alloc("f1_tst", [128, 2, 1024], BF16)
        r._psT_free = {}
        jobs = (
            (self.g["w1"], self.g["s1"], self.w1t, 4096, D),
            (self.g["w2"], self.g["s2"], self.w2t, 4096, D),
            (self.g["h1"], self.g["sh1"], self.h1t, 4096, 4096),
            (self.g["h2"], self.g["sh2"], self.h2t, 4096, 4096),
            (self.g["w3"], self.g["s3"], self.w3t, D, 4096),
        )
        nat_free = []
        tst_free = {}
        for (wsrc, ssrc, wdst, nout, nin) in jobs:
            r.dma("sync", scl[:, 0:nout // 128], ssrc[:, 0:nout // 128], "ldx",
                  waits=[("ve", r.val("ve"))])
            sl = r.val("ldx")
            for grp in range(nout // 1024):
                for j in range(8):
                    oc = grp * 8 + j
                    r.dma("gpsimd", natg[:, j, 0:nin],
                          wsrc[oc * 128:(oc + 1) * 128, :], "wdma",
                          waits=nat_free)
                wl = r.val("wdma")
                for j in range(8):
                    oc = grp * 8 + j
                    r.op("vector",
                         lambda e, j=j, oc=oc, nin=nin: e.tensor_scalar_mul(
                             natg[:, j, 0:nin], natg[:, j, 0:nin],
                             scl[:, oc:oc + 1]),
                         [("wdma", wl), ("ldx", sl)], (("ve", 1),))
                sc_ve = r.val("ve")
                for ic in range(nin // 128):
                    for j in range(8):
                        tb = j % 2
                        prev = r._psT_free.get(("psT", tb), ())
                        r.op("tensor",
                             lambda e, j=j, ic=ic, tb=tb: e.transpose(
                                 psT[:, tb, 0:128],
                                 natg[:, j, ic * 128:(ic + 1) * 128],
                                 B.ident[:, :]),
                             [("ve", sc_ve)] + list(prev) +
                             tst_free.get(ic % 2, []),
                             (("pe", 1),))
                        eng = "vector" if j % 2 == 0 else "scalar"
                        sem = "ve" if j % 2 == 0 else "se"
                        r.op(eng,
                             lambda e, j=j, ic=ic, tb=tb, eng=eng: (
                                 e.tensor_copy(tst[:, ic % 2, j * 128:(j + 1) * 128],
                                               psT[:, tb, 0:128])
                                 if eng == "vector" else
                                 e.activation(tst[:, ic % 2, j * 128:(j + 1) * 128],
                                              psT[:, tb, 0:128], AF.Copy)),
                             [("pe", r.val("pe"))], ((sem, 1),))
                        r._psT_free[("psT", tb)] = ((sem, r.val(sem)),)
                    r.dma("sync",
                          wdst[ic * 128:(ic + 1) * 128,
                               grp * 1024:(grp + 1) * 1024],
                          tst[:, ic % 2], "stx",
                          waits=[("ve", r.val("ve")), ("se", r.val("se"))])
                    tst_free[ic % 2] = [("stx", r.val("stx"))]
                nat_free = [("pe", r.val("pe"))]
        r.op("sync", None, (("stx", r.val("stx")),), ())
        ar.release()
        r.barrier()

    def emit(self, r, B, ar, ps, xin, xout):
        psA, psB, psC, psT = ps
        TB = 1024
        ar.mark()
        ystage = ar.alloc("f1_y", [128, 2, 1024], BF16)
        yT = ar.alloc("f1_yT", [128, 8, TB], BF16)
        xs = ar.alloc("f1_xs", [128, 2, 1024], F32)
        hblk = ar.alloc("f1_h", [128, 32, TB], BF16)
        gblk = ar.alloc("f1_g", [128, 32, TB], BF16)
        wbuf = ar.alloc("f1_wb", [128, 2, 2, 8, 128], BF16)
        hbuf = ar.alloc("f1_hb", [128, 2, 32, 128], BF16)
        silu = ar.alloc("f1_silu", [128, 2, 512], BF16)
        xnw = ar.alloc("f1_xn", [128, 2, 1024], F32)

        w1r = self.w1t.rearrange("(a p) n -> p a n", p=128)
        w2r = self.w2t.rearrange("(a p) n -> p a n", p=128)
        h1r = self.h1t.rearrange("(a p) n -> p a n", p=128)
        h2r = self.h2t.rearrange("(a p) n -> p a n", p=128)
        w3r = self.w3t.rearrange("(a p) n -> p a n", p=128)

        state = {"xs_free": {}, "y_free": {}, "psA": {}, "psB": {},
                 "w_free": {}, "h_free": {}, "silu_free": {}, "xnw_free": {},
                 "blk_free": [], "w3_free": []}
        r._ystage_free = {}
        r._psT_free = {}

        for tb in range(4):
            t0 = tb * TB
            for j in range(8):
                jb = j % 2
                r.dma("sync", xs[:, jb], xin[t0 + j * 128: t0 + (j + 1) * 128, :],
                      "ldx", waits=state["xs_free"].get(jb, []))
                ld = r.val("ldx")
                yd = emit_ln(r, B, [(xs[:, jb], ystage[:, jb])], [j % 4],
                             [("ldx", ld)] + state["y_free"].get(jb, []))
                for c in range(8):
                    emit_transpose(r, B, psT, ystage[:, jb, c * 128:(c + 1) * 128],
                                   yT[:, c, j * 128:(j + 1) * 128],
                                   (j * 8 + c) % 2, [("se", yd)],
                                   evac_eng="vector" if c % 2 == 0 else "scalar")
                state["y_free"][jb] = [("pe", r.val("pe"))]
                state["xs_free"][jb] = [("pe", r.val("pe"))]
            yT_rdy = (("ve", r.val("ve")), ("se", r.val("se")))

            for oc in range(32):
                wb = oc % 2
                r.dma("sync", wbuf[:, wb, 0], w1r[:, :, oc * 128:(oc + 1) * 128],
                      "ldx", waits=state["w_free"].get(wb, []))
                r.dma("sync", wbuf[:, wb, 1], w2r[:, :, oc * 128:(oc + 1) * 128],
                      "ldx", waits=[])
                wl = r.val("ldx")
                for nb in range(2):
                    b = nb
                    wA = [("ldx", wl)] + list(yT_rdy) + state["psA"].get(b, [])
                    for kc in range(8):
                        r.op("tensor",
                             lambda e, b=b, kc=kc, nb=nb, wb=wb: e.matmul(
                                 psA[:, b], wbuf[:, wb, 0, kc, :],
                                 yT[:, kc, nb * 512:(nb + 1) * 512],
                                 start=(kc == 0), stop=(kc == 7)),
                             wA if kc == 0 else (),
                             (("pe", 1),) if kc == 7 else ())
                    wB = state["psB"].get(b, [])
                    for kc in range(8):
                        r.op("tensor",
                             lambda e, b=b, kc=kc, nb=nb, wb=wb: e.matmul(
                                 psB[:, b], wbuf[:, wb, 1, kc, :],
                                 yT[:, kc, nb * 512:(nb + 1) * 512],
                                 start=(kc == 0), stop=(kc == 7)),
                             wB if kc == 0 else (),
                             (("pe", 1),) if kc == 7 else ())
                    pe_g = r.val("pe")
                    r.op("scalar",
                         lambda e, b=b: e.activation(
                             silu[:, b], psA[:, b], AF.Silu),
                         [("pe", pe_g - 1)] + state["silu_free"].get(b, []),
                         (("se", 1),))
                    state["psA"][b] = [("se", r.val("se"))]
                    r.op("vector",
                         lambda e, b=b, oc=oc, nb=nb: e.tensor_mul(
                             hblk[:, oc, nb * 512:(nb + 1) * 512],
                             psB[:, b], silu[:, b]),
                         [("pe", pe_g), ("se", r.val("se"))] +
                         state["blk_free"],
                         (("ve", 1),))
                    state["psB"][b] = [("ve", r.val("ve"))]
                    state["silu_free"][b] = [("ve", r.val("ve"))]
                state["w_free"][wb] = [("pe", r.val("pe"))]
            state["blk_free"] = []

            for stage, (hsrc, src_blk, dst_blk) in enumerate(
                    ((h1r, hblk, gblk), (h2r, gblk, hblk))):
                src_rdy = (("ve", r.val("ve")), ("se", r.val("se")))
                for oc in range(32):
                    wb = oc % 2
                    r.dma("sync", hbuf[:, wb], hsrc[:, :, oc * 128:(oc + 1) * 128],
                          "ldx", waits=state["h_free"].get(wb, []))
                    wl = r.val("ldx")
                    for nb in range(2):
                        b = nb
                        wA = [("ldx", wl)] + list(src_rdy) + \
                            state["psA"].get(b, [])
                        for kc in range(32):
                            r.op("tensor",
                                 lambda e, b=b, kc=kc, nb=nb, wb=wb,
                                 src_blk=src_blk: e.matmul(
                                     psA[:, b], hbuf[:, wb, kc, :],
                                     src_blk[:, kc, nb * 512:(nb + 1) * 512],
                                     start=(kc == 0), stop=(kc == 31)),
                                 wA if kc == 0 else (),
                                 (("pe", 1),) if kc == 31 else ())
                        r.op("scalar",
                             lambda e, b=b, oc=oc, nb=nb, dst_blk=dst_blk:
                             e.activation(
                                 dst_blk[:, oc, nb * 512:(nb + 1) * 512],
                                 psA[:, b], AF.Gelu),
                             [("pe", r.val("pe"))], (("se", 1),))
                        state["psA"][b] = [("se", r.val("se"))]
                    state["h_free"][wb] = [("pe", r.val("pe"))]

            hc_rdy = (("se", r.val("se")),)
            pe_c = r.val("pe")
            for q4 in range(4):
                r.dma("sync", gblk[:, q4 * 8:(q4 + 1) * 8, :],
                      w3r[:, q4 * 8:(q4 + 1) * 8, :], "ldx",
                      waits=[("pe", pe_c)] if q4 == 0 else [])
            w3l = r.val("ldx")
            for j in range(8):
                jb = j % 2
                r.dma("sync", xs[:, jb], xin[t0 + j * 128: t0 + (j + 1) * 128, :],
                      "ldx", waits=state["xs_free"].get(jb, []))
                xl = r.val("ldx")
                for oh in range(2):
                    b = oh
                    wD = [("ldx", w3l)] + list(hc_rdy) + state["psA"].get(b, [])
                    for kc in range(32):
                        r.op("tensor",
                             lambda e, b=b, kc=kc, j=j, oh=oh: e.matmul(
                                 psA[:, b], hblk[:, kc, j * 128:(j + 1) * 128],
                                 gblk[:, kc, oh * 512:(oh + 1) * 512],
                                 start=(kc == 0), stop=(kc == 31)),
                             wD if kc == 0 else (),
                             (("pe", 1),) if kc == 31 else ())
                    r.op("vector",
                         lambda e, b=b, jb=jb, oh=oh: e.tensor_add(
                             xnw[:, jb, oh * 512:(oh + 1) * 512], psA[:, b],
                             xs[:, jb, oh * 512:(oh + 1) * 512]),
                         [("pe", r.val("pe")), ("ldx", xl)] +
                         state["xnw_free"].get(jb, []),
                         (("ve", 1),))
                    state["psA"][b] = [("ve", r.val("ve"))]
                r.dma("sync", xout[t0 + j * 128: t0 + (j + 1) * 128, :],
                      xnw[:, jb], "stx", waits=[("ve", r.val("ve"))])
                state["xnw_free"][jb] = [("stx", r.val("stx"))]
                state["xs_free"][jb] = [("ve", r.val("ve"))]
            state["h_free"] = {0: [("pe", r.val("pe"))], 1: [("pe", r.val("pe"))]}
            state["blk_free"] = [("pe", r.val("pe"))]
        ar.release()
        r.op("sync", None, (("stx", r.val("stx")),), ())
        r.barrier()


def fresh_state(r):
    r._psA_free = {}
    r._psB_free = {}
    r._psC_free = {}
    r._psT_free = {}
    r._ystage_free = {}
    r._silu_free = {}
    r._h_free = {}
    r._wgwu_free = []
    r._wd_free = []


def declare_consts(nc, consts):
    cap = {}
    for k, v in consts.items():
        dt = BF16 if v.dtype == ml_dtypes.bfloat16 else F32
        cap[k] = nc.dram_tensor("c_" + k, list(v.shape), dt,
                                kind="ExternalInput").ap()
    return cap


def build_attn_program(cfg, tag):
    nc = bass.Bass("TRN2", target_bir_lowering=False)
    consts = host_consts()
    cap = declare_consts(nc, consts)
    xin = nc.dram_tensor("x_in", [T, D], F32, kind="ExternalInput").ap()
    xout = nc.dram_tensor("x_out", [T, D], F32, kind="ExternalOutput").ap()
    cfg.declare(nc)

    ar = Arena(nc)
    B = Bufs(ar)
    psA = nc.alloc_psum_tensor("psA", [128, 2, 512], F32).ap()
    psB = nc.alloc_psum_tensor("psB", [128, 2, 512], F32).ap()
    psC = nc.alloc_psum_tensor("psC", [128, 2, 512], F32).ap()
    psT = nc.alloc_psum_tensor("psT", [128, 2, 1024], BF16).ap()
    ps = (psA, psB, psC, psT)

    r = Rec(nc)
    for s in ("ldx", "stx", "wdma", "pe", "ve", "se", "po", "bar"):
        r.sem(s)
    fresh_state(r)
    emit_load_consts(r, B, cap)
    emit_attention(r, B, ar, cfg, nc, xin, xout, ps, cap, tag)
    r.op("sync", None, (("stx", r.val("stx")),), ())
    r.emit()
    return nc, consts


def build_ffn_l1_program():
    nc = bass.Bass("TRN2", target_bir_lowering=False)
    consts = host_consts()
    cap = declare_consts(nc, consts)
    xin = nc.dram_tensor("x_in", [T, D], F32, kind="ExternalInput").ap()
    xout = nc.dram_tensor("x_out", [T, D], F32, kind="ExternalOutput").ap()
    f1 = FfnL1()
    f1.declare(nc)

    ar = Arena(nc)
    B = Bufs(ar)
    psA = nc.alloc_psum_tensor("psA", [128, 2, 512], F32).ap()
    psB = nc.alloc_psum_tensor("psB", [128, 2, 512], F32).ap()
    psC = nc.alloc_psum_tensor("psC", [128, 2, 512], F32).ap()
    psT = nc.alloc_psum_tensor("psT", [128, 2, 1024], BF16).ap()
    ps = (psA, psB, psC, psT)

    r = Rec(nc)
    for s in ("ldx", "stx", "wdma", "pe", "ve", "se", "po", "bar"):
        r.sem(s)
    fresh_state(r)
    emit_load_consts(r, B, cap)
    f1.prep(r, B, ar, ps)
    f1.emit(r, B, ar, ps, xin, xout)
    r.op("sync", None, (("stx", r.val("stx")),), ())
    r.emit()
    return nc, consts


def build_ffn_l0_program():
    nc = bass.Bass("TRN2", target_bir_lowering=False)
    consts = host_consts()
    cap = declare_consts(nc, consts)
    xin = nc.dram_tensor("x_in", [T, D], F32, kind="ExternalInput").ap()
    wg = nc.dram_tensor("Wg", [8, D, 4096], F32, kind="ExternalInput").ap()
    wu = nc.dram_tensor("Wu", [8, D, 4096], F32, kind="ExternalInput").ap()
    wd = nc.dram_tensor("Wd", [8, 4096, D], F32, kind="ExternalInput").ap()
    xout = nc.dram_tensor("x_out", [T, D], F32, kind="ExternalOutput").ap()

    ar = Arena(nc)
    B = Bufs(ar)
    psA = nc.alloc_psum_tensor("psA", [128, 2, 512], F32).ap()
    psB = nc.alloc_psum_tensor("psB", [128, 2, 512], F32).ap()
    psC = nc.alloc_psum_tensor("psC", [128, 2, 512], F32).ap()
    psT = nc.alloc_psum_tensor("psT", [128, 2, 1024], BF16).ap()

    r = Rec(nc)
    for s in ("ldx", "stx", "wdma", "pe", "ve", "se", "po", "bar"):
        r.sem(s)
    fresh_state(r)
    emit_load_consts(r, B, cap)
    emit_ffn_l0(r, B, ar, {"Wg": wg, "Wu": wu, "Wd": wd}, xin, xout,
                psA, psB, psC, psT)
    r.op("sync", None, (("stx", r.val("stx")),), ())
    r.emit()
    return nc, consts


def build_full_program():
    nc = bass.Bass("TRN2", target_bir_lowering=False)
    consts = host_consts()
    cap = declare_consts(nc, consts)
    x0 = nc.dram_tensor("x0", [T, D], F32, kind="ExternalInput").ap()
    out = nc.dram_tensor("out", [T, D], F32, kind="ExternalOutput").ap()
    xa = nc.dram_tensor("xa", [T, D], F32, kind="Internal").ap()
    xb = nc.dram_tensor("xb", [T, D], F32, kind="Internal").ap()

    cfg_l0a = CfgL0AttnA()
    cfg_l0b = CfgL0AttnB()
    cfg_l1a = CfgL1AttnA()
    cfg_l1b = CfgL1AttnB()
    f1 = FfnL1()
    for cfg in (cfg_l0a, cfg_l0b, cfg_l1a, cfg_l1b, f1):
        cfg.declare(nc)
    wg = nc.dram_tensor("Wg", [8, D, 4096], F32, kind="ExternalInput").ap()
    wu = nc.dram_tensor("Wu", [8, D, 4096], F32, kind="ExternalInput").ap()
    wd = nc.dram_tensor("Wd", [8, 4096, D], F32, kind="ExternalInput").ap()

    ar = Arena(nc)
    B = Bufs(ar)
    psA = nc.alloc_psum_tensor("psA", [128, 2, 512], F32).ap()
    psB = nc.alloc_psum_tensor("psB", [128, 2, 512], F32).ap()
    psC = nc.alloc_psum_tensor("psC", [128, 2, 512], F32).ap()
    psT = nc.alloc_psum_tensor("psT", [128, 2, 1024], BF16).ap()
    ps = (psA, psB, psC, psT)

    r = Rec(nc)
    for s in ("ldx", "stx", "wdma", "pe", "ve", "se", "po", "bar"):
        r.sem(s)
    fresh_state(r)
    emit_load_consts(r, B, cap)

    def attn(cfg, tag, xin, xout):
        ar.mark()
        emit_attention(r, B, ar, cfg, nc, xin, xout, ps, cap, tag)
        ar.release()

    def ffn0(xin, xout):
        fresh_state(r)
        emit_ffn_l0(r, B, ar, {"Wg": wg, "Wu": wu, "Wd": wd}, xin, xout,
                    psA, psB, psC, psT)

    attn(cfg_l0a, "l0a", x0, xa)
    ffn0(xa, xb)
    attn(cfg_l0b, "l0b", xb, xa)
    ffn0(xa, xb)
    attn(cfg_l1a, "l1a", xb, xa)
    ar.mark()
    f1.prep(r, B, ar, ps)
    f1.emit(r, B, ar, ps, xa, xb)
    ar.release()
    attn(cfg_l1b, "l1b", xb, xa)
    ar.mark()
    f1.emit(r, B, ar, ps, xa, out)
    ar.release()

    r.op("sync", None, (("stx", r.val("stx")),), ())
    r.emit()
    return nc, consts


def full_host_inputs(inputs):
    im = {}
    im.update(CfgL0AttnA.host_inputs(inputs))
    im.update(CfgL0AttnB.host_inputs(inputs))
    im.update(CfgL1AttnA.host_inputs(inputs))
    im.update(CfgL1AttnB.host_inputs(inputs))
    im.update(FfnL1.host_inputs(inputs))
    im.update({"Wg": inputs["L0_ffn_Wg"], "Wu": inputs["L0_ffn_Wu"],
               "Wd": inputs["L0_ffn_Wd"]})
    return im


_PROG_CACHE = {}


def _get_programs():
    if "progs" in _PROG_CACHE:
        return _PROG_CACHE["progs"]
    progs = {}
    progs["l0a"] = build_attn_program(CfgL0AttnA(), "l0a")
    progs["ffn0"] = build_ffn_l0_program()
    progs["l0b"] = build_attn_program(CfgL0AttnB(), "l0b")
    progs["l1a"] = build_attn_program(CfgL1AttnA(), "l1a")
    progs["ffn1"] = build_ffn_l1_program()
    progs["l1b"] = build_attn_program(CfgL1AttnB(), "l1b")
    _PROG_CACHE["progs"] = progs
    return progs


def kernel(**inputs):
    if "full" not in _PROG_CACHE:
        _PROG_CACHE["full"] = build_full_program()
    nc, consts = _PROG_CACHE["full"]
    cmap = {"c_" + k: np.asarray(v) for k, v in consts.items()}
    wmap = {k: np.asarray(v) for k, v in full_host_inputs(inputs).items()}

    x = np.ascontiguousarray(np.asarray(inputs["L0_attn_a_x"], np.float32))
    in_maps = []
    for c in range(8):
        im = dict(cmap)
        im.update(wmap)
        im["x0"] = np.ascontiguousarray(x[c])
        in_maps.append(im)
    res = run_bass_kernel_spmd(nc, in_maps, core_ids=list(range(8)),
                               trace_cores=[0])
    return np.stack([res.results[c]["out"] for c in range(8)], axis=0)
